# version 34
# baseline (speedup 1.0000x reference)
import numpy as np
import ml_dtypes
import concourse.bass as bass
import concourse.mybir as mybir
from concourse.bass_utils import run_bass_kernel_spmd
from contextlib import ExitStack

F32 = mybir.dt.float32
BF16 = mybir.dt.bfloat16
AF = mybir.ActivationFunctionType
ALU = mybir.AluOpType
AX = mybir.AxisListType

D = 2048
DC = D // 128
CKV = 512
UP = 1024
NH = 16
VD = 64
RD = 32
HD = 96
I_FF = 5632
IC = I_FF // 128
GU = 2 * I_FF
SCALE = 1.0 / (NH ** 0.5 + RD ** 0.5)
EPS_LN = 1e-5
QMAX = 127.0
EPS_Q = 1e-8
MAGIC = 12582912.0

ENGS = ("sync", "tensor", "vector", "scalar", "gpsimd")


class Sem:
    def __init__(self, k, name):
        self.h = k.es.enter_context(k.nc.semaphore(name))
        self.val = 0
        self.name = name


class KB:

    def __init__(self, nc):
        self.nc = nc
        self.es = ExitStack()
        self.ops = {e: [] for e in ENGS}
        self.waited = {}
        self.sems = {}

    def sem(self, name):
        s = Sem(self, name)
        self.sems[name] = s
        return s

    def sb(self, es, name, shape, dtype):
        return es.enter_context(self.nc.sbuf_tensor(name, shape, dtype))

    def ps(self, es, name, shape):
        return es.enter_context(self.nc.psum_tensor(name, shape, F32))

    def wait(self, eng, sem, val=None):
        v = sem.val if val is None else val
        if v <= 0:
            return
        key = (eng, sem.name)
        if self.waited.get(key, -1) >= v:
            return
        self.waited[key] = v

        def fn(h, sem=sem, v=v):
            h.wait_ge(sem.h, v)
        self.ops[eng].append(fn)

    def wmark(self, eng, mark):
        if mark:
            self.wait(eng, mark[0], mark[1])

    def emit(self, eng, build, inc=None, incn=1, batched=False):
        if inc is not None:
            if incn == 16 and not batched:
                self.wait(eng, inc, inc.val)
            inc.val += incn

        def fn(h, build=build, inc=inc, incn=incn):
            ins = build(h)
            if inc is not None:
                ins.then_inc(inc.h, incn)
        self.ops[eng].append(fn)
        return inc.val if inc is not None else None

    def barrier(self):
        for e in ENGS:
            for s in self.sems.values():
                self.wait(e, s)

    def play(self):
        with self.nc.Block() as blk:
            for e in ENGS:
                sub = self.ops[e]

                def run(h, sub=sub):
                    for f in sub:
                        f(h)
                getattr(blk, e)(run)
        self.ops = {e: [] for e in ENGS}


def build_nc(L=4096, dbg=(), phases="ABCD01P2E"):
    import os
    phases = os.environ.get("KPHASES", phases)
    nc = bass.Bass("TRN2", target_bir_lowering=False)
    TB = L // 512
    TC = L // 128
    QBL = min(L, 1024)
    QB = L // QBL
    QJ = QBL // 512
    KC = L // 128

    dt = nc.dram_tensor
    x_in = dt("x_in", [L, D], F32, kind="ExternalInput")
    W = {}
    for nm, shp in [("W_dkv", [D, CKV]), ("b_dkv", [CKV]),
                    ("W_uk", [CKV, UP]), ("b_uk", [UP]),
                    ("W_uv", [CKV, UP]), ("b_uv", [UP]),
                    ("W_dq", [D, CKV]), ("b_dq", [CKV]),
                    ("W_uq", [CKV, UP]), ("b_uq", [UP]),
                    ("W_qr", [CKV, NH * RD]), ("b_qr", [NH * RD]),
                    ("W_kr", [D, RD]), ("b_kr", [RD]),
                    ("W_fc", [UP, D]), ("b_fc", [D]),
                    ("w_gate_up", [GU, D]), ("w_down", [D, I_FF])]:
        W[nm] = dt(nm, shp, F32, kind="ExternalInput")
    c_id = dt("c_id", [128, 128], BF16, kind="ExternalInput")
    c_perm = dt("c_perm", [128, 128], BF16, kind="ExternalInput")
    c_ra = dt("c_ra", [128, L], BF16, kind="ExternalInput")
    c_rb = dt("c_rb", [128, L], BF16, kind="ExternalInput")
    out_t = dt("out", [L, D], F32, kind="ExternalOutput")

    def scratch(name, shape, dtyp):
        kind = "ExternalOutput" if name in dbg else "Internal"
        return dt(name, shape, dtyp, kind=kind)

    k_hbm = scratch("k_hbm", [NH, HD, L], BF16)
    q_hbm = scratch("q_hbm", [NH, HD, L], BF16)
    v_hbm = scratch("v_hbm", [KC, 128, NH * 65], BF16)
    x2_hbm = scratch("x2_hbm", [L, D], F32)
    xq_hbm = scratch("xq_hbm", [TC, 128, D], BF16)
    w1qT_hbm = scratch("w1qT_hbm", [DC, 128, GU], BF16)
    w2qT_hbm = scratch("w2qT_hbm", [IC, 128, D], BF16)
    h_hbm = scratch("h_hbm", [TC, 128, I_FF], BF16)
    hqT_hbm = scratch("hqT_hbm", [TC, 128, I_FF], BF16)
    ffn_hbm = scratch("ffn_hbm", [TC, 128, D], F32)
    ao_dbg = scratch("ao_dbg", [UP, L], BF16) if "ao_dbg" in dbg else None

    k = KB(nc)
    pe = k.sem("pe")
    dve = k.sem("dve")
    act = k.sem("act")
    pool = k.sem("pool")
    sWL = k.sem("sWL")
    sBL = k.sem("sBL")
    sRT = [k.sem(f"sRT{i}") for i in range(2)]
    sX = [k.sem(f"sX{i}") for i in range(2)]
    sKQ = k.sem("sKQ")
    sVL = k.sem("sVL")
    sKH = [k.sem(f"sKH{i}") for i in range(2)]
    sQH = [k.sem(f"sQH{i}") for i in range(2)]
    sX2 = k.sem("sX2")
    sXQ = [k.sem(f"sXQ{i}") for i in range(2)]
    sW1 = [k.sem(f"sW1{i}") for i in range(2)]
    sST = k.sem("sST")
    sW2 = k.sem("sW2")
    sWJ = [k.sem(f"sWJ{i}") for i in range(2)]
    sXQL = [k.sem(f"sXQL{i}") for i in range(2)]
    sHS = [k.sem(f"sHS{i}") for i in range(2)]
    sHL = [k.sem(f"sHL{i}") for i in range(2)]
    sHQS = [k.sem(f"sHQS{i}") for i in range(2)]
    sW2B = [k.sem(f"sW2B{i}") for i in range(2)]
    sHQL = [k.sem(f"sHQL{i}") for i in range(2)]
    sFS = [k.sem(f"sFS{i}") for i in range(2)]
    sEL = [k.sem(f"sEL{i}") for i in range(2)]
    sOUT = k.sem("sOUT")

    esG = ExitStack()
    cid_sb = k.sb(esG, "cid", [128, 128], BF16)
    ones1_128 = k.sb(esG, "ones1_128", [1, 128], BF16)
    ones1_64 = k.sb(esG, "ones1_64", [1, 64], BF16)
    ones96_16 = k.sb(esG, "ones96_16", [96, 16], BF16)
    negc_bc = k.sb(esG, "negc_bc", [128, 16], F32)

    k.emit("sync", lambda h: h.dma_start(cid_sb[:, :], c_id[:, :]), sBL, 16)
    for m in (ones1_128, ones1_64, ones96_16):
        k.emit("gpsimd", lambda h, m=m: h.memset(m[:, :], 1.0), pool)

    esA = ExitStack()
    Wdkv = k.sb(esA, "Wdkv", [128, DC * 512], BF16)
    Wdq = k.sb(esA, "Wdq", [128, DC * 512], BF16)
    Wuk = k.sb(esA, "Wuk", [128, 4 * UP], BF16)
    Wuv = k.sb(esA, "Wuv", [128, 4 * UP], BF16)
    Wuq = k.sb(esA, "Wuq", [128, 4 * UP], BF16)
    Wqr = k.sb(esA, "Wqr", [128, 4 * 512], BF16)
    Wkr = k.sb(esA, "Wkr", [128, DC * RD], BF16)
    bdkv = k.sb(esA, "bdkv", [128, 4], F32)
    bdq = k.sb(esA, "bdq", [128, 4], F32)
    buk = k.sb(esA, "buk", [128, 8], F32)
    buq = k.sb(esA, "buq", [128, 8], F32)
    bqr = k.sb(esA, "bqr", [128, 4], F32)
    bkr = k.sb(esA, "bkr", [32, 1], F32)
    buv_row = k.sb(esA, "buv_row", [1, UP], BF16)
    buvf = k.sb(esA, "buvf", [1, UP], F32)
    buv_bc = k.sb(esA, "buv_bc", [128, UP], F32)
    ra_b = k.sb(esA, "ra_b", [128, 2 * 512], BF16)
    rb_b = k.sb(esA, "rb_b", [128, 2 * 512], BF16)
    perm_sb = k.sb(esA, "perm_sb", [128, 128], BF16)
    x_sb = k.sb(esA, "x_sb", [128, 2 * D], F32)
    xln_tm = k.sb(esA, "xln_tm", [128, 4 * D], BF16)
    x_lnT = k.sb(esA, "x_lnT", [128, 2 * DC * 512], BF16)
    ckv_sb = k.sb(esA, "ckv_sb", [128, 4 * 512], BF16)
    cq_sb = k.sb(esA, "cq_sb", [128, 4 * 512], BF16)
    kst = k.sb(esA, "kst", [128, NH * 512], BF16)
    qst = k.sb(esA, "qst", [128, NH * 512], BF16)
    vstg = k.sb(esA, "vstg", [128, 4 * NH * 65], BF16)
    qrp_sb = k.sb(esA, "qrp_sb", [128, 512], BF16)
    qre_sb = k.sb(esA, "qre_sb", [128, 512], BF16)
    krp_sb = k.sb(esA, "krp_sb", [32, 512], BF16)
    kre_sb = k.sb(esA, "kre_sb", [32, 512], BF16)
    q2m = k.sb(esA, "q2m", [128, 512], F32)
    qam = k.sb(esA, "qam", [128, 512], F32)
    stat = k.sb(esA, "stat", [128, 24], F32)
    mv = k.sb(esA, "mv", [128, 2], F32)
    ve = k.sb(esA, "ve", [128, 1], F32)
    sd = k.sb(esA, "sd", [128, 1], F32)
    rstd = k.sb(esA, "rstd", [128, 1], F32)
    negmr = k.sb(esA, "negmr", [128, 1], F32)

    ps_tr = [k.ps(esA, f"ps_tr{i}", [128, 512]) for i in range(2)]
    ps_c = [k.ps(esA, f"ps_c{i}", [128, 512]) for i in range(2)]
    ps_o = [k.ps(esA, f"ps_o{i}", [128, 512]) for i in range(2)]
    ps_small = k.ps(esA, "ps_small", [32, 512])

    def stage_cast(dst_sb, src3d, nchunks, ow, stg, ld_sem):
        per = max(1, 4096 // ow)
        c0 = 0
        while c0 < nchunks:
            c1 = min(nchunks, c0 + per)
            n = (c1 - c0) * ow
            k.wait("sync", dve)
            k.emit("sync", lambda h, src3d=src3d, c0=c0, c1=c1, n=n, stg=stg:
                   h.dma_start(stg[:, 0:n].rearrange(
                       "p (c o) -> p c o", c=c1 - c0), src3d[:, c0:c1, :]),
                   ld_sem, 16)
            k.wait("vector", ld_sem)
            k.emit("vector", lambda h, dst_sb=dst_sb, c0=c0, n=n, ow=ow,
                   stg=stg: h.tensor_copy(
                       dst_sb[:, c0 * ow:c0 * ow + n], stg[:, 0:n]), dve)
            c0 = c1

    stage_cast(Wdkv, W["W_dkv"][:, :].rearrange("(c p) o -> p c o", p=128),
               DC, 512, x_sb, sWL)
    stage_cast(Wdq, W["W_dq"][:, :].rearrange("(c p) o -> p c o", p=128),
               DC, 512, x_sb, sWL)
    stage_cast(Wuk, W["W_uk"][:, :].rearrange("(c p) o -> p c o", p=128),
               4, UP, x_sb, sWL)
    stage_cast(Wuv, W["W_uv"][:, :].rearrange("(c p) o -> p c o", p=128),
               4, UP, x_sb, sWL)
    stage_cast(Wuq, W["W_uq"][:, :].rearrange("(c p) o -> p c o", p=128),
               4, UP, x_sb, sWL)
    stage_cast(Wqr, W["W_qr"][:, :].rearrange("(c p) o -> p c o", p=128),
               4, 512, x_sb, sWL)
    stage_cast(Wkr, W["W_kr"][:, :].rearrange("(c p) o -> p c o", p=128),
               DC, RD, x_sb, sWL)
    k.emit("sync", lambda h: h.dma_start(
        buvf[:, :], W["b_uv"][:].rearrange("(o c) -> o c", o=1)), sWL, 16)
    k.wait("vector", sWL)
    k.emit("vector", lambda h: h.tensor_copy(buv_row[:, :], buvf[:, :]), dve)

    def f32_load(dst, src_ap):
        k.emit("sync", lambda h, dst=dst, src_ap=src_ap:
               h.dma_start(dst, src_ap, allow_slow_non_contiguous=True),
               sBL, 16)

    f32_load(bdkv[:, :], W["b_dkv"][:].rearrange("(c p) -> p c", p=128))
    f32_load(bdq[:, :], W["b_dq"][:].rearrange("(c p) -> p c", p=128))
    f32_load(buk[:, :], W["b_uk"][:].rearrange("(c p) -> p c", p=128))
    f32_load(buq[:, :], W["b_uq"][:].rearrange("(c p) -> p c", p=128))
    f32_load(bqr[:, :], W["b_qr"][:].rearrange("(c p) -> p c", p=128))
    f32_load(bkr[:, :], W["b_kr"][:].rearrange("(p c) -> p c", c=1))
    f32_load(perm_sb[:, :], c_perm[:, :])

    k.wait("tensor", sWL)
    k.wait("tensor", dve)
    k.wait("tensor", sBL)
    k.wait("tensor", pool)
    k.wait("scalar", sBL)
    k.wait("vector", sBL)
    for j in range(2):
        k.emit("tensor", lambda h, j=j: h.matmul(
            ps_c[j][:, :], ones1_128[:, :], buv_row[:, 512 * j:512 * (j + 1)],
            start=True, stop=True), pe)
    k.wait("scalar", pe)
    buv_mk = 0
    for j in range(2):
        buv_mk = k.emit("scalar", lambda h, j=j: h.copy(
            buv_bc[:, 512 * j:512 * (j + 1)], ps_c[j][:, :]), act)

    x_free = [(dve, dve.val), (dve, dve.val)]
    rt_free = [None, None]
    ps_tr_free = [None, None]
    ps_c_free = [(act, buv_mk), (act, buv_mk)]
    ps_o_free = [None, None]
    kqv_store_mark = 0

    def ln_stats(src, n512):
        k.wait("vector", dve)
        for c in range(n512):
            k.emit("vector", lambda h, c=c, src=src: h.bn_stats(
                stat[:, 6 * c:6 * c + 6], src[:, 512 * c:512 * (c + 1)]),
                dve)
        k.wait("vector", dve)
        k.emit("vector", lambda h, n512=n512: h.bn_aggr(
            mv[:, :], stat[:, 0:6 * n512]), dve)
        k.wait("vector", dve)
        k.emit("vector", lambda h: h.tensor_scalar(
            ve[:, :], mv[:, 1:2], EPS_LN, None, ALU.add), dve)
        k.wait("scalar", dve)
        k.emit("scalar", lambda h: h.activation(sd[:, :], ve[:, :], AF.Sqrt),
               act)
        k.wait("vector", act)
        k.emit("vector", lambda h: h.reciprocal(rstd[:, :], sd[:, :]), dve)
        k.wait("vector", dve)
        k.emit("vector", lambda h: h.tensor_scalar(
            negmr[:, :], mv[:, 0:1], rstd[:, :], -1.0, ALU.mult, ALU.mult),
            dve)

    xlnT_mk = [0, 0]
    xlnT_rd = [0, 0]
    rt_val = [0, 0]

    nrm_mks = [0, 0, 0, 0]

    def ln_vec(b):
        bb = b % 2
        k.wmark("sync", rt_free[bb])
        k.emit("sync", lambda h, b=b, bb=bb: h.dma_start(
            ra_b[:, 512 * bb:512 * (bb + 1)],
            c_ra[:, 512 * b:512 * (b + 1)]), sRT[bb], 16)
        k.emit("sync", lambda h, b=b, bb=bb: h.dma_start(
            rb_b[:, 512 * bb:512 * (bb + 1)],
            c_rb[:, 512 * b:512 * (b + 1)]), sRT[bb], 16)
        rt_val[bb] = sRT[bb].val
        for cb in range(4):
            tc = 4 * b + cb
            xb = tc % 2
            k.wmark("sync", x_free[xb])
            k.emit("sync", lambda h, tc=tc, xb=xb: h.dma_start(
                x_sb[:, xb * D:(xb + 1) * D],
                x_in[128 * tc:128 * (tc + 1), :]), sX[xb], 16)
            xld_val = sX[xb].val
            xc = x_sb[:, xb * D:(xb + 1) * D]
            k.wait("vector", sX[xb], xld_val)
            ln_stats(xc, 4)
            dst = xln_tm[:, cb * D:(cb + 1) * D]
            k.wait("scalar", dve)
            k.wait("scalar", pe)
            k.wait("scalar", sX[xb], xld_val)
            nrm_mks[cb] = k.emit(
                "scalar", lambda h, dst=dst, xc=xc: h.activation(
                    dst, xc, AF.Identity, bias=negmr[:, :],
                    scale=rstd[:, :]), act)
            x_free[xb] = (act, nrm_mks[cb])

    def ln_pe(b):
        bb = b % 2
        for cb in range(4):
            dst = xln_tm[:, cb * D:(cb + 1) * D]
            for g in range(4):
                pt = g % 2
                k.wait("tensor", act, nrm_mks[cb])
                k.wmark("tensor", ps_tr_free[pt])
                for jj in range(4):
                    dc = 4 * g + jj
                    k.emit("tensor", lambda h, pt=pt, jj=jj, dc=dc, dst=dst:
                           h.matmul(ps_tr[pt][:, 128 * jj:128 * (jj + 1)],
                                    dst[:, 128 * dc:128 * (dc + 1)],
                                    cid_sb[:, :], start=True, stop=True), pe)
                k.wait("scalar", pe)
                k.wait("scalar", pe, xlnT_rd[bb])
                dstT = x_lnT[:, :].rearrange(
                    "p (u dc t) -> p u dc t", u=2, dc=DC)[
                        :, bb, 4 * g:4 * g + 4, 128 * cb:128 * (cb + 1)]
                mk = k.emit(
                    "scalar", lambda h, pt=pt, dstT=dstT: h.copy(
                        dstT, ps_tr[pt][:, :].rearrange(
                            "p (j t) -> p j t", j=4)), act)
                ps_tr_free[pt] = (act, mk)
        xlnT_mk[bb] = ps_tr_free[1][1]

    def rest_part(b):
        nonlocal kqv_store_mark
        bb = b % 2
        base = bb * DC * 512
        t0 = 512 * b
        xT = lambda dc: x_lnT[:, base + dc * 512:base + (dc + 1) * 512]
        k.wait("tensor", act, xlnT_mk[bb])
        c_mk = 0
        for Wsb, bsb, cdst in ((Wdkv, bdkv, ckv_sb), (Wdq, bdq, cq_sb)):
            for cs in range(4):
                pc = cs % 2
                k.wmark("tensor", ps_c_free[pc])
                for dc in range(DC):
                    k.emit("tensor", lambda h, pc=pc, Wsb=Wsb, dc=dc, cs=cs:
                           h.matmul(ps_c[pc][:, :],
                                    Wsb[:, dc * 512 + 128 * cs:
                                        dc * 512 + 128 * (cs + 1)],
                                    xT(dc),
                                    start=(dc == 0), stop=(dc == DC - 1)), pe)
                k.wait("scalar", pe)
                mk = k.emit(
                    "scalar", lambda h, pc=pc, cdst=cdst, bsb=bsb, cs=cs:
                    h.activation(cdst[:, 512 * cs:512 * (cs + 1)],
                                 ps_c[pc][:, :], AF.Identity,
                                 bias=bsb[:, cs:cs + 1]), act)
                ps_c_free[pc] = (act, mk)
                c_mk = mk

        k.wait("tensor", act, c_mk)
        k.wait("scalar", sKQ, kqv_store_mark)
        for Wsb, bsb, sdst, csrc in ((Wuk, buk, kst, ckv_sb),
                                     (Wuq, buq, qst, cq_sb)):
            for oc in range(8):
                po = oc % 2
                k.wmark("tensor", ps_o_free[po])
                for cc in range(4):
                    k.emit("tensor", lambda h, po=po, Wsb=Wsb, oc=oc, cc=cc,
                           csrc=csrc: h.matmul(
                               ps_o[po][:, :],
                               Wsb[:, cc * UP + 128 * oc:
                                   cc * UP + 128 * (oc + 1)],
                               csrc[:, cc * 512:(cc + 1) * 512],
                               start=(cc == 0), stop=(cc == 3)), pe)
                k.wait("vector", pe)
                k.wait("vector", sKQ, kqv_store_mark)
                mk = 0
                for hh in range(2):
                    hd = 2 * oc + hh
                    mk = k.emit(
                        "vector", lambda h, po=po, hh=hh, hd=hd, sdst=sdst,
                        bsb=bsb, oc=oc: h.tensor_scalar(
                            sdst[0:64, 512 * hd:512 * hd + 512],
                            ps_o[po][64 * hh:64 * (hh + 1), :],
                            bsb[64 * hh:64 * (hh + 1), oc:oc + 1], None,
                            ALU.add), dve)
                ps_o_free[po] = (dve, mk)

        k.wait("vector", sKQ, kqv_store_mark)
        k.wait("vector", act, buv_mk)
        k.wait("gpsimd", sKQ, kqv_store_mark)
        vre = vstg[:, :].rearrange("p (c hh d) -> p c hh d", c=4, hh=NH)
        k.emit("gpsimd", lambda h, vre=vre: h.memset(
            vre[:, :, :, 64:65], 1.0), pool)
        for ts in range(4):
            for half in range(2):
                po = (ts * 2 + half) % 2
                k.wmark("tensor", ps_o_free[po])
                for cc in range(4):
                    k.emit("tensor", lambda h, po=po, ts=ts, half=half, cc=cc:
                           h.matmul(ps_o[po][:, :],
                                    ckv_sb[:, cc * 512 + 128 * ts:
                                           cc * 512 + 128 * (ts + 1)],
                                    Wuv[:, cc * UP + 512 * half:
                                        cc * UP + 512 * (half + 1)],
                                    start=(cc == 0), stop=(cc == 3)), pe)
                k.wait("vector", pe)
                vdst = vre[:, ts, 8 * half:8 * half + 8, 0:64]
                mk = k.emit("vector", lambda h, po=po, vdst=vdst, half=half:
                            h.tensor_tensor(
                                vdst, ps_o[po][:, :].rearrange(
                                    "p (hh d) -> p hh d", hh=8),
                                buv_bc[:, 512 * half:512 * (half + 1)]
                                .rearrange("p (hh d) -> p hh d", hh=8),
                                ALU.add), dve)
                ps_o_free[po] = (dve, mk)

        for cc in range(4):
            po = cc % 2
            k.wmark("tensor", ps_o_free[po])
            for c4 in range(4):
                k.emit("tensor", lambda h, po=po, cc=cc, c4=c4: h.matmul(
                    ps_o[po][:, :],
                    Wqr[:, c4 * 512 + 128 * cc:c4 * 512 + 128 * (cc + 1)],
                    cq_sb[:, c4 * 512:(c4 + 1) * 512],
                    start=(c4 == 0), stop=(c4 == 3)), pe)
            k.wait("scalar", pe)
            k.wait("scalar", dve)
            qrp_mk = k.emit("scalar", lambda h, po=po, cc=cc: h.activation(
                qrp_sb[:, :], ps_o[po][:, :], AF.Identity,
                bias=bqr[:, cc:cc + 1]), act)
            ps_o_free[po] = (act, qrp_mk)
            po2 = (cc + 1) % 2
            k.wait("tensor", act, qrp_mk)
            k.wmark("tensor", ps_o_free[po2])
            rot_mk = k.emit("tensor", lambda h, po2=po2: h.matmul(
                ps_o[po2][:, :], perm_sb[:, :], qrp_sb[:, :],
                start=True, stop=True), pe)
            k.wait("vector", pe, rot_mk)
            k.wait("vector", sRT[bb], rt_val[bb])
            k.wait("vector", dve)
            k.emit("vector", lambda h, po2=po2, bb=bb: h.tensor_tensor(
                q2m[:, :], ps_o[po2][:, :],
                rb_b[:, 512 * bb:512 * (bb + 1)], ALU.mult), dve)
            q2_mk = k.emit("vector", lambda h, bb=bb: h.tensor_tensor(
                qam[:, :], qrp_sb[:, :],
                ra_b[:, 512 * bb:512 * (bb + 1)], ALU.mult), dve)
            ps_o_free[po2] = (dve, q2_mk)
            k.wait("vector", dve)
            k.wait("vector", pe)
            qre_mk = k.emit("vector", lambda h: h.tensor_tensor(
                qre_sb[:, :], q2m[:, :], qam[:, :], ALU.add), dve)
            for j in range(4):
                hd = 4 * cc + j
                k.wait("tensor", dve, qre_mk)
                k.wait("tensor", act)
                k.emit("tensor", lambda h, j=j: h.matmul(
                    ps_small[:, :], cid_sb[:, 32 * j:32 * (j + 1)],
                    qre_sb[:, :], start=True, stop=True), pe)
                k.wait("scalar", pe)
                k.emit("scalar", lambda h, hd=hd: h.copy(
                    qst[64:96, 512 * hd:512 * hd + 512], ps_small[:, :]), act)

        k.wait("tensor", act)
        for dc in range(DC):
            k.emit("tensor", lambda h, dc=dc: h.matmul(
                ps_small[:, :], Wkr[:, dc * RD:(dc + 1) * RD],
                xT(dc),
                start=(dc == 0), stop=(dc == DC - 1)), pe)
        k.wait("scalar", pe)
        k.wait("scalar", pool)
        k.wait("scalar", dve)
        krp_mk = k.emit("scalar", lambda h: h.activation(
            krp_sb[:, :], ps_small[:, :], AF.Identity, bias=bkr[:, :]), act)
        k.wait("tensor", act, krp_mk)
        rot_mk = k.emit("tensor", lambda h: h.matmul(
            ps_small[:, :], perm_sb[0:32, 0:32], krp_sb[:, :],
            start=True, stop=True), pe)
        xlnT_rd[bb] = pe.val
        k.wait("vector", pe, rot_mk)
        k.wait("vector", dve)
        k.emit("vector", lambda h, bb=bb: h.tensor_tensor(
            q2m[0:32, :], ps_small[:, :],
            rb_b[0:32, 512 * bb:512 * (bb + 1)], ALU.mult), dve)
        k.emit("vector", lambda h, bb=bb: h.tensor_tensor(
            qam[0:32, :], krp_sb[:, :],
            ra_b[0:32, 512 * bb:512 * (bb + 1)], ALU.mult), dve)
        rt_free[bb] = (dve, dve.val)
        k.wait("vector", dve)
        k.wait("vector", pool)
        kre_mk = k.emit("vector", lambda h: h.tensor_tensor(
            kre_sb[:, :], q2m[0:32, :], qam[0:32, :], ALU.add), dve)
        k.wait("vector", dve, kre_mk)
        k.wait("vector", act)
        k.wait("vector", sKQ, kqv_store_mark)
        for hd in range(NH):
            k.emit("vector", lambda h, hd=hd: h.tensor_copy(
                kst[64:96, 512 * hd:512 * hd + 512], kre_sb[:, :]), dve)

        k.wait("sync", act)
        k.wait("sync", dve)
        for st_, hbm in ((kst, k_hbm), (qst, q_hbm)):
            k.emit("sync", lambda h, st_=st_, hbm=hbm, t0=t0: h.dma_start(
                hbm[:, :, :].rearrange("hh p t -> p hh t")[:, :, t0:t0 + 512],
                st_[0:96, :].rearrange("p (hh t) -> p hh t", hh=NH)),
                sKQ, 16)
        k.emit("sync", lambda h, b=b: h.dma_start(
            v_hbm[:, :, :].rearrange("c p d -> p c d")[:, 4 * b:4 * b + 4, :],
            vstg[:, :].rearrange("p (c d) -> p c d", c=4)), sKQ, 16)
        kqv_store_mark = sKQ.val

    ln_vec(0)
    ln_pe(0)
    for b in range(TB):
        if b + 1 < TB:
            ln_vec(b + 1)
        rest_part(b)
        if b + 1 < TB:
            ln_pe(b + 1)

    esA.close()


    if "B" not in phases:
        k.barrier()
        k.play()
        return nc

    k.barrier()
    esBC = ExitStack()
    ao_sb = k.sb(esBC, "ao_sb", [128, 8 * L], BF16)
    esB = ExitStack()
    v_sb = k.sb(esB, "v_sb", [128, KC * NH * 65], BF16)
    kh_sb = k.sb(esB, "kh_sb", [96, 2 * L], BF16)
    qh_sb = k.sb(esB, "qh_sb", [96, 2 * L], BF16)
    probs = k.sb(esB, "probs", [128, 6 * QBL], BF16)
    sq_sb = k.sb(esB, "sq_sb", [96, 2 * L], BF16)
    qn_h = k.sb(esB, "qn_h", [16, 2], F32)
    tmp16 = k.sb(esB, "tmp16", [16, 1], F32)
    negc1 = k.sb(esB, "negc1", [16, 1], BF16)
    rvb_sb = k.sb(esB, "rvb_sb", [1, QBL], BF16)
    rbc_sb = k.sb(esB, "rbc_sb", [64, QBL], BF16)
    ps_s = [k.ps(esB, f"ps_s{i}", [128, QBL]) for i in range(2)]
    ps_av = k.ps(esB, "ps_av", [65, QBL])
    ps_rb = k.ps(esB, "ps_rb", [128, max(QBL, 1024)])

    negc_mark = {}
    ps_rb_free = None
    sq_rd = 0
    sq_mk = [0, 0]
    sq_pool_mk = 0

    kh_val = [0, 0]
    qh_val = [0, 0]
    head_pe_done = [0, 0]

    def load_head(hd):
        buf = hd % 2
        k.wait("sync", pe, head_pe_done[buf])
        k.wait("sync", dve, sq_pool_mk)
        k.emit("sync", lambda h, hd=hd, buf=buf: h.dma_start(
            kh_sb[:, buf * L:(buf + 1) * L], k_hbm[hd, :, :]), sKH[buf], 16)
        kh_val[buf] = sKH[buf].val
        k.emit("sync", lambda h, hd=hd, buf=buf: h.dma_start(
            qh_sb[:, buf * L:(buf + 1) * L], q_hbm[hd, :, :]), sQH[buf], 16)
        qh_val[buf] = sQH[buf].val

    def head_sq(hd):
        nonlocal sq_pool_mk
        buf = hd % 2
        for which, src_sb in ((0, qh_sb), (1, kh_sb)):
            k.wait("vector", (sQH if which == 0 else sKH)[buf],
                   (qh_val if which == 0 else kh_val)[buf])
            k.wait("vector", pe, sq_rd)
            k.wait("vector", dve)
            sq_mk[which] = k.emit(
                "vector", lambda h, src_sb=src_sb, buf=buf, which=which:
                h.tensor_tensor(sq_sb[:, which * L:(which + 1) * L],
                                src_sb[:, buf * L:(buf + 1) * L],
                                src_sb[:, buf * L:(buf + 1) * L],
                                ALU.mult), dve)
        sq_pool_mk = dve.val

    norm_red = {}

    def head_norm1_part(hd, part):
        nonlocal ps_rb_free, sq_rd
        NJ = L // 512
        if part == 0:
            k.wait("tensor", dve, max(sq_mk))
            k.wmark("tensor", ps_rb_free)
            norm_red.clear()
        for idx in (2 * part, 2 * part + 1):
            which, j = idx // NJ, idx % NJ
            if idx >= 2:
                k.wait("tensor", dve, norm_red[idx - 2])
            nm = k.emit("tensor", lambda h, idx=idx, which=which, j=j:
                        h.matmul(
                            ps_rb[0:16, 512 * (idx % 2):512 * (idx % 2) + 512],
                            ones96_16[:, :],
                            sq_sb[:, which * L + 512 * j:
                                  which * L + 512 * (j + 1)],
                            start=True, stop=True), pe)
            k.wait("vector", pe, nm)
            k.wait("vector", dve)
            norm_red[idx] = k.emit("vector", lambda h, idx=idx: h.reduce_max(
                tmp16[:, :],
                ps_rb[0:16, 512 * (idx % 2):512 * (idx % 2) + 512],
                axis=AX.X), dve)
            k.wait("vector", dve)
            if j == 0:
                k.emit("vector", lambda h, which=which: h.tensor_copy(
                    qn_h[:, which:which + 1], tmp16[:, :]), dve)
            else:
                k.emit("vector", lambda h, which=which: h.tensor_tensor(
                    qn_h[:, which:which + 1], qn_h[:, which:which + 1],
                    tmp16[:, :], ALU.max), dve)
        if 2 * part + 1 == 2 * NJ - 1:
            sq_rd = pe.val
            k.wait("vector", dve)
            k.emit("vector", lambda h: h.tensor_tensor(
                tmp16[:, :], qn_h[:, 0:1], qn_h[:, 1:2], ALU.mult), dve)
            ps_rb_free = (dve, dve.val)

    def head_norm1(hd):
        for p in range(L // 512):
            head_norm1_part(hd, p)

    def head_norm2(hd):
        nonlocal ps_rb_free
        k.wait("scalar", dve, sq_pool_mk)
        k.wait("scalar", dve)
        k.emit("scalar", lambda h: h.activation(tmp16[:, :], tmp16[:, :],
                                                AF.Sqrt), act)
        k.wait("vector", act)
        k.emit("vector", lambda h: h.tensor_scalar(
            negc1[:, :], tmp16[:, :], -SCALE, None, ALU.mult), dve)
        k.wait("tensor", dve)
        k.wmark("tensor", ps_rb_free)
        nb = k.emit("tensor", lambda h: h.matmul(
            ps_rb[0:64, 0:1], ones1_64[:, :], negc1[0:1, :],
            start=True, stop=True), pe)
        k.wait("vector", pe, nb)
        k.emit("vector", lambda h, hd=hd: h.tensor_copy(
            negc_bc[0:64, hd:hd + 1], ps_rb[0:64, 0:1]), dve)
        k.wait("vector", dve)
        negc_mark[hd] = k.emit("vector", lambda h, hd=hd: h.tensor_copy(
            negc_bc[64:128, hd:hd + 1], ps_rb[0:64, 0:1]), dve)
        ps_rb_free = (dve, negc_mark[hd])

    load_head(0)
    pending_epi = None
    k.emit("sync", lambda h: h.dma_start(
        v_sb[:, :].rearrange("p (c d) -> p c d", c=KC),
        v_hbm[:, :, :].rearrange("c p d -> p c d")), sVL, 16)
    head_sq(0)
    head_norm1(0)
    head_norm2(0)
    scores_ok = None
    av_ok = None
    exp_tail = 0
    rvb_rd = 0

    for hd in range(NH):
        buf = hd % 2
        negc_hd_mk = negc_mark[hd]
        for qb in range(QB):
            if hd + 1 < NH and qb == 0:
                load_head(hd + 1)
                if QB == 1:
                    head_sq(hd + 1)
                    head_norm1(hd + 1)
                    head_norm2(hd + 1)
            q0 = qb * QBL
            exp_mark = {}
            av_mark = {}

            def emit_av(c):
                pb = c % 6
                k.wait("tensor", act, exp_mark[c])
                if c == 0:
                    k.wmark("tensor", av_ok)
                for j in range(QJ):
                    k.emit("tensor", lambda h, c=c, pb=pb, j=j, hd=hd:
                           h.matmul(ps_av[:, 512 * j:512 * (j + 1)],
                                    v_sb[:, (NH * 65) * c + 65 * hd:
                                         (NH * 65) * c + 65 * hd + 65],
                                    probs[:, pb * QBL + 512 * j:
                                          pb * QBL + 512 * (j + 1)],
                                    start=(c == 0), stop=(c == KC - 1)), pe)
                av_mark[c] = pe.val

            for c in range(KC):
                pb = c % 6
                sb2 = c % 2
                k.wait("tensor", sKH[buf], kh_val[buf])
                k.wait("tensor", sQH[buf], qh_val[buf])
                k.wait("tensor", sVL, 16)
                if c > 1:
                    k.wait("tensor", act, exp_mark[c - 2])
                else:
                    k.wait("tensor", act, exp_tail)
                    if c == 0:
                        k.wmark("tensor", scores_ok)
                for j in range(QJ):
                    k.emit("tensor", lambda h, buf=buf, c=c, j=j, q0=q0,
                           sb2=sb2:
                           h.matmul(ps_s[sb2][:, 512 * j:512 * (j + 1)],
                                    kh_sb[:, buf * L + 128 * c:
                                          buf * L + 128 * (c + 1)],
                                    qh_sb[:, buf * L + q0 + 512 * j:
                                          buf * L + q0 + 512 * (j + 1)],
                                    start=True, stop=True), pe)
                sc_mk = pe.val
                if c == min(1, KC - 1) and pending_epi is not None:
                    pending_epi()
                    pending_epi = None
                if c >= 5:
                    emit_av(c - 5)
                if hd + 1 < NH and QB > 1 and c == 6 and qb == 0:
                    head_sq(hd + 1)
                if (hd + 1 < NH and QB > 1 and qb == 1 and c >= 5
                        and (c - 5) % 3 == 0 and (c - 5) // 3 < L // 512):
                    head_norm1_part(hd + 1, (c - 5) // 3)
                if hd + 1 < NH and QB > 1 and c == 4 and qb == 2:
                    head_norm2(hd + 1)
                k.wait("scalar", dve, negc_hd_mk)
                k.wait("scalar", pe, sc_mk)
                if c >= 6:
                    k.wait("scalar", pe, av_mark[c - 6])
                exp_mark[c] = k.emit(
                    "scalar", lambda h, pb=pb, hd=hd, sb2=sb2: h.activation(
                        probs[:, pb * QBL:(pb + 1) * QBL], ps_s[sb2][:, :],
                        AF.Exp,
                        bias=negc_bc[:, hd:hd + 1], scale=SCALE), act)
            for cc in range(max(0, KC - 5), KC):
                emit_av(cc)
            k.wait("vector", pe, av_mark[KC - 1])
            k.wait("vector", pe, rvb_rd)
            k.wait("vector", dve)
            def _recip(h):
                with nc.allow_low_precision(reason="1/l to bf16 as before"):
                    return h.reciprocal(rvb_sb[:, :], ps_av[64:65, :])
            recip_mk = k.emit("vector", _recip, dve)

            def mk_epi(recip_mk=recip_mk, hd=hd, q0=q0):
                def epi():
                    nonlocal ps_rb_free, scores_ok, av_ok, rvb_rd
                    k.wait("tensor", dve, recip_mk)
                    k.wmark("tensor", ps_rb_free)
                    for j in range(QJ):
                        k.emit("tensor", lambda h, j=j: h.matmul(
                            ps_rb[0:64, 512 * j:512 * (j + 1)],
                            ones1_64[:, :],
                            rvb_sb[:, 512 * j:512 * (j + 1)],
                            start=True, stop=True), pe)
                    rbc_mk = pe.val
                    rvb_rd = rbc_mk
                    k.wait("vector", pe, rbc_mk)
                    k.emit("vector", lambda h: h.tensor_copy(
                        rbc_sb[:, :], ps_rb[0:64, 0:QBL]), dve)
                    ps_rb_free = (dve, dve.val)
                    k.wait("vector", dve)
                    cc_, hh_ = hd // 2, hd % 2
                    dst = ao_sb[64 * hh_:64 * (hh_ + 1),
                                cc_ * L + q0:cc_ * L + q0 + QBL]
                    ao_mk = k.emit("vector", lambda h, dst=dst:
                                   h.tensor_tensor(dst, ps_av[0:64, :],
                                                   rbc_sb[:, :], ALU.mult),
                                   dve)
                    scores_ok = (dve, ao_mk)
                    av_ok = (dve, ao_mk)
                return epi
            pending_epi = mk_epi()
            exp_tail = exp_mark[KC - 1]
        head_pe_done[buf] = pe.val

    if pending_epi is not None:
        pending_epi()
        pending_epi = None

    if ao_dbg is not None:
        k.wait("sync", dve)
        k.emit("sync", lambda h: h.dma_start(
            ao_dbg[:, :].rearrange("(cc pp) t -> pp cc t", pp=128),
            ao_sb[:, :].rearrange("p (cc t) -> p cc t", cc=8)), sKQ, 16)
    esB.close()

    if "C" not in phases:
        k.barrier()
        k.play()
        return nc

    k.barrier()
    esC = ExitStack()
    Wfc = k.sb(esC, "Wfc", [128, 8 * D], BF16)
    bfc_row = k.sb(esC, "bfc_row", [1, D], BF16)
    xc_sb = k.sb(esC, "xc_sb", [128, 2 * D], F32)
    y_sb = k.sb(esC, "y_sb", [128, 2 * D], F32)
    x3_sb = k.sb(esC, "x3_sb", [128, 2 * D], F32)
    xq_tm = k.sb(esC, "xq_tm", [128, 2 * D], BF16)
    xqst = k.sb(esC, "xqst", [128, 2 * D], BF16)
    qm = k.sb(esC, "qm", [128, 2], F32)
    qsc = k.sb(esC, "qsc", [128, 2], F32)
    qrc = k.sb(esC, "qrc", [128, 2], F32)
    statC = k.sb(esC, "statC", [128, 2 * 24], F32)
    mvC = k.sb(esC, "mvC", [128, 2 * 2], F32)
    veC = k.sb(esC, "veC", [128, 2], F32)
    sdC = k.sb(esC, "sdC", [128, 2], F32)
    rstdC = k.sb(esC, "rstdC", [128, 2], F32)
    negmrC = k.sb(esC, "negmrC", [128, 2], F32)
    ps_y = k.ps(esC, "ps_y", [128, D])
    ps_t2 = [k.ps(esC, f"ps_t2{i}", [128, 512]) for i in range(2)]

    bfcf = k.sb(esC, "bfcf", [1, D], F32)
    stage_cast(Wfc, W["W_fc"][:, :].rearrange("(c p) o -> p c o", p=128),
               8, D, xc_sb, sWL)
    k.emit("sync", lambda h: h.dma_start(
        bfcf[:, :], W["b_fc"][:].rearrange("(o c) -> o c", o=1)), sWL, 16)
    k.wait("vector", sWL)
    bfc_mk = k.emit("vector", lambda h: h.tensor_copy(
        bfc_row[:, :], bfcf[:, :]), dve)

    def ln_chainC(par, src, dst, wr_free_mk, src_mk=None):
        k.wmark("vector", src_mk)
        k.wait("vector", dve)
        for c in range(4):
            k.emit("vector", lambda h, c=c, par=par, src=src: h.bn_stats(
                statC[:, par * 24 + 6 * c:par * 24 + 6 * (c + 1)],
                src[:, 512 * c:512 * (c + 1)]), dve)
        k.wait("vector", dve)
        k.emit("vector", lambda h, par=par: h.bn_aggr(
            mvC[:, 2 * par:2 * par + 2],
            statC[:, par * 24:par * 24 + 24]), dve)
        k.wait("vector", dve)
        k.emit("vector", lambda h, par=par: h.tensor_scalar(
            veC[:, par:par + 1], mvC[:, 2 * par + 1:2 * par + 2], EPS_LN,
            None, ALU.add), dve)
        k.wait("scalar", dve)
        k.emit("scalar", lambda h, par=par: h.activation(
            sdC[:, par:par + 1], veC[:, par:par + 1], AF.Sqrt), act)
        k.wait("vector", act)
        k.emit("vector", lambda h, par=par: h.reciprocal(
            rstdC[:, par:par + 1], sdC[:, par:par + 1]), dve)
        k.wait("vector", dve)
        k.emit("vector", lambda h, par=par: h.tensor_scalar(
            negmrC[:, par:par + 1], mvC[:, 2 * par:2 * par + 1],
            rstdC[:, par:par + 1], -1.0, ALU.mult, ALU.mult), dve)
        k.wait("scalar", dve)
        k.wmark("scalar", wr_free_mk)
        return k.emit("scalar", lambda h, par=par, src=src, dst=dst:
                      h.activation(dst, src, AF.Identity,
                                   bias=negmrC[:, par:par + 1],
                                   scale=rstdC[:, par:par + 1]), act)

    yev_mk = [[None] * 4, [None] * 4]
    xq_mk = [0, 0]
    yln_rd = [0, 0]
    x3_rd = [0, 0]
    y_rd = [0, 0]
    xC_free = [None, None]
    xq_dma_mk = [0, 0]
    x2_dma_mk = [0, 0]
    tgroup_done = [0, 0]

    def emit_tgroup(t):
        par = t % 2
        for g in range(4):
            pt = g % 2
            k.wait("tensor", dve, xq_mk[par])
            k.wait("tensor", act)
            for jj in range(4):
                dc = 4 * g + jj
                k.emit("tensor", lambda h, pt=pt, jj=jj, dc=dc, par=par:
                       h.matmul(ps_t2[pt][:, 128 * jj:128 * (jj + 1)],
                                xq_tm[:, par * D + 128 * dc:
                                      par * D + 128 * (dc + 1)],
                                cid_sb[:, :], start=True, stop=True), pe)
            k.wait("scalar", pe)
            k.wait("scalar", sXQ[par], xq_dma_mk[par])
            k.emit("scalar", lambda h, pt=pt, g=g, par=par: h.copy(
                xqst[:, par * D + 512 * g:par * D + 512 * (g + 1)],
                ps_t2[pt][:, :]), act)
        tgroup_done[par] = pe.val
        k.wait("sync", act)
        k.emit("sync", lambda h, t=t, par=par: h.dma_start(
            xq_hbm[t, :, :], xqst[:, par * D:(par + 1) * D]), sXQ[par], 16)
        xq_dma_mk[par] = sXQ[par].val

    k.wait("sync", dve)
    for t0_ in range(2):
        k.emit("sync", lambda h, t0_=t0_: h.dma_start(
            xc_sb[:, t0_ * D:(t0_ + 1) * D],
            x_in[128 * t0_:128 * (t0_ + 1), :]), sX[t0_], 16)
    xld_val = [sX[0].val, sX[1].val]

    for t in range(TC):
        par = t % 2
        for db in range(4):
            if t >= 1:
                k.wmark("tensor", yev_mk[1 - par][db])
            k.wait("tensor", dve, bfc_mk)
            k.emit("tensor", lambda h, db=db: h.matmul(
                ps_y[:, 512 * db:512 * (db + 1)], ones1_128[:, :],
                bfc_row[:, 512 * db:512 * (db + 1)],
                start=True, stop=False), pe)
            for cc in range(8):
                k.emit("tensor", lambda h, db=db, cc=cc, t=t: h.matmul(
                    ps_y[:, 512 * db:512 * (db + 1)],
                    ao_sb[:, cc * L + 128 * t:cc * L + 128 * (t + 1)],
                    Wfc[:, cc * D + 512 * db:cc * D + 512 * (db + 1)],
                    start=False, stop=(cc == 7)), pe)
            ymk = pe.val
            k.wait("scalar", pe, ymk)
            k.wait("scalar", act, y_rd[par])
            yev_mk[par][db] = (act, k.emit(
                "scalar", lambda h, db=db, par=par: h.copy(
                    y_sb[:, par * D + 512 * db:par * D + 512 * (db + 1)],
                    ps_y[:, 512 * db:512 * (db + 1)]), act))
        if t >= 2:
            emit_tgroup(t - 2)
        ysrc = y_sb[:, par * D:(par + 1) * D]
        yln_mk = ln_chainC(par, ysrc, x3_sb[:, par * D:(par + 1) * D],
                           (dve, x3_rd[par]), src_mk=yev_mk[par][3])
        y_rd[par] = yln_mk
        k.wait("vector", act, yln_mk)
        k.wait("vector", sX[par], xld_val[par])
        k.wait("vector", sX2, x2_dma_mk[par])
        x2_mk = k.emit("vector", lambda h, par=par: h.tensor_tensor(
            xc_sb[:, par * D:(par + 1) * D],
            x3_sb[:, par * D:(par + 1) * D],
            xc_sb[:, par * D:(par + 1) * D], ALU.add), dve)
        x3_rd[par] = x2_mk
        k.wait("sync", dve, x2_mk)
        k.emit("sync", lambda h, t=t, par=par: h.dma_start(
            x2_hbm[128 * t:128 * (t + 1), :],
            xc_sb[:, par * D:(par + 1) * D]), sX2, 16)
        x2_dma_mk[par] = sX2.val
        x3_mk = ln_chainC(par, xc_sb[:, par * D:(par + 1) * D],
                          x3_sb[:, par * D:(par + 1) * D],
                          (dve, x3_rd[par]))
        if t + 2 < TC:
            k.wait("sync", act, x3_mk)
            k.wait("sync", sX2, x2_dma_mk[par])
            k.emit("sync", lambda h, t=t, par=par: h.dma_start(
                xc_sb[:, par * D:(par + 1) * D],
                x_in[128 * (t + 2):128 * (t + 3), :]), sX[par], 16)
            xld_val[par] = sX[par].val
        k.wait("vector", act, x3_mk)
        k.emit("vector", lambda h, par=par: h.reduce_max(
            qm[:, par:par + 1], x3_sb[:, par * D:(par + 1) * D],
            axis=AX.X, apply_absolute_value=True), dve)
        k.wait("vector", dve)
        k.emit("vector", lambda h, par=par: h.tensor_scalar(
            qsc[:, par:par + 1], qm[:, par:par + 1], 1.0 / QMAX, EPS_Q,
            ALU.mult, ALU.max), dve)
        k.wait("vector", dve)
        k.emit("vector", lambda h, par=par: h.reciprocal(
            qrc[:, par:par + 1], qsc[:, par:par + 1]), dve)
        k.wait("vector", dve)
        k.emit("vector", lambda h, par=par: h.tensor_scalar(
            x3_sb[:, par * D:(par + 1) * D],
            x3_sb[:, par * D:(par + 1) * D],
            qrc[:, par:par + 1], MAGIC, ALU.mult, ALU.add), dve)
        k.wait("vector", dve)
        k.wait("vector", pe, tgroup_done[par])
        m2 = k.emit("vector", lambda h, par=par: h.tensor_scalar(
            xq_tm[:, par * D:(par + 1) * D],
            x3_sb[:, par * D:(par + 1) * D], MAGIC, qsc[:, par:par + 1],
            ALU.subtract, ALU.mult), dve)
        xq_mk[par] = m2
        x3_rd[par] = m2

    emit_tgroup(TC - 2)
    emit_tgroup(TC - 1)

    esC.close()
    esBC.close()

    if "0" not in phases:
        k.barrier()
        k.play()
        return nc

    k.barrier()
    esD0 = ExitStack()
    w1raw = k.sb(esD0, "w1raw", [128, 2 * 4 * D], F32)
    w1q = k.sb(esD0, "w1q", [128, 4 * D], BF16)
    stg = k.sb(esD0, "stg", [128, 2 * DC * 512], BF16)
    qmD = k.sb(esD0, "qmD", [128, 1], F32)
    qscD = k.sb(esD0, "qscD", [128, 1], F32)
    qrcD = k.sb(esD0, "qrcD", [128, 1], F32)
    ps_tD = [k.ps(esD0, f"ps_tD{i}", [128, 512]) for i in range(2)]

    NB1 = GU // 512
    w1_free = [None, None]
    stg_free = [0, 0]
    tD_free = [None, None]
    w1_val_arr = [0, 0]
    w1q_rd = [0, 0, 0, 0]

    def load_w1(blk):
        wb = blk % 2
        k.wmark("sync", w1_free[wb])
        k.emit("sync", lambda h, blk=blk, wb=wb: h.dma_start(
            w1raw[:, :].rearrange("p (u c d) -> p u c d", u=2, c=4)[:, wb],
            W["w_gate_up"][512 * blk:512 * (blk + 1), :].rearrange(
                "(c p) d -> p c d", p=128)), sW1[wb], 16)
        w1_val_arr[wb] = sW1[wb].val

    load_w1(0)
    for blk in range(NB1):
        wb = blk % 2
        if blk + 1 < NB1:
            load_w1(blk + 1)
        w1_val = w1_val_arr[wb]
        for oc in range(4):
            src = w1raw[:, (4 * wb + oc) * D:(4 * wb + oc + 1) * D]
            k.wait("vector", sW1[wb], w1_val)
            k.wait("vector", dve)
            k.wait("vector", pe, w1q_rd[oc])
            k.emit("vector", lambda h, src=src: h.reduce_max(
                qmD[:, :], src, axis=AX.X, apply_absolute_value=True), dve)
            k.wait("vector", dve)
            k.emit("vector", lambda h: h.tensor_scalar(
                qscD[:, :], qmD[:, :], 1.0 / QMAX, EPS_Q, ALU.mult, ALU.max), dve)
            k.wait("vector", dve)
            k.emit("vector", lambda h: h.reciprocal(qrcD[:, :], qscD[:, :]),
                   dve)
            k.wait("vector", dve)
            k.emit("vector", lambda h, src=src: h.tensor_scalar(
                src, src, qrcD[:, :], MAGIC, ALU.mult, ALU.add), dve)
            k.wait("vector", dve)
            wq_mk = k.emit("vector", lambda h, src=src, oc=oc: h.tensor_scalar(
                w1q[:, oc * D:(oc + 1) * D], src, MAGIC, qscD[:, :],
                ALU.subtract, ALU.mult), dve)
            w1_free[wb] = (dve, wq_mk)
            for g in range(4):
                pt = g % 2
                k.wait("tensor", dve, wq_mk)
                k.wmark("tensor", tD_free[pt])
                for jj in range(4):
                    dc = 4 * g + jj
                    k.emit("tensor", lambda h, pt=pt, jj=jj, dc=dc, oc=oc:
                           h.matmul(ps_tD[pt][:, 128 * jj:128 * (jj + 1)],
                                    w1q[:, oc * D + 128 * dc:
                                        oc * D + 128 * (dc + 1)],
                                    cid_sb[:, :], start=True, stop=True), pe)
                w1q_rd[oc] = pe.val
                k.wait("scalar", pe)
                k.wait("scalar", sST, stg_free[wb])
                dstT = stg[:, :].rearrange(
                    "p (u hc o) -> p u hc o", u=2, hc=DC)[
                        :, wb, 4 * g:4 * g + 4, 128 * oc:128 * (oc + 1)]
                mk = k.emit(
                    "scalar", lambda h, pt=pt, dstT=dstT: h.copy(
                        dstT, ps_tD[pt][:, :].rearrange(
                            "p (j t) -> p j t", j=4)), act)
                tD_free[pt] = (act, mk)
        k.wait("sync", act)
        k.emit("sync", lambda h, blk=blk, wb=wb: h.dma_start(
            w1qT_hbm[:, :, :].rearrange("hc p o -> p hc o")[
                :, :, 512 * blk:512 * (blk + 1)],
            stg[:, wb * DC * 512:(wb + 1) * DC * 512].rearrange(
                "p (hc o) -> p hc o", hc=DC)), sST, 16)
        stg_free[wb] = sST.val

    esD0.close()

    if "1" not in phases:
        k.barrier()
        k.play()
        return nc

    k.barrier()
    esD0b = ExitStack()
    w2raw = k.sb(esD0b, "w2raw", [128, 2 * I_FF], F32)
    w2q = k.sb(esD0b, "w2q", [128, 2 * I_FF], BF16)
    stg2 = k.sb(esD0b, "stg2", [128, IC * 256], BF16)
    qmD2 = k.sb(esD0b, "qmD2", [128, 1], F32)
    qscD2 = k.sb(esD0b, "qscD2", [128, 1], F32)
    qrcD2 = k.sb(esD0b, "qrcD2", [128, 1], F32)
    ps_tE = [k.ps(esD0b, f"ps_tE{i}", [128, 512]) for i in range(2)]

    w2_free = None
    stg2_free = 0
    tD_free = [None, None]
    w2q_rd = [0, 0]
    for rnd in range(8):
        k.wmark("sync", w2_free)
        k.emit("sync", lambda h, rnd=rnd: h.dma_start(
            w2raw[:, :].rearrange("p (c i) -> p c i", c=2),
            W["w_down"][256 * rnd:256 * (rnd + 1), :].rearrange(
                "(c p) i -> p c i", p=128)), sW2, 16)
        w2_val = sW2.val
        for dc2 in range(2):
            src = w2raw[:, dc2 * I_FF:(dc2 + 1) * I_FF]
            k.wait("vector", sW2, w2_val)
            k.wait("vector", dve)
            k.wait("vector", pe, w2q_rd[dc2])
            k.emit("vector", lambda h, src=src: h.reduce_max(
                qmD2[:, :], src, axis=AX.X, apply_absolute_value=True), dve)
            k.wait("vector", dve)
            k.emit("vector", lambda h: h.tensor_scalar(
                qscD2[:, :], qmD2[:, :], 1.0 / QMAX, EPS_Q, ALU.mult, ALU.max), dve)
            k.wait("vector", dve)
            k.emit("vector", lambda h: h.reciprocal(qrcD2[:, :], qscD2[:, :]),
                   dve)
            k.wait("vector", dve)
            k.emit("vector", lambda h, src=src: h.tensor_scalar(
                src, src, qrcD2[:, :], MAGIC, ALU.mult, ALU.add), dve)
            k.wait("vector", dve)
            wq_mk = k.emit("vector", lambda h, src=src, dc2=dc2:
                           h.tensor_scalar(
                               w2q[:, dc2 * I_FF:(dc2 + 1) * I_FF], src,
                               MAGIC, qscD2[:, :], ALU.subtract, ALU.mult),
                           dve)
            if dc2 == 1:
                w2_free = (dve, wq_mk)
            for g in range(11):
                pt = g % 2
                k.wait("tensor", dve, wq_mk)
                k.wmark("tensor", tD_free[pt])
                for jj in range(4):
                    ic = 4 * g + jj
                    k.emit("tensor", lambda h, pt=pt, jj=jj, ic=ic, dc2=dc2:
                           h.matmul(ps_tE[pt][:, 128 * jj:128 * (jj + 1)],
                                    w2q[:, dc2 * I_FF + 128 * ic:
                                        dc2 * I_FF + 128 * (ic + 1)],
                                    cid_sb[:, :], start=True, stop=True), pe)
                w2q_rd[dc2] = pe.val
                k.wait("scalar", pe)
                k.wait("scalar", sST, stg2_free)
                dstT = stg2[:, :].rearrange(
                    "p (ic d) -> p ic d", ic=IC)[
                        :, 4 * g:4 * g + 4, 128 * dc2:128 * (dc2 + 1)]
                mk = k.emit(
                    "scalar", lambda h, pt=pt, dstT=dstT: h.copy(
                        dstT, ps_tE[pt][:, :].rearrange(
                            "p (j t) -> p j t", j=4)), act)
                tD_free[pt] = (act, mk)
        k.wait("sync", act)
        k.emit("sync", lambda h, rnd=rnd: h.dma_start(
            w2qT_hbm[:, :, :].rearrange("ic p d -> p ic d")[
                :, :, 256 * rnd:256 * (rnd + 1)],
            stg2[:, :].rearrange("p (ic d) -> p ic d", ic=IC)), sST, 16)
        stg2_free = sST.val

    esD0b.close()

    if "D" not in phases:
        k.barrier()
        k.play()
        return nc

    k.barrier()
    esD1 = ExitStack()
    w1g = k.sb(esD1, "w1g", [128, 2 * DC * 512], BF16)
    w1u = k.sb(esD1, "w1u", [128, 2 * DC * 512], BF16)
    xq_sb = k.sb(esD1, "xq_sb", [128, 2 * D], BF16)
    sg_sb = k.sb(esD1, "sg_sb", [128, 2 * 512], F32)
    hst = k.sb(esD1, "hst", [128, 2 * 512], BF16)
    ps_g = [k.ps(esD1, f"ps_g{i}", [128, 512]) for i in range(2)]
    ps_u = [k.ps(esD1, f"ps_u{i}", [128, 512]) for i in range(2)]

    NJ = I_FF // 512
    wj_val = [0, 0]
    xql_val = [0, 0]
    wj_pe_done = [0, 0]
    xq_pe_done = [0, 0]

    def load_wj(j):
        buf = j % 2
        k.wait("sync", pe, wj_pe_done[buf])
        k.emit("sync", lambda h, j=j, buf=buf: h.dma_start(
            w1g[:, buf * DC * 512:(buf + 1) * DC * 512].rearrange(
                "p (hc o) -> p hc o", hc=DC),
            w1qT_hbm[:, :, :].rearrange("hc p o -> p hc o")[
                :, :, 512 * j:512 * (j + 1)]), sWJ[buf], 16)
        k.emit("sync", lambda h, j=j, buf=buf: h.dma_start(
            w1u[:, buf * DC * 512:(buf + 1) * DC * 512].rearrange(
                "p (hc o) -> p hc o", hc=DC),
            w1qT_hbm[:, :, :].rearrange("hc p o -> p hc o")[
                :, :, I_FF + 512 * j:I_FF + 512 * (j + 1)]), sWJ[buf], 16)
        wj_val[buf] = sWJ[buf].val

    def load_xq(t):
        buf = t % 2
        k.wait("sync", pe, xq_pe_done[buf])
        k.emit("sync", lambda h, t=t, buf=buf: h.dma_start(
            xq_sb[:, buf * D:(buf + 1) * D], xq_hbm[t % TC, :, :]),
            sXQL[buf], 16)
        xql_val[buf] = sXQL[buf].val

    load_wj(0)
    load_xq(0)
    load_xq(1)
    ps_g_free = [None, None]
    ps_u_free = [None, None]
    sg_free = [None, None]
    hst_free = [0, 0]
    for j in range(NJ):
        buf = j % 2
        if j + 1 < NJ:
            load_wj(j + 1)
        for t in range(TC):
            tb = t % 2
            pg = t % 2
            k.wait("tensor", sWJ[buf], wj_val[buf])
            k.wait("tensor", sXQL[tb], xql_val[tb])
            k.wmark("tensor", ps_g_free[pg])
            for dc in range(DC):
                k.emit("tensor", lambda h, pg=pg, buf=buf, tb=tb, dc=dc:
                       h.matmul(ps_g[pg][:, :],
                                xq_sb[:, tb * D + 128 * dc:
                                      tb * D + 128 * (dc + 1)],
                                w1g[:, buf * DC * 512 + dc * 512:
                                    buf * DC * 512 + (dc + 1) * 512],
                                start=(dc == 0), stop=(dc == DC - 1)), pe)
            g_mk = pe.val
            k.wmark("tensor", ps_u_free[pg])
            for dc in range(DC):
                k.emit("tensor", lambda h, pg=pg, buf=buf, tb=tb, dc=dc:
                       h.matmul(ps_u[pg][:, :],
                                xq_sb[:, tb * D + 128 * dc:
                                      tb * D + 128 * (dc + 1)],
                                w1u[:, buf * DC * 512 + dc * 512:
                                    buf * DC * 512 + (dc + 1) * 512],
                                start=(dc == 0), stop=(dc == DC - 1)), pe)
            u_mk = pe.val
            xq_pe_done[tb] = pe.val
            wj_pe_done[buf] = pe.val
            if t + 2 < TC:
                load_xq(t + 2)
            elif j + 1 < NJ:
                load_xq(t + 2)
            k.wait("scalar", pe, g_mk)
            k.wmark("scalar", sg_free[pg])
            silu_mk = k.emit(
                "scalar", lambda h, pg=pg: h.activation(
                    sg_sb[:, pg * 512:(pg + 1) * 512], ps_g[pg][:, :],
                    AF.Silu), act)
            ps_g_free[pg] = (act, silu_mk)
            k.wait("vector", act, silu_mk)
            k.wait("vector", pe, u_mk)
            k.wait("vector", sHS[pg], hst_free[pg])
            h_mk = k.emit("vector", lambda h, pg=pg: h.tensor_tensor(
                hst[:, pg * 512:(pg + 1) * 512],
                sg_sb[:, pg * 512:(pg + 1) * 512], ps_u[pg][:, :],
                ALU.mult), dve)
            ps_u_free[pg] = (dve, h_mk)
            sg_free[pg] = (dve, h_mk)
            k.wait("sync", dve, h_mk)
            k.emit("sync", lambda h, t=t, j=j, pg=pg: h.dma_start(
                h_hbm[t, :, 512 * j:512 * (j + 1)],
                hst[:, pg * 512:(pg + 1) * 512]), sHS[pg], 16)
            hst_free[pg] = sHS[pg].val

    esD1.close()

    if "P" not in phases:
        k.barrier()
        k.play()
        return nc

    k.barrier()
    esD2p = ExitStack()
    h_sb = k.sb(esD2p, "h_sb", [128, 2 * I_FF], BF16)
    hq_sb = k.sb(esD2p, "hq_sb", [128, 2 * I_FF], BF16)
    hqf = k.sb(esD2p, "hqf", [128, I_FF], F32)
    hqstg = k.sb(esD2p, "hqstg", [128, 2 * I_FF], BF16)
    w2b0 = k.sb(esD2p, "w2b0", [128, IC * 512], BF16)
    foA = k.sb(esD2p, "foA", [128, 2 * 512], F32)
    qmD3 = k.sb(esD2p, "qmD3", [128, 1], F32)
    qscD3 = k.sb(esD2p, "qscD3", [128, 1], F32)
    qrcD3 = k.sb(esD2p, "qrcD3", [128, 1], F32)
    ps_tF = [k.ps(esD2p, f"ps_tF{i}", [128, 512]) for i in range(2)]
    ps_d = [k.ps(esD2p, f"ps_d{i}", [128, 512]) for i in range(2)]

    k.emit("sync", lambda h: h.dma_start(
        w2b0[:, :].rearrange("p (ic d) -> p ic d", ic=IC),
        w2qT_hbm[:, :, :].rearrange("ic p d -> p ic d")[:, :, 0:512]),
        sW2B[0], 16)
    w2b0_val = sW2B[0].val

    h_free = [None, None]
    hl_val = [0, 0]
    hqstg_free = [(0, 0), (0, 0)]
    tD_free = [None, None]
    hq_rd = [0, 0]
    fo_free = [None, None]
    fo_dma_mk = [0, 0]

    def load_h(t):
        buf = t % 2
        k.wmark("sync", h_free[buf])
        k.emit("sync", lambda h, t=t, buf=buf: h.dma_start(
            h_sb[:, buf * I_FF:(buf + 1) * I_FF], h_hbm[t, :, :]),
            sHL[buf], 16)
        hl_val[buf] = sHL[buf].val

    load_h(0)
    load_h(1)
    for t in range(TC):
        tb = t % 2
        src = h_sb[:, tb * I_FF:(tb + 1) * I_FF]
        k.wait("vector", sHL[tb], hl_val[tb])
        k.wait("vector", dve)
        k.emit("vector", lambda h, src=src: h.reduce_max(
            qmD3[:, :], src, axis=AX.X, apply_absolute_value=True), dve)
        k.wait("vector", dve)
        k.emit("vector", lambda h: h.tensor_scalar(
            qscD3[:, :], qmD3[:, :], 1.0 / QMAX, EPS_Q, ALU.mult, ALU.max),
            dve)
        k.wait("vector", dve)
        k.emit("vector", lambda h: h.reciprocal(qrcD3[:, :], qscD3[:, :]),
               dve)
        k.wait("vector", dve)
        hf_mk = k.emit("vector", lambda h, src=src: h.tensor_scalar(
            hqf[:, :], src, qrcD3[:, :], MAGIC, ALU.mult, ALU.add), dve)
        h_free[tb] = (dve, hf_mk)
        k.wait("vector", dve)
        k.wait("vector", pe, hq_rd[tb])
        hq_mk = k.emit("vector", lambda h, tb=tb: h.tensor_scalar(
            hq_sb[:, tb * I_FF:(tb + 1) * I_FF], hqf[:, :], MAGIC,
            qscD3[:, :], ALU.subtract, ALU.mult), dve)
        if t + 2 < TC:
            load_h(t + 2)
        for g in range(11):
            pt = g % 2
            k.wait("tensor", dve, hq_mk)
            k.wmark("tensor", tD_free[pt])
            for jj in range(4):
                ic = 4 * g + jj
                k.emit("tensor", lambda h, pt=pt, jj=jj, ic=ic, tb=tb:
                       h.matmul(
                    ps_tF[pt][:, 128 * jj:128 * (jj + 1)],
                    hq_sb[:, tb * I_FF + 128 * ic:tb * I_FF + 128 * (ic + 1)],
                    cid_sb[:, :], start=True, stop=True), pe)
            k.wait("scalar", pe)
            k.wait("scalar", sHQS[tb], hqstg_free[tb][0])
            k.wait("scalar", pe, hqstg_free[tb][1])
            mk = k.emit(
                "scalar", lambda h, pt=pt, g=g, tb=tb: h.copy(
                    hqstg[:, tb * I_FF + 512 * g:tb * I_FF + 512 * (g + 1)],
                    ps_tF[pt][:, :]), act)
            tD_free[pt] = (act, mk)
        hq_rd[tb] = pe.val
        k.wait("sync", act)
        k.emit("sync", lambda h, t=t, tb=tb: h.dma_start(
            hqT_hbm[t, :, :], hqstg[:, tb * I_FF:(tb + 1) * I_FF]),
            sHQS[tb], 16)
        pd = t % 2
        k.wait("tensor", sW2B[0], w2b0_val)
        k.wait("tensor", act)
        k.wmark("tensor", fo_free[pd])
        for ic in range(IC):
            k.emit("tensor", lambda h, pd=pd, tb=tb, ic=ic: h.matmul(
                ps_d[pd][:, :],
                hqstg[:, tb * I_FF + 128 * ic:tb * I_FF + 128 * (ic + 1)],
                w2b0[:, ic * 512:(ic + 1) * 512],
                start=(ic == 0), stop=(ic == IC - 1)), pe)
        d_mk = pe.val
        hqstg_free[tb] = (sHQS[tb].val, d_mk)
        k.wait("scalar", pe, d_mk)
        k.wait("scalar", sFS[pd], fo_dma_mk[pd])
        mk = k.emit("scalar", lambda h, pd=pd: h.copy(
            foA[:, pd * 512:(pd + 1) * 512], ps_d[pd][:, :]), act)
        fo_free[pd] = (act, mk)
        k.wait("sync", act, mk)
        k.emit("sync", lambda h, t=t, pd=pd: h.dma_start(
            ffn_hbm[t, :, 0:512],
            foA[:, pd * 512:(pd + 1) * 512]), sFS[pd], 16)
        fo_dma_mk[pd] = sFS[pd].val

    esD2p.close()

    k.barrier()
    esD2 = ExitStack()
    w2b = k.sb(esD2, "w2b", [128, 2 * IC * 512], BF16)
    hqT_sb = k.sb(esD2, "hqT_sb", [128, 2 * I_FF], BF16)
    foB = k.sb(esD2, "foB_sb", [128, 2 * 512], F32)
    fe_sb = k.sb(esD2, "fe_sb", [128, 2 * D], F32)
    x2e_sb = k.sb(esD2, "x2e_sb", [128, 2 * D], F32)
    fln_sb = k.sb(esD2, "fln_sb", [128, 2 * D], F32)
    o_sb = k.sb(esD2, "o_sb", [128, 2 * D], F32)
    statE = k.sb(esD2, "statE", [128, 2 * 24], F32)
    mvE = k.sb(esD2, "mvE", [128, 2 * 2], F32)
    veE = k.sb(esD2, "veE", [128, 2], F32)
    sdE = k.sb(esD2, "sdE", [128, 2], F32)
    rstdE = k.sb(esD2, "rstdE", [128, 2], F32)
    negmrE = k.sb(esD2, "negmrE", [128, 2], F32)
    ps_dB = [k.ps(esD2, f"ps_dB{i}", [128, 512]) for i in range(2)]

    w2b_val = [0, 0]
    hql_val = [0, 0]
    w2b_pe_done = [0, 0]
    hq_pe_done = [0, 0]
    foB_free = [None, None]
    foB_dma_mk = [0, 0]
    fs3_mark = {}

    def load_w2b(db):
        buf = db % 2
        k.wait("sync", pe, w2b_pe_done[buf])
        k.emit("sync", lambda h, db=db, buf=buf: h.dma_start(
            w2b[:, buf * IC * 512:(buf + 1) * IC * 512].rearrange(
                "p (ic d) -> p ic d", ic=IC),
            w2qT_hbm[:, :, :].rearrange("ic p d -> p ic d")[
                :, :, 512 * db:512 * (db + 1)]), sW2B[buf], 16)
        w2b_val[buf] = sW2B[buf].val

    def load_hqT(t):
        buf = t % 2
        k.wait("sync", pe, hq_pe_done[buf])
        k.emit("sync", lambda h, t=t, buf=buf: h.dma_start(
            hqT_sb[:, buf * I_FF:(buf + 1) * I_FF], hqT_hbm[t % TC, :, :]),
            sHQL[buf], 16)
        hql_val[buf] = sHQL[buf].val

    e_free = [None, None]
    el_val = [0, 0]
    o_free = [0, 0]
    fln_rd = [0, 0]

    def load_e(t):
        buf = t % 2
        k.wmark("sync", e_free[buf])
        k.wait("sync", sEL[buf], sEL[buf].val)
        k.wait("sync", sFS[t % 2], fs3_mark.get(t, 0))
        k.emit("sync", lambda h, t=t, buf=buf: h.dma_start(
            fe_sb[:, buf * D:(buf + 1) * D], ffn_hbm[t, :, :]),
            sEL[buf], 16, batched=True)
        k.emit("sync", lambda h, t=t, buf=buf: h.dma_start(
            x2e_sb[:, buf * D:(buf + 1) * D],
            x2_hbm[128 * t:128 * (t + 1), :]), sEL[buf], 16, batched=True)
        el_val[buf] = sEL[buf].val

    def e_step(t):
        tb = t % 2
        src = fe_sb[:, tb * D:(tb + 1) * D]
        k.wait("vector", sEL[tb], el_val[tb])
        k.wait("vector", dve)
        for c in range(4):
            k.emit("vector", lambda h, c=c, src=src, tb=tb: h.bn_stats(
                statE[:, tb * 24 + 6 * c:tb * 24 + 6 * (c + 1)],
                src[:, 512 * c:512 * (c + 1)]), dve)
        k.wait("vector", dve)
        k.emit("vector", lambda h, tb=tb: h.bn_aggr(
            mvE[:, 2 * tb:2 * tb + 2], statE[:, tb * 24:tb * 24 + 24]), dve)
        k.wait("vector", dve)
        k.emit("vector", lambda h, tb=tb: h.tensor_scalar(
            veE[:, tb:tb + 1], mvE[:, 2 * tb + 1:2 * tb + 2], EPS_LN, None,
            ALU.add), dve)
        k.wait("scalar", dve)
        k.emit("scalar", lambda h, tb=tb: h.activation(
            sdE[:, tb:tb + 1], veE[:, tb:tb + 1], AF.Sqrt), act)
        k.wait("vector", act)
        k.emit("vector", lambda h, tb=tb: h.reciprocal(
            rstdE[:, tb:tb + 1], sdE[:, tb:tb + 1]), dve)
        k.wait("vector", dve)
        k.emit("vector", lambda h, tb=tb: h.tensor_scalar(
            negmrE[:, tb:tb + 1], mvE[:, 2 * tb:2 * tb + 1],
            rstdE[:, tb:tb + 1], -1.0, ALU.mult, ALU.mult), dve)
        k.wait("scalar", dve)
        k.wait("scalar", dve, fln_rd[tb])
        fln_mk = k.emit("scalar", lambda h, src=src, tb=tb: h.activation(
            fln_sb[:, tb * D:(tb + 1) * D], src, AF.Identity,
            bias=negmrE[:, tb:tb + 1], scale=rstdE[:, tb:tb + 1]), act)
        k.wait("vector", act, fln_mk)
        k.wait("vector", sOUT, o_free[tb])
        o_mk = k.emit("vector", lambda h, tb=tb: h.tensor_tensor(
            o_sb[:, tb * D:(tb + 1) * D], fln_sb[:, tb * D:(tb + 1) * D],
            x2e_sb[:, tb * D:(tb + 1) * D], ALU.add), dve)
        fln_rd[tb] = o_mk
        e_free[tb] = (dve, o_mk)
        k.wait("sync", dve, o_mk)
        k.emit("sync", lambda h, t=t, tb=tb: h.dma_start(
            out_t[128 * t:128 * (t + 1), :], o_sb[:, tb * D:(tb + 1) * D]),
            sOUT, 16)
        o_free[tb] = sOUT.val

    load_w2b(1)
    load_hqT(0)
    load_hqT(1)
    for db in range(1, 4):
        buf = db % 2
        if db + 1 < 4:
            load_w2b(db + 1)
        for t in range(TC):
            tb = t % 2
            pd = t % 2
            k.wait("tensor", sW2B[buf], w2b_val[buf])
            k.wait("tensor", sHQL[tb], hql_val[tb])
            k.wmark("tensor", foB_free[pd])
            for ic in range(IC):
                k.emit("tensor", lambda h, pd=pd, tb=tb, buf=buf, ic=ic:
                       h.matmul(ps_dB[pd][:, :],
                                hqT_sb[:, tb * I_FF + 128 * ic:
                                       tb * I_FF + 128 * (ic + 1)],
                                w2b[:, buf * IC * 512 + ic * 512:
                                    buf * IC * 512 + (ic + 1) * 512],
                                start=(ic == 0), stop=(ic == IC - 1)), pe)
            d_mk = pe.val
            hq_pe_done[tb] = d_mk
            w2b_pe_done[buf] = d_mk
            if t + 2 < TC:
                load_hqT(t + 2)
            elif db + 1 < 4:
                load_hqT(t + 2)
            k.wait("scalar", pe, d_mk)
            k.wait("scalar", sFS[pd], foB_dma_mk[pd])
            mk = k.emit("scalar", lambda h, pd=pd: h.copy(
                foB[:, pd * 512:(pd + 1) * 512], ps_dB[pd][:, :]), act)
            foB_free[pd] = (act, mk)
            k.wait("sync", act, mk)
            k.emit("sync", lambda h, t=t, db=db, pd=pd: h.dma_start(
                ffn_hbm[t, :, 512 * db:512 * (db + 1)],
                foB[:, pd * 512:(pd + 1) * 512]), sFS[pd], 16)
            foB_dma_mk[pd] = sFS[pd].val
            if db == 3:
                fs3_mark[t] = sFS[pd].val
                if t == 1:
                    load_e(0)
                    load_e(1)
                if t >= 3 and (t - 3) < TC:
                    e_step(t - 3)
                    if t - 3 + 2 < TC:
                        load_e(t - 3 + 2)
    e_step(TC - 3)
    load_e(TC - 1)
    e_step(TC - 2)
    e_step(TC - 1)

    esD2.close()

    k.wait("sync", sOUT)
    k.barrier()

    k.play()
    esG.close()
    k.es.close()
    return nc


def make_consts(L):
    d32 = np.arange(RD)
    theta = np.exp((d32 // 2 * 2).astype(np.float64) *
                   (-np.log(10000.0) / RD))
    pos = np.arange(L, dtype=np.float64)
    ang = pos[None, :] * theta[:, None]
    ra = np.sin(ang)
    rb = np.cos(ang)
    ra4 = np.tile(ra, (4, 1)).astype(ml_dtypes.bfloat16)
    rb4 = np.tile(rb, (4, 1)).astype(ml_dtypes.bfloat16)
    perm = np.zeros((128, 128), np.float32)
    for blk in range(4):
        for j in range(16):
            perm[blk * 32 + 2 * j + 1, blk * 32 + 2 * j] = -1.0
            perm[blk * 32 + 2 * j, blk * 32 + 2 * j + 1] = 1.0
    return {
        "c_id": np.eye(128, dtype=ml_dtypes.bfloat16),
        "c_perm": perm.astype(ml_dtypes.bfloat16),
        "c_ra": np.ascontiguousarray(ra4),
        "c_rb": np.ascontiguousarray(rb4),
    }


def kernel(L0_attn_h, **w):
    L = L0_attn_h.shape[1]
    nc = build_nc(L=L)
    consts = make_consts(L)
    in_maps = []
    for core in range(8):
        m = {"x_in": np.ascontiguousarray(L0_attn_h[core])}
        for nm in ("W_dkv b_dkv W_uk b_uk W_uv b_uv W_dq b_dq W_uq b_uq "
                   "W_qr b_qr W_kr b_kr W_fc b_fc").split():
            m[nm] = w["L0_attn_" + nm]
        m["w_gate_up"] = w["L0_ffn_w_gate_up"]
        m["w_down"] = w["L0_ffn_w_down"]
        m.update(consts)
        in_maps.append(m)
    res = run_bass_kernel_spmd(nc, in_maps, core_ids=list(range(8)))
    return np.stack([np.asarray(r["out"]) for r in res.results], axis=0)


# version 35
# speedup vs baseline: 1.0564x; 1.0564x over previous
import numpy as np
import ml_dtypes
import concourse.bass as bass
import concourse.mybir as mybir
from concourse.bass_utils import run_bass_kernel_spmd
from contextlib import ExitStack

F32 = mybir.dt.float32
BF16 = mybir.dt.bfloat16
AF = mybir.ActivationFunctionType
ALU = mybir.AluOpType
AX = mybir.AxisListType

D = 2048
DC = D // 128
CKV = 512
UP = 1024
NH = 16
VD = 64
RD = 32
HD = 96
I_FF = 5632
IC = I_FF // 128
GU = 2 * I_FF
SCALE = 1.0 / (NH ** 0.5 + RD ** 0.5)
EPS_LN = 1e-5
QMAX = 127.0
EPS_Q = 1e-8
MAGIC = 12582912.0

ENGS = ("sync", "tensor", "vector", "scalar", "gpsimd")


class Sem:
    def __init__(self, k, name):
        self.h = k.es.enter_context(k.nc.semaphore(name))
        self.val = 0
        self.name = name


class KB:

    def __init__(self, nc):
        self.nc = nc
        self.es = ExitStack()
        self.ops = {e: [] for e in ENGS}
        self.waited = {}
        self.sems = {}

    def sem(self, name):
        s = Sem(self, name)
        self.sems[name] = s
        return s

    def sb(self, es, name, shape, dtype):
        return es.enter_context(self.nc.sbuf_tensor(name, shape, dtype))

    def ps(self, es, name, shape):
        return es.enter_context(self.nc.psum_tensor(name, shape, F32))

    def wait(self, eng, sem, val=None):
        v = sem.val if val is None else val
        if v <= 0:
            return
        key = (eng, sem.name)
        if self.waited.get(key, -1) >= v:
            return
        self.waited[key] = v

        def fn(h, sem=sem, v=v):
            h.wait_ge(sem.h, v)
        self.ops[eng].append(fn)

    def wmark(self, eng, mark):
        if mark:
            self.wait(eng, mark[0], mark[1])

    def emit(self, eng, build, inc=None, incn=1, batched=False):
        if inc is not None:
            if incn == 16 and not batched:
                self.wait(eng, inc, inc.val)
            inc.val += incn

        def fn(h, build=build, inc=inc, incn=incn):
            ins = build(h)
            if inc is not None:
                ins.then_inc(inc.h, incn)
        self.ops[eng].append(fn)
        return inc.val if inc is not None else None

    def barrier(self):
        for e in ENGS:
            for s in self.sems.values():
                self.wait(e, s)

    def play(self):
        with self.nc.Block() as blk:
            for e in ENGS:
                sub = self.ops[e]

                def run(h, sub=sub):
                    for f in sub:
                        f(h)
                getattr(blk, e)(run)
        self.ops = {e: [] for e in ENGS}


def build_nc(L=4096, dbg=(), phases="ABCD01P2E"):
    import os
    phases = os.environ.get("KPHASES", phases)
    nc = bass.Bass("TRN2", target_bir_lowering=False)
    TB = L // 512
    TC = L // 128
    QBL = min(L, 1024)
    QB = L // QBL
    QJ = QBL // 512
    KC = L // 128

    dt = nc.dram_tensor
    x_in = dt("x_in", [L, D], F32, kind="ExternalInput")
    W = {}
    for nm, shp in [("W_dkv", [D, CKV]), ("b_dkv", [CKV]),
                    ("W_uk", [CKV, UP]), ("b_uk", [UP]),
                    ("W_uv", [CKV, UP]), ("b_uv", [UP]),
                    ("W_dq", [D, CKV]), ("b_dq", [CKV]),
                    ("W_uq", [CKV, UP]), ("b_uq", [UP]),
                    ("W_qr", [CKV, NH * RD]), ("b_qr", [NH * RD]),
                    ("W_kr", [D, RD]), ("b_kr", [RD]),
                    ("W_fc", [UP, D]), ("b_fc", [D]),
                    ("w_gate_up", [GU, D]), ("w_down", [D, I_FF])]:
        W[nm] = dt(nm, shp, F32, kind="ExternalInput")
    c_id = dt("c_id", [128, 128], BF16, kind="ExternalInput")
    c_perm = dt("c_perm", [128, 128], BF16, kind="ExternalInput")
    c_ra = dt("c_ra", [128, L], BF16, kind="ExternalInput")
    c_rb = dt("c_rb", [128, L], BF16, kind="ExternalInput")
    out_t = dt("out", [L, D], F32, kind="ExternalOutput")

    def scratch(name, shape, dtyp):
        kind = "ExternalOutput" if name in dbg else "Internal"
        return dt(name, shape, dtyp, kind=kind)

    k_hbm = scratch("k_hbm", [NH, HD, L], BF16)
    q_hbm = scratch("q_hbm", [NH, HD, L], BF16)
    v_hbm = scratch("v_hbm", [KC, 128, NH * 65], BF16)
    x2_hbm = scratch("x2_hbm", [L, D], F32)
    xq_hbm = scratch("xq_hbm", [TC, 128, D], BF16)
    w1qT_hbm = scratch("w1qT_hbm", [DC, 128, GU], BF16)
    w2qT_hbm = scratch("w2qT_hbm", [IC, 128, D], BF16)
    h_hbm = scratch("h_hbm", [TC, 128, I_FF], BF16)
    hqT_hbm = scratch("hqT_hbm", [TC, 128, I_FF], BF16)
    ffn_hbm = scratch("ffn_hbm", [TC, 128, D], F32)
    ao_dbg = scratch("ao_dbg", [UP, L], BF16) if "ao_dbg" in dbg else None

    k = KB(nc)
    pe = k.sem("pe")
    dve = k.sem("dve")
    act = k.sem("act")
    pool = k.sem("pool")
    sWL = k.sem("sWL")
    sBL = k.sem("sBL")
    sRT = [k.sem(f"sRT{i}") for i in range(2)]
    sX = [k.sem(f"sX{i}") for i in range(2)]
    sKQ = k.sem("sKQ")
    sVL = k.sem("sVL")
    sKH = [k.sem(f"sKH{i}") for i in range(2)]
    sQH = [k.sem(f"sQH{i}") for i in range(2)]
    sX2 = k.sem("sX2")
    sXQ = [k.sem(f"sXQ{i}") for i in range(2)]
    sW1 = [k.sem(f"sW1{i}") for i in range(2)]
    sST = k.sem("sST")
    sW2 = k.sem("sW2")
    sWJ = [k.sem(f"sWJ{i}") for i in range(2)]
    sXQL = [k.sem(f"sXQL{i}") for i in range(2)]
    sHS = [k.sem(f"sHS{i}") for i in range(2)]
    sHL = [k.sem(f"sHL{i}") for i in range(2)]
    sHQS = [k.sem(f"sHQS{i}") for i in range(2)]
    sW2B = [k.sem(f"sW2B{i}") for i in range(2)]
    sHQL = [k.sem(f"sHQL{i}") for i in range(2)]
    sFS = [k.sem(f"sFS{i}") for i in range(2)]
    sEL = [k.sem(f"sEL{i}") for i in range(2)]
    sOUT = k.sem("sOUT")

    esG = ExitStack()
    cid_sb = k.sb(esG, "cid", [128, 128], BF16)
    ones1_128 = k.sb(esG, "ones1_128", [1, 128], BF16)
    ones1_64 = k.sb(esG, "ones1_64", [1, 64], BF16)
    ones96_16 = k.sb(esG, "ones96_16", [96, 16], BF16)
    negc_bc = k.sb(esG, "negc_bc", [128, 16], F32)

    k.emit("sync", lambda h: h.dma_start(cid_sb[:, :], c_id[:, :]), sBL, 16)
    for m in (ones1_128, ones1_64, ones96_16):
        k.emit("gpsimd", lambda h, m=m: h.memset(m[:, :], 1.0), pool)

    esA = ExitStack()
    Wdkv = k.sb(esA, "Wdkv", [128, DC * 512], BF16)
    Wdq = k.sb(esA, "Wdq", [128, DC * 512], BF16)
    Wuk = k.sb(esA, "Wuk", [128, 4 * UP], BF16)
    Wuv = k.sb(esA, "Wuv", [128, 4 * UP], BF16)
    Wuq = k.sb(esA, "Wuq", [128, 4 * UP], BF16)
    Wqr = k.sb(esA, "Wqr", [128, 4 * 512], BF16)
    Wkr = k.sb(esA, "Wkr", [128, DC * RD], BF16)
    bdkv = k.sb(esA, "bdkv", [128, 4], F32)
    bdq = k.sb(esA, "bdq", [128, 4], F32)
    buk = k.sb(esA, "buk", [128, 8], F32)
    buq = k.sb(esA, "buq", [128, 8], F32)
    bqr = k.sb(esA, "bqr", [128, 4], F32)
    bkr = k.sb(esA, "bkr", [32, 1], F32)
    buv_row = k.sb(esA, "buv_row", [1, UP], BF16)
    buvf = k.sb(esA, "buvf", [1, UP], F32)
    buv_bc = k.sb(esA, "buv_bc", [128, UP], F32)
    ra_b = k.sb(esA, "ra_b", [128, 2 * 512], BF16)
    rb_b = k.sb(esA, "rb_b", [128, 2 * 512], BF16)
    perm_sb = k.sb(esA, "perm_sb", [128, 128], BF16)
    x_sb = k.sb(esA, "x_sb", [128, 2 * D], F32)
    xln_tm = k.sb(esA, "xln_tm", [128, 4 * D], BF16)
    x_lnT = k.sb(esA, "x_lnT", [128, 2 * DC * 512], BF16)
    ckv_sb = k.sb(esA, "ckv_sb", [128, 4 * 512], BF16)
    cq_sb = k.sb(esA, "cq_sb", [128, 4 * 512], BF16)
    kst = k.sb(esA, "kst", [128, NH * 512], BF16)
    qst = k.sb(esA, "qst", [128, NH * 512], BF16)
    vstg = k.sb(esA, "vstg", [128, 4 * NH * 65], BF16)
    qrp_sb = k.sb(esA, "qrp_sb", [128, 512], BF16)
    qre_sb = k.sb(esA, "qre_sb", [128, 512], BF16)
    krp_sb = k.sb(esA, "krp_sb", [32, 512], BF16)
    kre_sb = k.sb(esA, "kre_sb", [32, 512], BF16)
    q2m = k.sb(esA, "q2m", [128, 512], F32)
    qam = k.sb(esA, "qam", [128, 512], F32)
    stat = k.sb(esA, "stat", [128, 24], F32)
    mv = k.sb(esA, "mv", [128, 2], F32)
    ve = k.sb(esA, "ve", [128, 1], F32)
    sd = k.sb(esA, "sd", [128, 1], F32)
    rstd = k.sb(esA, "rstd", [128, 1], F32)
    negmr = k.sb(esA, "negmr", [128, 1], F32)

    ps_tr = [k.ps(esA, f"ps_tr{i}", [128, 512]) for i in range(2)]
    ps_c = [k.ps(esA, f"ps_c{i}", [128, 512]) for i in range(2)]
    ps_o = [k.ps(esA, f"ps_o{i}", [128, 512]) for i in range(2)]
    ps_small = k.ps(esA, "ps_small", [32, 512])

    def stage_cast(dst_sb, src3d, nchunks, ow, stg, ld_sem):
        per = max(1, 4096 // ow)
        c0 = 0
        while c0 < nchunks:
            c1 = min(nchunks, c0 + per)
            n = (c1 - c0) * ow
            k.wait("sync", dve)
            k.emit("sync", lambda h, src3d=src3d, c0=c0, c1=c1, n=n, stg=stg:
                   h.dma_start(stg[:, 0:n].rearrange(
                       "p (c o) -> p c o", c=c1 - c0), src3d[:, c0:c1, :]),
                   ld_sem, 16)
            k.wait("vector", ld_sem)
            k.emit("vector", lambda h, dst_sb=dst_sb, c0=c0, n=n, ow=ow,
                   stg=stg: h.tensor_copy(
                       dst_sb[:, c0 * ow:c0 * ow + n], stg[:, 0:n]), dve)
            c0 = c1

    stage_cast(Wdkv, W["W_dkv"][:, :].rearrange("(c p) o -> p c o", p=128),
               DC, 512, x_sb, sWL)
    stage_cast(Wdq, W["W_dq"][:, :].rearrange("(c p) o -> p c o", p=128),
               DC, 512, x_sb, sWL)
    stage_cast(Wuk, W["W_uk"][:, :].rearrange("(c p) o -> p c o", p=128),
               4, UP, x_sb, sWL)
    stage_cast(Wuv, W["W_uv"][:, :].rearrange("(c p) o -> p c o", p=128),
               4, UP, x_sb, sWL)
    stage_cast(Wuq, W["W_uq"][:, :].rearrange("(c p) o -> p c o", p=128),
               4, UP, x_sb, sWL)
    stage_cast(Wqr, W["W_qr"][:, :].rearrange("(c p) o -> p c o", p=128),
               4, 512, x_sb, sWL)
    stage_cast(Wkr, W["W_kr"][:, :].rearrange("(c p) o -> p c o", p=128),
               DC, RD, x_sb, sWL)
    k.emit("sync", lambda h: h.dma_start(
        buvf[:, :], W["b_uv"][:].rearrange("(o c) -> o c", o=1)), sWL, 16)
    k.wait("vector", sWL)
    k.emit("vector", lambda h: h.tensor_copy(buv_row[:, :], buvf[:, :]), dve)

    def f32_load(dst, src_ap):
        k.emit("sync", lambda h, dst=dst, src_ap=src_ap:
               h.dma_start(dst, src_ap, allow_slow_non_contiguous=True),
               sBL, 16)

    f32_load(bdkv[:, :], W["b_dkv"][:].rearrange("(c p) -> p c", p=128))
    f32_load(bdq[:, :], W["b_dq"][:].rearrange("(c p) -> p c", p=128))
    f32_load(buk[:, :], W["b_uk"][:].rearrange("(c p) -> p c", p=128))
    f32_load(buq[:, :], W["b_uq"][:].rearrange("(c p) -> p c", p=128))
    f32_load(bqr[:, :], W["b_qr"][:].rearrange("(c p) -> p c", p=128))
    f32_load(bkr[:, :], W["b_kr"][:].rearrange("(p c) -> p c", c=1))
    f32_load(perm_sb[:, :], c_perm[:, :])

    k.wait("tensor", sWL)
    k.wait("tensor", dve)
    k.wait("tensor", sBL)
    k.wait("tensor", pool)
    k.wait("scalar", sBL)
    k.wait("vector", sBL)
    for j in range(2):
        k.emit("tensor", lambda h, j=j: h.matmul(
            ps_c[j][:, :], ones1_128[:, :], buv_row[:, 512 * j:512 * (j + 1)],
            start=True, stop=True), pe)
    k.wait("scalar", pe)
    buv_mk = 0
    for j in range(2):
        buv_mk = k.emit("scalar", lambda h, j=j: h.copy(
            buv_bc[:, 512 * j:512 * (j + 1)], ps_c[j][:, :]), act)

    x_free = [(dve, dve.val), (dve, dve.val)]
    rt_free = [None, None]
    ps_tr_free = [None, None]
    ps_c_free = [(act, buv_mk), (act, buv_mk)]
    ps_o_free = [None, None]
    kqv_store_mark = 0

    def ln_stats(src, n512):
        k.wait("vector", dve)
        for c in range(n512):
            k.emit("vector", lambda h, c=c, src=src: h.bn_stats(
                stat[:, 6 * c:6 * c + 6], src[:, 512 * c:512 * (c + 1)]),
                dve)
        k.wait("vector", dve)
        k.emit("vector", lambda h, n512=n512: h.bn_aggr(
            mv[:, :], stat[:, 0:6 * n512]), dve)
        k.wait("vector", dve)
        k.emit("vector", lambda h: h.tensor_scalar(
            ve[:, :], mv[:, 1:2], EPS_LN, None, ALU.add), dve)
        k.wait("scalar", dve)
        k.emit("scalar", lambda h: h.activation(sd[:, :], ve[:, :], AF.Sqrt),
               act)
        k.wait("vector", act)
        k.emit("vector", lambda h: h.reciprocal(rstd[:, :], sd[:, :]), dve)
        k.wait("vector", dve)
        k.emit("vector", lambda h: h.tensor_scalar(
            negmr[:, :], mv[:, 0:1], rstd[:, :], -1.0, ALU.mult, ALU.mult),
            dve)

    xlnT_mk = [0, 0]
    xlnT_rd = [0, 0]
    rt_val = [0, 0]

    nrm_mks = [0, 0, 0, 0]

    def ln_vec(b):
        bb = b % 2
        k.wmark("sync", rt_free[bb])
        k.emit("sync", lambda h, b=b, bb=bb: h.dma_start(
            ra_b[:, 512 * bb:512 * (bb + 1)],
            c_ra[:, 512 * b:512 * (b + 1)]), sRT[bb], 16)
        k.emit("sync", lambda h, b=b, bb=bb: h.dma_start(
            rb_b[:, 512 * bb:512 * (bb + 1)],
            c_rb[:, 512 * b:512 * (b + 1)]), sRT[bb], 16)
        rt_val[bb] = sRT[bb].val
        for cb in range(4):
            tc = 4 * b + cb
            xb = tc % 2
            k.wmark("sync", x_free[xb])
            k.emit("sync", lambda h, tc=tc, xb=xb: h.dma_start(
                x_sb[:, xb * D:(xb + 1) * D],
                x_in[128 * tc:128 * (tc + 1), :]), sX[xb], 16)
            xld_val = sX[xb].val
            xc = x_sb[:, xb * D:(xb + 1) * D]
            k.wait("vector", sX[xb], xld_val)
            ln_stats(xc, 4)
            dst = xln_tm[:, cb * D:(cb + 1) * D]
            k.wait("scalar", dve)
            k.wait("scalar", pe)
            k.wait("scalar", sX[xb], xld_val)
            nrm_mks[cb] = k.emit(
                "scalar", lambda h, dst=dst, xc=xc: h.activation(
                    dst, xc, AF.Identity, bias=negmr[:, :],
                    scale=rstd[:, :]), act)
            x_free[xb] = (act, nrm_mks[cb])

    def ln_pe(b):
        bb = b % 2
        for cb in range(4):
            dst = xln_tm[:, cb * D:(cb + 1) * D]
            for g in range(4):
                pt = g % 2
                k.wait("tensor", act, nrm_mks[cb])
                k.wmark("tensor", ps_tr_free[pt])
                for jj in range(4):
                    dc = 4 * g + jj
                    k.emit("tensor", lambda h, pt=pt, jj=jj, dc=dc, dst=dst:
                           h.matmul(ps_tr[pt][:, 128 * jj:128 * (jj + 1)],
                                    dst[:, 128 * dc:128 * (dc + 1)],
                                    cid_sb[:, :], start=True, stop=True), pe)
                k.wait("scalar", pe)
                k.wait("scalar", pe, xlnT_rd[bb])
                dstT = x_lnT[:, :].rearrange(
                    "p (u dc t) -> p u dc t", u=2, dc=DC)[
                        :, bb, 4 * g:4 * g + 4, 128 * cb:128 * (cb + 1)]
                mk = k.emit(
                    "scalar", lambda h, pt=pt, dstT=dstT: h.copy(
                        dstT, ps_tr[pt][:, :].rearrange(
                            "p (j t) -> p j t", j=4)), act)
                ps_tr_free[pt] = (act, mk)
        xlnT_mk[bb] = ps_tr_free[1][1]

    def rest_part(b):
        nonlocal kqv_store_mark
        bb = b % 2
        base = bb * DC * 512
        t0 = 512 * b
        xT = lambda dc: x_lnT[:, base + dc * 512:base + (dc + 1) * 512]
        k.wait("tensor", act, xlnT_mk[bb])
        c_mk = 0
        for Wsb, bsb, cdst in ((Wdkv, bdkv, ckv_sb), (Wdq, bdq, cq_sb)):
            for cs in range(4):
                pc = cs % 2
                k.wmark("tensor", ps_c_free[pc])
                for dc in range(DC):
                    k.emit("tensor", lambda h, pc=pc, Wsb=Wsb, dc=dc, cs=cs:
                           h.matmul(ps_c[pc][:, :],
                                    Wsb[:, dc * 512 + 128 * cs:
                                        dc * 512 + 128 * (cs + 1)],
                                    xT(dc),
                                    start=(dc == 0), stop=(dc == DC - 1)), pe)
                k.wait("scalar", pe)
                mk = k.emit(
                    "scalar", lambda h, pc=pc, cdst=cdst, bsb=bsb, cs=cs:
                    h.activation(cdst[:, 512 * cs:512 * (cs + 1)],
                                 ps_c[pc][:, :], AF.Identity,
                                 bias=bsb[:, cs:cs + 1]), act)
                ps_c_free[pc] = (act, mk)
                c_mk = mk

        k.wait("tensor", act, c_mk)
        k.wait("scalar", sKQ, kqv_store_mark)
        for Wsb, bsb, sdst, csrc in ((Wuk, buk, kst, ckv_sb),
                                     (Wuq, buq, qst, cq_sb)):
            for oc in range(8):
                po = oc % 2
                k.wmark("tensor", ps_o_free[po])
                for cc in range(4):
                    k.emit("tensor", lambda h, po=po, Wsb=Wsb, oc=oc, cc=cc,
                           csrc=csrc: h.matmul(
                               ps_o[po][:, :],
                               Wsb[:, cc * UP + 128 * oc:
                                   cc * UP + 128 * (oc + 1)],
                               csrc[:, cc * 512:(cc + 1) * 512],
                               start=(cc == 0), stop=(cc == 3)), pe)
                k.wait("vector", pe)
                k.wait("vector", sKQ, kqv_store_mark)
                mk = 0
                for hh in range(2):
                    hd = 2 * oc + hh
                    mk = k.emit(
                        "vector", lambda h, po=po, hh=hh, hd=hd, sdst=sdst,
                        bsb=bsb, oc=oc: h.tensor_scalar(
                            sdst[0:64, 512 * hd:512 * hd + 512],
                            ps_o[po][64 * hh:64 * (hh + 1), :],
                            bsb[64 * hh:64 * (hh + 1), oc:oc + 1], None,
                            ALU.add), dve)
                ps_o_free[po] = (dve, mk)

        k.wait("vector", sKQ, kqv_store_mark)
        k.wait("vector", act, buv_mk)
        k.wait("gpsimd", sKQ, kqv_store_mark)
        vre = vstg[:, :].rearrange("p (c hh d) -> p c hh d", c=4, hh=NH)
        k.emit("gpsimd", lambda h, vre=vre: h.memset(
            vre[:, :, :, 64:65], 1.0), pool)
        for ts in range(4):
            for half in range(2):
                po = (ts * 2 + half) % 2
                k.wmark("tensor", ps_o_free[po])
                for cc in range(4):
                    k.emit("tensor", lambda h, po=po, ts=ts, half=half, cc=cc:
                           h.matmul(ps_o[po][:, :],
                                    ckv_sb[:, cc * 512 + 128 * ts:
                                           cc * 512 + 128 * (ts + 1)],
                                    Wuv[:, cc * UP + 512 * half:
                                        cc * UP + 512 * (half + 1)],
                                    start=(cc == 0), stop=(cc == 3)), pe)
                k.wait("vector", pe)
                vdst = vre[:, ts, 8 * half:8 * half + 8, 0:64]
                mk = k.emit("vector", lambda h, po=po, vdst=vdst, half=half:
                            h.tensor_tensor(
                                vdst, ps_o[po][:, :].rearrange(
                                    "p (hh d) -> p hh d", hh=8),
                                buv_bc[:, 512 * half:512 * (half + 1)]
                                .rearrange("p (hh d) -> p hh d", hh=8),
                                ALU.add), dve)
                ps_o_free[po] = (dve, mk)

        for cc in range(4):
            po = cc % 2
            k.wmark("tensor", ps_o_free[po])
            for c4 in range(4):
                k.emit("tensor", lambda h, po=po, cc=cc, c4=c4: h.matmul(
                    ps_o[po][:, :],
                    Wqr[:, c4 * 512 + 128 * cc:c4 * 512 + 128 * (cc + 1)],
                    cq_sb[:, c4 * 512:(c4 + 1) * 512],
                    start=(c4 == 0), stop=(c4 == 3)), pe)
            k.wait("scalar", pe)
            k.wait("scalar", dve)
            qrp_mk = k.emit("scalar", lambda h, po=po, cc=cc: h.activation(
                qrp_sb[:, :], ps_o[po][:, :], AF.Identity,
                bias=bqr[:, cc:cc + 1]), act)
            ps_o_free[po] = (act, qrp_mk)
            po2 = (cc + 1) % 2
            k.wait("tensor", act, qrp_mk)
            k.wmark("tensor", ps_o_free[po2])
            rot_mk = k.emit("tensor", lambda h, po2=po2: h.matmul(
                ps_o[po2][:, :], perm_sb[:, :], qrp_sb[:, :],
                start=True, stop=True), pe)
            k.wait("vector", pe, rot_mk)
            k.wait("vector", sRT[bb], rt_val[bb])
            k.wait("vector", dve)
            k.emit("vector", lambda h, po2=po2, bb=bb: h.tensor_tensor(
                q2m[:, :], ps_o[po2][:, :],
                rb_b[:, 512 * bb:512 * (bb + 1)], ALU.mult), dve)
            q2_mk = k.emit("vector", lambda h, bb=bb: h.tensor_tensor(
                qam[:, :], qrp_sb[:, :],
                ra_b[:, 512 * bb:512 * (bb + 1)], ALU.mult), dve)
            ps_o_free[po2] = (dve, q2_mk)
            k.wait("vector", dve)
            k.wait("vector", pe)
            qre_mk = k.emit("vector", lambda h: h.tensor_tensor(
                qre_sb[:, :], q2m[:, :], qam[:, :], ALU.add), dve)
            for j in range(4):
                hd = 4 * cc + j
                k.wait("tensor", dve, qre_mk)
                k.wait("tensor", act)
                k.emit("tensor", lambda h, j=j: h.matmul(
                    ps_small[:, :], cid_sb[:, 32 * j:32 * (j + 1)],
                    qre_sb[:, :], start=True, stop=True), pe)
                k.wait("scalar", pe)
                k.emit("scalar", lambda h, hd=hd: h.copy(
                    qst[64:96, 512 * hd:512 * hd + 512], ps_small[:, :]), act)

        k.wait("tensor", act)
        for dc in range(DC):
            k.emit("tensor", lambda h, dc=dc: h.matmul(
                ps_small[:, :], Wkr[:, dc * RD:(dc + 1) * RD],
                xT(dc),
                start=(dc == 0), stop=(dc == DC - 1)), pe)
        k.wait("scalar", pe)
        k.wait("scalar", pool)
        k.wait("scalar", dve)
        krp_mk = k.emit("scalar", lambda h: h.activation(
            krp_sb[:, :], ps_small[:, :], AF.Identity, bias=bkr[:, :]), act)
        k.wait("tensor", act, krp_mk)
        rot_mk = k.emit("tensor", lambda h: h.matmul(
            ps_small[:, :], perm_sb[0:32, 0:32], krp_sb[:, :],
            start=True, stop=True), pe)
        xlnT_rd[bb] = pe.val
        k.wait("vector", pe, rot_mk)
        k.wait("vector", dve)
        k.emit("vector", lambda h, bb=bb: h.tensor_tensor(
            q2m[0:32, :], ps_small[:, :],
            rb_b[0:32, 512 * bb:512 * (bb + 1)], ALU.mult), dve)
        k.emit("vector", lambda h, bb=bb: h.tensor_tensor(
            qam[0:32, :], krp_sb[:, :],
            ra_b[0:32, 512 * bb:512 * (bb + 1)], ALU.mult), dve)
        rt_free[bb] = (dve, dve.val)
        k.wait("vector", dve)
        k.wait("vector", pool)
        kre_mk = k.emit("vector", lambda h: h.tensor_tensor(
            kre_sb[:, :], q2m[0:32, :], qam[0:32, :], ALU.add), dve)
        k.wait("vector", dve, kre_mk)
        k.wait("vector", act)
        k.wait("vector", sKQ, kqv_store_mark)
        for hd in range(NH):
            k.emit("vector", lambda h, hd=hd: h.tensor_copy(
                kst[64:96, 512 * hd:512 * hd + 512], kre_sb[:, :]), dve)

        k.wait("sync", act)
        k.wait("sync", dve)
        for st_, hbm in ((kst, k_hbm), (qst, q_hbm)):
            k.emit("sync", lambda h, st_=st_, hbm=hbm, t0=t0: h.dma_start(
                hbm[:, :, :].rearrange("hh p t -> p hh t")[:, :, t0:t0 + 512],
                st_[0:96, :].rearrange("p (hh t) -> p hh t", hh=NH)),
                sKQ, 16)
        k.emit("sync", lambda h, b=b: h.dma_start(
            v_hbm[:, :, :].rearrange("c p d -> p c d")[:, 4 * b:4 * b + 4, :],
            vstg[:, :].rearrange("p (c d) -> p c d", c=4)), sKQ, 16)
        kqv_store_mark = sKQ.val

    ln_vec(0)
    ln_pe(0)
    for b in range(TB):
        if b + 1 < TB:
            ln_vec(b + 1)
        rest_part(b)
        if b + 1 < TB:
            ln_pe(b + 1)

    esA.close()


    if "B" not in phases:
        k.barrier()
        k.play()
        return nc

    k.barrier()
    esBC = ExitStack()
    ao_sb = k.sb(esBC, "ao_sb", [128, 8 * L], BF16)
    esB = ExitStack()
    v_sb = k.sb(esB, "v_sb", [128, KC * NH * 65], BF16)
    kh_sb = k.sb(esB, "kh_sb", [96, 2 * L], BF16)
    qh_sb = k.sb(esB, "qh_sb", [96, 2 * L], BF16)
    probs = k.sb(esB, "probs", [128, 6 * QBL], BF16)
    sq_sb = k.sb(esB, "sq_sb", [96, 2 * L], BF16)
    qn_h = k.sb(esB, "qn_h", [16, 2], F32)
    tmp16 = k.sb(esB, "tmp16", [16, 1], F32)
    negc1 = k.sb(esB, "negc1", [16, 1], BF16)
    rvb_sb = k.sb(esB, "rvb_sb", [1, QBL], BF16)
    rbc_sb = k.sb(esB, "rbc_sb", [64, QBL], BF16)
    ps_s = [k.ps(esB, f"ps_s{i}", [128, QBL]) for i in range(2)]
    ps_av = k.ps(esB, "ps_av", [65, QBL])
    ps_rb = k.ps(esB, "ps_rb", [128, max(QBL, 1024)])

    negc_mark = {}
    ps_rb_free = None
    sq_rd = 0
    sq_mk = [0, 0]
    sq_pool_mk = 0

    kh_val = [0, 0]
    qh_val = [0, 0]
    head_pe_done = [0, 0]

    def load_head(hd):
        buf = hd % 2
        k.wait("sync", pe, head_pe_done[buf])
        k.wait("sync", dve, sq_pool_mk)
        k.emit("sync", lambda h, hd=hd, buf=buf: h.dma_start(
            kh_sb[:, buf * L:(buf + 1) * L], k_hbm[hd, :, :]), sKH[buf], 16)
        kh_val[buf] = sKH[buf].val
        k.emit("sync", lambda h, hd=hd, buf=buf: h.dma_start(
            qh_sb[:, buf * L:(buf + 1) * L], q_hbm[hd, :, :]), sQH[buf], 16)
        qh_val[buf] = sQH[buf].val

    def head_sq(hd):
        nonlocal sq_pool_mk
        buf = hd % 2
        for which, src_sb in ((0, qh_sb), (1, kh_sb)):
            k.wait("vector", (sQH if which == 0 else sKH)[buf],
                   (qh_val if which == 0 else kh_val)[buf])
            k.wait("vector", pe, sq_rd)
            k.wait("vector", dve)
            sq_mk[which] = k.emit(
                "vector", lambda h, src_sb=src_sb, buf=buf, which=which:
                h.tensor_tensor(sq_sb[:, which * L:(which + 1) * L],
                                src_sb[:, buf * L:(buf + 1) * L],
                                src_sb[:, buf * L:(buf + 1) * L],
                                ALU.mult), dve)
        sq_pool_mk = dve.val

    norm_red = {}

    def head_norm1_part(hd, part):
        nonlocal ps_rb_free, sq_rd
        NJ = L // 512
        if part == 0:
            k.wait("tensor", dve, max(sq_mk))
            k.wmark("tensor", ps_rb_free)
            norm_red.clear()
        for idx in (2 * part, 2 * part + 1):
            which, j = idx // NJ, idx % NJ
            if idx >= 2:
                k.wait("tensor", dve, norm_red[idx - 2])
            nm = k.emit("tensor", lambda h, idx=idx, which=which, j=j:
                        h.matmul(
                            ps_rb[0:16, 512 * (idx % 2):512 * (idx % 2) + 512],
                            ones96_16[:, :],
                            sq_sb[:, which * L + 512 * j:
                                  which * L + 512 * (j + 1)],
                            start=True, stop=True), pe)
            k.wait("vector", pe, nm)
            k.wait("vector", dve)
            norm_red[idx] = k.emit("vector", lambda h, idx=idx: h.reduce_max(
                tmp16[:, :],
                ps_rb[0:16, 512 * (idx % 2):512 * (idx % 2) + 512],
                axis=AX.X), dve)
            k.wait("vector", dve)
            if j == 0:
                k.emit("vector", lambda h, which=which: h.tensor_copy(
                    qn_h[:, which:which + 1], tmp16[:, :]), dve)
            else:
                k.emit("vector", lambda h, which=which: h.tensor_tensor(
                    qn_h[:, which:which + 1], qn_h[:, which:which + 1],
                    tmp16[:, :], ALU.max), dve)
        if 2 * part + 1 == 2 * NJ - 1:
            sq_rd = pe.val
            k.wait("vector", dve)
            k.emit("vector", lambda h: h.tensor_tensor(
                tmp16[:, :], qn_h[:, 0:1], qn_h[:, 1:2], ALU.mult), dve)
            ps_rb_free = (dve, dve.val)

    def head_norm1(hd):
        for p in range(L // 512):
            head_norm1_part(hd, p)

    def head_norm2(hd):
        nonlocal ps_rb_free
        k.wait("scalar", dve, sq_pool_mk)
        k.wait("scalar", dve)
        k.emit("scalar", lambda h: h.activation(tmp16[:, :], tmp16[:, :],
                                                AF.Sqrt), act)
        k.wait("vector", act)
        k.emit("vector", lambda h: h.tensor_scalar(
            negc1[:, :], tmp16[:, :], -SCALE, None, ALU.mult), dve)
        k.wait("tensor", dve)
        k.wmark("tensor", ps_rb_free)
        nb = k.emit("tensor", lambda h: h.matmul(
            ps_rb[0:64, 0:1], ones1_64[:, :], negc1[0:1, :],
            start=True, stop=True), pe)
        k.wait("vector", pe, nb)
        k.emit("vector", lambda h, hd=hd: h.tensor_copy(
            negc_bc[0:64, hd:hd + 1], ps_rb[0:64, 0:1]), dve)
        k.wait("vector", dve)
        negc_mark[hd] = k.emit("vector", lambda h, hd=hd: h.tensor_copy(
            negc_bc[64:128, hd:hd + 1], ps_rb[0:64, 0:1]), dve)
        ps_rb_free = (dve, negc_mark[hd])

    load_head(0)
    pending_epi = None
    k.emit("sync", lambda h: h.dma_start(
        v_sb[:, :].rearrange("p (c d) -> p c d", c=KC),
        v_hbm[:, :, :].rearrange("c p d -> p c d")), sVL, 16)
    head_sq(0)
    head_norm1(0)
    head_norm2(0)
    scores_ok = None
    av_ok = None
    exp_tail = 0
    exp_tail2 = 0
    rvb_rd = 0

    for hd in range(NH):
        buf = hd % 2
        negc_hd_mk = negc_mark[hd]
        for qb in range(QB):
            if hd + 1 < NH and qb == 0:
                load_head(hd + 1)
                if QB == 1:
                    head_sq(hd + 1)
                    head_norm1(hd + 1)
                    head_norm2(hd + 1)
            q0 = qb * QBL
            exp_mark = {}
            av_mark = {}

            def emit_av(c):
                pb = c % 6
                k.wait("tensor", act, exp_mark[c])
                if c == 0:
                    k.wmark("tensor", av_ok)
                for j in range(QJ):
                    k.emit("tensor", lambda h, c=c, pb=pb, j=j, hd=hd:
                           h.matmul(ps_av[:, 512 * j:512 * (j + 1)],
                                    v_sb[:, (NH * 65) * c + 65 * hd:
                                         (NH * 65) * c + 65 * hd + 65],
                                    probs[:, pb * QBL + 512 * j:
                                          pb * QBL + 512 * (j + 1)],
                                    start=(c == 0), stop=(c == KC - 1)), pe)
                av_mark[c] = pe.val

            for c in range(KC):
                pb = c % 6
                sb2 = c % 2
                k.wait("tensor", sKH[buf], kh_val[buf])
                k.wait("tensor", sQH[buf], qh_val[buf])
                k.wait("tensor", sVL, 16)
                if c > 1:
                    k.wait("tensor", act, exp_mark[c - 2])
                elif c == 0:
                    k.wait("tensor", act, exp_tail2)
                    k.wmark("tensor", scores_ok)
                else:
                    k.wait("tensor", act, exp_tail)
                for j in range(QJ):
                    k.emit("tensor", lambda h, buf=buf, c=c, j=j, q0=q0,
                           sb2=sb2:
                           h.matmul(ps_s[sb2][:, 512 * j:512 * (j + 1)],
                                    kh_sb[:, buf * L + 128 * c:
                                          buf * L + 128 * (c + 1)],
                                    qh_sb[:, buf * L + q0 + 512 * j:
                                          buf * L + q0 + 512 * (j + 1)],
                                    start=True, stop=True), pe)
                sc_mk = pe.val
                if c == min(1, KC - 1) and pending_epi is not None:
                    pending_epi()
                    pending_epi = None
                if c >= 5:
                    emit_av(c - 5)
                if hd + 1 < NH and QB > 1 and c == 6 and qb == 0:
                    head_sq(hd + 1)
                if (hd + 1 < NH and QB > 1 and qb == 1 and c >= 5
                        and (c - 5) % 3 == 0 and (c - 5) // 3 < L // 512):
                    head_norm1_part(hd + 1, (c - 5) // 3)
                if hd + 1 < NH and QB > 1 and c == 4 and qb == 2:
                    head_norm2(hd + 1)
                k.wait("scalar", dve, negc_hd_mk)
                k.wait("scalar", pe, sc_mk)
                if c >= 6:
                    k.wait("scalar", pe, av_mark[c - 6])
                exp_mark[c] = k.emit(
                    "scalar", lambda h, pb=pb, hd=hd, sb2=sb2: h.activation(
                        probs[:, pb * QBL:(pb + 1) * QBL], ps_s[sb2][:, :],
                        AF.Exp,
                        bias=negc_bc[:, hd:hd + 1], scale=SCALE), act)
            for cc in range(max(0, KC - 5), KC):
                emit_av(cc)
            k.wait("vector", pe, av_mark[KC - 1])
            k.wait("vector", pe, rvb_rd)
            k.wait("vector", dve)
            def _recip(h):
                with nc.allow_low_precision(reason="1/l to bf16 as before"):
                    return h.reciprocal(rvb_sb[:, :], ps_av[64:65, :])
            recip_mk = k.emit("vector", _recip, dve)

            def mk_epi(recip_mk=recip_mk, hd=hd, q0=q0):
                def epi():
                    nonlocal ps_rb_free, scores_ok, av_ok, rvb_rd
                    k.wait("tensor", dve, recip_mk)
                    k.wmark("tensor", ps_rb_free)
                    for j in range(QJ):
                        k.emit("tensor", lambda h, j=j: h.matmul(
                            ps_rb[0:64, 512 * j:512 * (j + 1)],
                            ones1_64[:, :],
                            rvb_sb[:, 512 * j:512 * (j + 1)],
                            start=True, stop=True), pe)
                    rbc_mk = pe.val
                    rvb_rd = rbc_mk
                    k.wait("vector", pe, rbc_mk)
                    k.emit("vector", lambda h: h.tensor_copy(
                        rbc_sb[:, :], ps_rb[0:64, 0:QBL]), dve)
                    ps_rb_free = (dve, dve.val)
                    k.wait("vector", dve)
                    cc_, hh_ = hd // 2, hd % 2
                    dst = ao_sb[64 * hh_:64 * (hh_ + 1),
                                cc_ * L + q0:cc_ * L + q0 + QBL]
                    ao_mk = k.emit("vector", lambda h, dst=dst:
                                   h.tensor_tensor(dst, ps_av[0:64, :],
                                                   rbc_sb[:, :], ALU.mult),
                                   dve)
                    scores_ok = (dve, ao_mk)
                    av_ok = (dve, ao_mk)
                return epi
            pending_epi = mk_epi()
            exp_tail = exp_mark[KC - 1]
            exp_tail2 = exp_mark[KC - 2] if KC >= 2 else exp_mark[KC - 1]
        head_pe_done[buf] = pe.val

    if pending_epi is not None:
        pending_epi()
        pending_epi = None

    if ao_dbg is not None:
        k.wait("sync", dve)
        k.emit("sync", lambda h: h.dma_start(
            ao_dbg[:, :].rearrange("(cc pp) t -> pp cc t", pp=128),
            ao_sb[:, :].rearrange("p (cc t) -> p cc t", cc=8)), sKQ, 16)
    esB.close()

    if "C" not in phases:
        k.barrier()
        k.play()
        return nc

    k.barrier()
    esC = ExitStack()
    Wfc = k.sb(esC, "Wfc", [128, 8 * D], BF16)
    bfc_row = k.sb(esC, "bfc_row", [1, D], BF16)
    xc_sb = k.sb(esC, "xc_sb", [128, 2 * D], F32)
    y_sb = k.sb(esC, "y_sb", [128, 2 * D], F32)
    x3_sb = k.sb(esC, "x3_sb", [128, 2 * D], F32)
    xq_tm = k.sb(esC, "xq_tm", [128, 2 * D], BF16)
    xqst = k.sb(esC, "xqst", [128, 2 * D], BF16)
    qm = k.sb(esC, "qm", [128, 2], F32)
    qsc = k.sb(esC, "qsc", [128, 2], F32)
    qrc = k.sb(esC, "qrc", [128, 2], F32)
    statC = k.sb(esC, "statC", [128, 2 * 24], F32)
    mvC = k.sb(esC, "mvC", [128, 2 * 2], F32)
    veC = k.sb(esC, "veC", [128, 2], F32)
    sdC = k.sb(esC, "sdC", [128, 2], F32)
    rstdC = k.sb(esC, "rstdC", [128, 2], F32)
    negmrC = k.sb(esC, "negmrC", [128, 2], F32)
    ps_y = k.ps(esC, "ps_y", [128, D])
    ps_t2 = [k.ps(esC, f"ps_t2{i}", [128, 512]) for i in range(2)]

    bfcf = k.sb(esC, "bfcf", [1, D], F32)
    stage_cast(Wfc, W["W_fc"][:, :].rearrange("(c p) o -> p c o", p=128),
               8, D, xc_sb, sWL)
    k.emit("sync", lambda h: h.dma_start(
        bfcf[:, :], W["b_fc"][:].rearrange("(o c) -> o c", o=1)), sWL, 16)
    k.wait("vector", sWL)
    bfc_mk = k.emit("vector", lambda h: h.tensor_copy(
        bfc_row[:, :], bfcf[:, :]), dve)

    def ln_chainC(par, src, dst, wr_free_mk, src_mk=None):
        k.wmark("vector", src_mk)
        k.wait("vector", dve)
        for c in range(4):
            k.emit("vector", lambda h, c=c, par=par, src=src: h.bn_stats(
                statC[:, par * 24 + 6 * c:par * 24 + 6 * (c + 1)],
                src[:, 512 * c:512 * (c + 1)]), dve)
        k.wait("vector", dve)
        k.emit("vector", lambda h, par=par: h.bn_aggr(
            mvC[:, 2 * par:2 * par + 2],
            statC[:, par * 24:par * 24 + 24]), dve)
        k.wait("vector", dve)
        k.emit("vector", lambda h, par=par: h.tensor_scalar(
            veC[:, par:par + 1], mvC[:, 2 * par + 1:2 * par + 2], EPS_LN,
            None, ALU.add), dve)
        k.wait("scalar", dve)
        k.emit("scalar", lambda h, par=par: h.activation(
            sdC[:, par:par + 1], veC[:, par:par + 1], AF.Sqrt), act)
        k.wait("vector", act)
        k.emit("vector", lambda h, par=par: h.reciprocal(
            rstdC[:, par:par + 1], sdC[:, par:par + 1]), dve)
        k.wait("vector", dve)
        k.emit("vector", lambda h, par=par: h.tensor_scalar(
            negmrC[:, par:par + 1], mvC[:, 2 * par:2 * par + 1],
            rstdC[:, par:par + 1], -1.0, ALU.mult, ALU.mult), dve)
        k.wait("scalar", dve)
        k.wmark("scalar", wr_free_mk)
        return k.emit("scalar", lambda h, par=par, src=src, dst=dst:
                      h.activation(dst, src, AF.Identity,
                                   bias=negmrC[:, par:par + 1],
                                   scale=rstdC[:, par:par + 1]), act)

    yev_mk = [[None] * 4, [None] * 4]
    xq_mk = [0, 0]
    yln_rd = [0, 0]
    x3_rd = [0, 0]
    y_rd = [0, 0]
    xC_free = [None, None]
    xq_dma_mk = [0, 0]
    x2_dma_mk = [0, 0]
    tgroup_done = [0, 0]

    def emit_tgroup(t):
        par = t % 2
        for g in range(4):
            pt = g % 2
            k.wait("tensor", dve, xq_mk[par])
            k.wait("tensor", act)
            for jj in range(4):
                dc = 4 * g + jj
                k.emit("tensor", lambda h, pt=pt, jj=jj, dc=dc, par=par:
                       h.matmul(ps_t2[pt][:, 128 * jj:128 * (jj + 1)],
                                xq_tm[:, par * D + 128 * dc:
                                      par * D + 128 * (dc + 1)],
                                cid_sb[:, :], start=True, stop=True), pe)
            k.wait("scalar", pe)
            k.wait("scalar", sXQ[par], xq_dma_mk[par])
            k.emit("scalar", lambda h, pt=pt, g=g, par=par: h.copy(
                xqst[:, par * D + 512 * g:par * D + 512 * (g + 1)],
                ps_t2[pt][:, :]), act)
        tgroup_done[par] = pe.val
        k.wait("sync", act)
        k.emit("sync", lambda h, t=t, par=par: h.dma_start(
            xq_hbm[t, :, :], xqst[:, par * D:(par + 1) * D]), sXQ[par], 16)
        xq_dma_mk[par] = sXQ[par].val

    k.wait("sync", dve)
    for t0_ in range(2):
        k.emit("sync", lambda h, t0_=t0_: h.dma_start(
            xc_sb[:, t0_ * D:(t0_ + 1) * D],
            x_in[128 * t0_:128 * (t0_ + 1), :]), sX[t0_], 16)
    xld_val = [sX[0].val, sX[1].val]

    for t in range(TC):
        par = t % 2
        for db in range(4):
            if t >= 1:
                k.wmark("tensor", yev_mk[1 - par][db])
            k.wait("tensor", dve, bfc_mk)
            k.emit("tensor", lambda h, db=db: h.matmul(
                ps_y[:, 512 * db:512 * (db + 1)], ones1_128[:, :],
                bfc_row[:, 512 * db:512 * (db + 1)],
                start=True, stop=False), pe)
            for cc in range(8):
                k.emit("tensor", lambda h, db=db, cc=cc, t=t: h.matmul(
                    ps_y[:, 512 * db:512 * (db + 1)],
                    ao_sb[:, cc * L + 128 * t:cc * L + 128 * (t + 1)],
                    Wfc[:, cc * D + 512 * db:cc * D + 512 * (db + 1)],
                    start=False, stop=(cc == 7)), pe)
            ymk = pe.val
            k.wait("scalar", pe, ymk)
            k.wait("scalar", act, y_rd[par])
            yev_mk[par][db] = (act, k.emit(
                "scalar", lambda h, db=db, par=par: h.copy(
                    y_sb[:, par * D + 512 * db:par * D + 512 * (db + 1)],
                    ps_y[:, 512 * db:512 * (db + 1)]), act))
        if t >= 2:
            emit_tgroup(t - 2)
        ysrc = y_sb[:, par * D:(par + 1) * D]
        yln_mk = ln_chainC(par, ysrc, x3_sb[:, par * D:(par + 1) * D],
                           (dve, x3_rd[par]), src_mk=yev_mk[par][3])
        y_rd[par] = yln_mk
        k.wait("vector", act, yln_mk)
        k.wait("vector", sX[par], xld_val[par])
        k.wait("vector", sX2, x2_dma_mk[par])
        x2_mk = k.emit("vector", lambda h, par=par: h.tensor_tensor(
            xc_sb[:, par * D:(par + 1) * D],
            x3_sb[:, par * D:(par + 1) * D],
            xc_sb[:, par * D:(par + 1) * D], ALU.add), dve)
        x3_rd[par] = x2_mk
        k.wait("sync", dve, x2_mk)
        k.emit("sync", lambda h, t=t, par=par: h.dma_start(
            x2_hbm[128 * t:128 * (t + 1), :],
            xc_sb[:, par * D:(par + 1) * D]), sX2, 16)
        x2_dma_mk[par] = sX2.val
        x3_mk = ln_chainC(par, xc_sb[:, par * D:(par + 1) * D],
                          x3_sb[:, par * D:(par + 1) * D],
                          (dve, x3_rd[par]))
        if t + 2 < TC:
            k.wait("sync", act, x3_mk)
            k.wait("sync", sX2, x2_dma_mk[par])
            k.emit("sync", lambda h, t=t, par=par: h.dma_start(
                xc_sb[:, par * D:(par + 1) * D],
                x_in[128 * (t + 2):128 * (t + 3), :]), sX[par], 16)
            xld_val[par] = sX[par].val
        k.wait("vector", act, x3_mk)
        k.emit("vector", lambda h, par=par: h.reduce_max(
            qm[:, par:par + 1], x3_sb[:, par * D:(par + 1) * D],
            axis=AX.X, apply_absolute_value=True), dve)
        k.wait("vector", dve)
        k.emit("vector", lambda h, par=par: h.tensor_scalar(
            qsc[:, par:par + 1], qm[:, par:par + 1], 1.0 / QMAX, EPS_Q,
            ALU.mult, ALU.max), dve)
        k.wait("vector", dve)
        k.emit("vector", lambda h, par=par: h.reciprocal(
            qrc[:, par:par + 1], qsc[:, par:par + 1]), dve)
        k.wait("vector", dve)
        k.emit("vector", lambda h, par=par: h.tensor_scalar(
            x3_sb[:, par * D:(par + 1) * D],
            x3_sb[:, par * D:(par + 1) * D],
            qrc[:, par:par + 1], MAGIC, ALU.mult, ALU.add), dve)
        k.wait("vector", dve)
        k.wait("vector", pe, tgroup_done[par])
        m2 = k.emit("vector", lambda h, par=par: h.tensor_scalar(
            xq_tm[:, par * D:(par + 1) * D],
            x3_sb[:, par * D:(par + 1) * D], MAGIC, qsc[:, par:par + 1],
            ALU.subtract, ALU.mult), dve)
        xq_mk[par] = m2
        x3_rd[par] = m2

    emit_tgroup(TC - 2)
    emit_tgroup(TC - 1)

    esC.close()
    esBC.close()

    if "0" not in phases:
        k.barrier()
        k.play()
        return nc

    k.barrier()
    esD0 = ExitStack()
    w1raw = k.sb(esD0, "w1raw", [128, 2 * 4 * D], F32)
    w1q = k.sb(esD0, "w1q", [128, 4 * D], BF16)
    stg = k.sb(esD0, "stg", [128, 2 * DC * 512], BF16)
    qmD = k.sb(esD0, "qmD", [128, 1], F32)
    qscD = k.sb(esD0, "qscD", [128, 1], F32)
    qrcD = k.sb(esD0, "qrcD", [128, 1], F32)
    ps_tD = [k.ps(esD0, f"ps_tD{i}", [128, 512]) for i in range(2)]

    NB1 = GU // 512
    w1_free = [None, None]
    stg_free = [0, 0]
    tD_free = [None, None]
    w1_val_arr = [0, 0]
    w1q_rd = [0, 0, 0, 0]

    def load_w1(blk):
        wb = blk % 2
        k.wmark("sync", w1_free[wb])
        k.emit("sync", lambda h, blk=blk, wb=wb: h.dma_start(
            w1raw[:, :].rearrange("p (u c d) -> p u c d", u=2, c=4)[:, wb],
            W["w_gate_up"][512 * blk:512 * (blk + 1), :].rearrange(
                "(c p) d -> p c d", p=128)), sW1[wb], 16)
        w1_val_arr[wb] = sW1[wb].val

    load_w1(0)
    for blk in range(NB1):
        wb = blk % 2
        if blk + 1 < NB1:
            load_w1(blk + 1)
        w1_val = w1_val_arr[wb]
        for oc in range(4):
            src = w1raw[:, (4 * wb + oc) * D:(4 * wb + oc + 1) * D]
            k.wait("vector", sW1[wb], w1_val)
            k.wait("vector", dve)
            k.wait("vector", pe, w1q_rd[oc])
            k.emit("vector", lambda h, src=src: h.reduce_max(
                qmD[:, :], src, axis=AX.X, apply_absolute_value=True), dve)
            k.wait("vector", dve)
            k.emit("vector", lambda h: h.tensor_scalar(
                qscD[:, :], qmD[:, :], 1.0 / QMAX, EPS_Q, ALU.mult, ALU.max), dve)
            k.wait("vector", dve)
            k.emit("vector", lambda h: h.reciprocal(qrcD[:, :], qscD[:, :]),
                   dve)
            k.wait("vector", dve)
            k.emit("vector", lambda h, src=src: h.tensor_scalar(
                src, src, qrcD[:, :], MAGIC, ALU.mult, ALU.add), dve)
            k.wait("vector", dve)
            wq_mk = k.emit("vector", lambda h, src=src, oc=oc: h.tensor_scalar(
                w1q[:, oc * D:(oc + 1) * D], src, MAGIC, qscD[:, :],
                ALU.subtract, ALU.mult), dve)
            w1_free[wb] = (dve, wq_mk)
            for g in range(4):
                pt = g % 2
                k.wait("tensor", dve, wq_mk)
                k.wmark("tensor", tD_free[pt])
                for jj in range(4):
                    dc = 4 * g + jj
                    k.emit("tensor", lambda h, pt=pt, jj=jj, dc=dc, oc=oc:
                           h.matmul(ps_tD[pt][:, 128 * jj:128 * (jj + 1)],
                                    w1q[:, oc * D + 128 * dc:
                                        oc * D + 128 * (dc + 1)],
                                    cid_sb[:, :], start=True, stop=True), pe)
                w1q_rd[oc] = pe.val
                k.wait("scalar", pe)
                k.wait("scalar", sST, stg_free[wb])
                dstT = stg[:, :].rearrange(
                    "p (u hc o) -> p u hc o", u=2, hc=DC)[
                        :, wb, 4 * g:4 * g + 4, 128 * oc:128 * (oc + 1)]
                mk = k.emit(
                    "scalar", lambda h, pt=pt, dstT=dstT: h.copy(
                        dstT, ps_tD[pt][:, :].rearrange(
                            "p (j t) -> p j t", j=4)), act)
                tD_free[pt] = (act, mk)
        k.wait("sync", act)
        k.emit("sync", lambda h, blk=blk, wb=wb: h.dma_start(
            w1qT_hbm[:, :, :].rearrange("hc p o -> p hc o")[
                :, :, 512 * blk:512 * (blk + 1)],
            stg[:, wb * DC * 512:(wb + 1) * DC * 512].rearrange(
                "p (hc o) -> p hc o", hc=DC)), sST, 16)
        stg_free[wb] = sST.val

    esD0.close()

    if "1" not in phases:
        k.barrier()
        k.play()
        return nc

    k.barrier()
    esD0b = ExitStack()
    w2raw = k.sb(esD0b, "w2raw", [128, 2 * I_FF], F32)
    w2q = k.sb(esD0b, "w2q", [128, 2 * I_FF], BF16)
    stg2 = k.sb(esD0b, "stg2", [128, IC * 256], BF16)
    qmD2 = k.sb(esD0b, "qmD2", [128, 1], F32)
    qscD2 = k.sb(esD0b, "qscD2", [128, 1], F32)
    qrcD2 = k.sb(esD0b, "qrcD2", [128, 1], F32)
    ps_tE = [k.ps(esD0b, f"ps_tE{i}", [128, 512]) for i in range(2)]

    w2_free = None
    stg2_free = 0
    tD_free = [None, None]
    w2q_rd = [0, 0]
    for rnd in range(8):
        k.wmark("sync", w2_free)
        k.emit("sync", lambda h, rnd=rnd: h.dma_start(
            w2raw[:, :].rearrange("p (c i) -> p c i", c=2),
            W["w_down"][256 * rnd:256 * (rnd + 1), :].rearrange(
                "(c p) i -> p c i", p=128)), sW2, 16)
        w2_val = sW2.val
        for dc2 in range(2):
            src = w2raw[:, dc2 * I_FF:(dc2 + 1) * I_FF]
            k.wait("vector", sW2, w2_val)
            k.wait("vector", dve)
            k.wait("vector", pe, w2q_rd[dc2])
            k.emit("vector", lambda h, src=src: h.reduce_max(
                qmD2[:, :], src, axis=AX.X, apply_absolute_value=True), dve)
            k.wait("vector", dve)
            k.emit("vector", lambda h: h.tensor_scalar(
                qscD2[:, :], qmD2[:, :], 1.0 / QMAX, EPS_Q, ALU.mult, ALU.max), dve)
            k.wait("vector", dve)
            k.emit("vector", lambda h: h.reciprocal(qrcD2[:, :], qscD2[:, :]),
                   dve)
            k.wait("vector", dve)
            k.emit("vector", lambda h, src=src: h.tensor_scalar(
                src, src, qrcD2[:, :], MAGIC, ALU.mult, ALU.add), dve)
            k.wait("vector", dve)
            wq_mk = k.emit("vector", lambda h, src=src, dc2=dc2:
                           h.tensor_scalar(
                               w2q[:, dc2 * I_FF:(dc2 + 1) * I_FF], src,
                               MAGIC, qscD2[:, :], ALU.subtract, ALU.mult),
                           dve)
            if dc2 == 1:
                w2_free = (dve, wq_mk)
            for g in range(11):
                pt = g % 2
                k.wait("tensor", dve, wq_mk)
                k.wmark("tensor", tD_free[pt])
                for jj in range(4):
                    ic = 4 * g + jj
                    k.emit("tensor", lambda h, pt=pt, jj=jj, ic=ic, dc2=dc2:
                           h.matmul(ps_tE[pt][:, 128 * jj:128 * (jj + 1)],
                                    w2q[:, dc2 * I_FF + 128 * ic:
                                        dc2 * I_FF + 128 * (ic + 1)],
                                    cid_sb[:, :], start=True, stop=True), pe)
                w2q_rd[dc2] = pe.val
                k.wait("scalar", pe)
                k.wait("scalar", sST, stg2_free)
                dstT = stg2[:, :].rearrange(
                    "p (ic d) -> p ic d", ic=IC)[
                        :, 4 * g:4 * g + 4, 128 * dc2:128 * (dc2 + 1)]
                mk = k.emit(
                    "scalar", lambda h, pt=pt, dstT=dstT: h.copy(
                        dstT, ps_tE[pt][:, :].rearrange(
                            "p (j t) -> p j t", j=4)), act)
                tD_free[pt] = (act, mk)
        k.wait("sync", act)
        k.emit("sync", lambda h, rnd=rnd: h.dma_start(
            w2qT_hbm[:, :, :].rearrange("ic p d -> p ic d")[
                :, :, 256 * rnd:256 * (rnd + 1)],
            stg2[:, :].rearrange("p (ic d) -> p ic d", ic=IC)), sST, 16)
        stg2_free = sST.val

    esD0b.close()

    if "D" not in phases:
        k.barrier()
        k.play()
        return nc

    k.barrier()
    esD1 = ExitStack()
    w1g = k.sb(esD1, "w1g", [128, 2 * DC * 512], BF16)
    w1u = k.sb(esD1, "w1u", [128, 2 * DC * 512], BF16)
    xq_sb = k.sb(esD1, "xq_sb", [128, 2 * D], BF16)
    sg_sb = k.sb(esD1, "sg_sb", [128, 2 * 512], F32)
    hst = k.sb(esD1, "hst", [128, 2 * 512], BF16)
    ps_g = [k.ps(esD1, f"ps_g{i}", [128, 512]) for i in range(2)]
    ps_u = [k.ps(esD1, f"ps_u{i}", [128, 512]) for i in range(2)]

    NJ = I_FF // 512
    wj_val = [0, 0]
    xql_val = [0, 0]
    wj_pe_done = [0, 0]
    xq_pe_done = [0, 0]

    def load_wj(j):
        buf = j % 2
        k.wait("sync", pe, wj_pe_done[buf])
        k.emit("sync", lambda h, j=j, buf=buf: h.dma_start(
            w1g[:, buf * DC * 512:(buf + 1) * DC * 512].rearrange(
                "p (hc o) -> p hc o", hc=DC),
            w1qT_hbm[:, :, :].rearrange("hc p o -> p hc o")[
                :, :, 512 * j:512 * (j + 1)]), sWJ[buf], 16)
        k.emit("sync", lambda h, j=j, buf=buf: h.dma_start(
            w1u[:, buf * DC * 512:(buf + 1) * DC * 512].rearrange(
                "p (hc o) -> p hc o", hc=DC),
            w1qT_hbm[:, :, :].rearrange("hc p o -> p hc o")[
                :, :, I_FF + 512 * j:I_FF + 512 * (j + 1)]), sWJ[buf], 16)
        wj_val[buf] = sWJ[buf].val

    def load_xq(t):
        buf = t % 2
        k.wait("sync", pe, xq_pe_done[buf])
        k.emit("sync", lambda h, t=t, buf=buf: h.dma_start(
            xq_sb[:, buf * D:(buf + 1) * D], xq_hbm[t % TC, :, :]),
            sXQL[buf], 16)
        xql_val[buf] = sXQL[buf].val

    load_wj(0)
    load_xq(0)
    load_xq(1)
    ps_g_free = [None, None]
    ps_u_free = [None, None]
    sg_free = [None, None]
    hst_free = [0, 0]
    for j in range(NJ):
        buf = j % 2
        if j + 1 < NJ:
            load_wj(j + 1)
        for t in range(TC):
            tb = t % 2
            pg = t % 2
            k.wait("tensor", sWJ[buf], wj_val[buf])
            k.wait("tensor", sXQL[tb], xql_val[tb])
            k.wmark("tensor", ps_g_free[pg])
            for dc in range(DC):
                k.emit("tensor", lambda h, pg=pg, buf=buf, tb=tb, dc=dc:
                       h.matmul(ps_g[pg][:, :],
                                xq_sb[:, tb * D + 128 * dc:
                                      tb * D + 128 * (dc + 1)],
                                w1g[:, buf * DC * 512 + dc * 512:
                                    buf * DC * 512 + (dc + 1) * 512],
                                start=(dc == 0), stop=(dc == DC - 1)), pe)
            g_mk = pe.val
            k.wmark("tensor", ps_u_free[pg])
            for dc in range(DC):
                k.emit("tensor", lambda h, pg=pg, buf=buf, tb=tb, dc=dc:
                       h.matmul(ps_u[pg][:, :],
                                xq_sb[:, tb * D + 128 * dc:
                                      tb * D + 128 * (dc + 1)],
                                w1u[:, buf * DC * 512 + dc * 512:
                                    buf * DC * 512 + (dc + 1) * 512],
                                start=(dc == 0), stop=(dc == DC - 1)), pe)
            u_mk = pe.val
            xq_pe_done[tb] = pe.val
            wj_pe_done[buf] = pe.val
            if t + 2 < TC:
                load_xq(t + 2)
            elif j + 1 < NJ:
                load_xq(t + 2)
            k.wait("scalar", pe, g_mk)
            k.wmark("scalar", sg_free[pg])
            silu_mk = k.emit(
                "scalar", lambda h, pg=pg: h.activation(
                    sg_sb[:, pg * 512:(pg + 1) * 512], ps_g[pg][:, :],
                    AF.Silu), act)
            ps_g_free[pg] = (act, silu_mk)
            k.wait("vector", act, silu_mk)
            k.wait("vector", pe, u_mk)
            k.wait("vector", sHS[pg], hst_free[pg])
            h_mk = k.emit("vector", lambda h, pg=pg: h.tensor_tensor(
                hst[:, pg * 512:(pg + 1) * 512],
                sg_sb[:, pg * 512:(pg + 1) * 512], ps_u[pg][:, :],
                ALU.mult), dve)
            ps_u_free[pg] = (dve, h_mk)
            sg_free[pg] = (dve, h_mk)
            k.wait("sync", dve, h_mk)
            k.emit("sync", lambda h, t=t, j=j, pg=pg: h.dma_start(
                h_hbm[t, :, 512 * j:512 * (j + 1)],
                hst[:, pg * 512:(pg + 1) * 512]), sHS[pg], 16)
            hst_free[pg] = sHS[pg].val

    esD1.close()

    if "P" not in phases:
        k.barrier()
        k.play()
        return nc

    k.barrier()
    esD2p = ExitStack()
    h_sb = k.sb(esD2p, "h_sb", [128, 2 * I_FF], BF16)
    hq_sb = k.sb(esD2p, "hq_sb", [128, I_FF], BF16)
    hqf = k.sb(esD2p, "hqf", [128, I_FF], F32)
    hqstg = k.sb(esD2p, "hqstg", [128, 2 * I_FF], BF16)
    w2b0 = k.sb(esD2p, "w2b0", [128, IC * 512], BF16)
    foA = k.sb(esD2p, "foA", [128, 2 * 512], F32)
    qmD3 = k.sb(esD2p, "qmD3", [128, 1], F32)
    qscD3 = k.sb(esD2p, "qscD3", [128, 1], F32)
    qrcD3 = k.sb(esD2p, "qrcD3", [128, 1], F32)
    ps_tF = [k.ps(esD2p, f"ps_tF{i}", [128, 512]) for i in range(2)]
    ps_d = [k.ps(esD2p, f"ps_d{i}", [128, 512]) for i in range(2)]

    k.emit("sync", lambda h: h.dma_start(
        w2b0[:, :].rearrange("p (ic d) -> p ic d", ic=IC),
        w2qT_hbm[:, :, :].rearrange("ic p d -> p ic d")[:, :, 0:512]),
        sW2B[0], 16)
    w2b0_val = sW2B[0].val

    h_free = [None, None]
    hl_val = [0, 0]
    hqstg_free = [(0, 0), (0, 0)]
    tD_free = [None, None]
    hq_rd = 0
    fo_free = [None, None]
    fo_dma_mk = [0, 0]

    def load_h(t):
        buf = t % 2
        k.wmark("sync", h_free[buf])
        k.emit("sync", lambda h, t=t, buf=buf: h.dma_start(
            h_sb[:, buf * I_FF:(buf + 1) * I_FF], h_hbm[t, :, :]),
            sHL[buf], 16)
        hl_val[buf] = sHL[buf].val

    load_h(0)
    load_h(1)
    for t in range(TC):
        tb = t % 2
        src = h_sb[:, tb * I_FF:(tb + 1) * I_FF]
        k.wait("vector", sHL[tb], hl_val[tb])
        k.wait("vector", dve)
        k.emit("vector", lambda h, src=src: h.reduce_max(
            qmD3[:, :], src, axis=AX.X, apply_absolute_value=True), dve)
        k.wait("vector", dve)
        k.emit("vector", lambda h: h.tensor_scalar(
            qscD3[:, :], qmD3[:, :], 1.0 / QMAX, EPS_Q, ALU.mult, ALU.max),
            dve)
        k.wait("vector", dve)
        k.emit("vector", lambda h: h.reciprocal(qrcD3[:, :], qscD3[:, :]),
               dve)
        k.wait("vector", dve)
        hf_mk = k.emit("vector", lambda h, src=src: h.tensor_scalar(
            hqf[:, :], src, qrcD3[:, :], MAGIC, ALU.mult, ALU.add), dve)
        h_free[tb] = (dve, hf_mk)
        k.wait("vector", dve)
        k.wait("vector", pe, hq_rd)
        hq_mk = k.emit("vector", lambda h: h.tensor_scalar(
            hq_sb[:, :], hqf[:, :], MAGIC, qscD3[:, :], ALU.subtract,
            ALU.mult), dve)
        if t + 2 < TC:
            load_h(t + 2)
        for g in range(11):
            pt = g % 2
            k.wait("tensor", dve, hq_mk)
            k.wmark("tensor", tD_free[pt])
            for jj in range(4):
                ic = 4 * g + jj
                k.emit("tensor", lambda h, pt=pt, jj=jj, ic=ic: h.matmul(
                    ps_tF[pt][:, 128 * jj:128 * (jj + 1)],
                    hq_sb[:, 128 * ic:128 * (ic + 1)], cid_sb[:, :],
                    start=True, stop=True), pe)
            k.wait("scalar", pe)
            k.wait("scalar", sHQS[tb], hqstg_free[tb][0])
            k.wait("scalar", pe, hqstg_free[tb][1])
            mk = k.emit(
                "scalar", lambda h, pt=pt, g=g, tb=tb: h.copy(
                    hqstg[:, tb * I_FF + 512 * g:tb * I_FF + 512 * (g + 1)],
                    ps_tF[pt][:, :]), act)
            tD_free[pt] = (act, mk)
        hq_rd = pe.val
        k.wait("sync", act)
        k.emit("sync", lambda h, t=t, tb=tb: h.dma_start(
            hqT_hbm[t, :, :], hqstg[:, tb * I_FF:(tb + 1) * I_FF]),
            sHQS[tb], 16)
        pd = t % 2
        k.wait("tensor", sW2B[0], w2b0_val)
        k.wait("tensor", act)
        k.wmark("tensor", fo_free[pd])
        for ic in range(IC):
            k.emit("tensor", lambda h, pd=pd, tb=tb, ic=ic: h.matmul(
                ps_d[pd][:, :],
                hqstg[:, tb * I_FF + 128 * ic:tb * I_FF + 128 * (ic + 1)],
                w2b0[:, ic * 512:(ic + 1) * 512],
                start=(ic == 0), stop=(ic == IC - 1)), pe)
        d_mk = pe.val
        hqstg_free[tb] = (sHQS[tb].val, d_mk)
        k.wait("scalar", pe, d_mk)
        k.wait("scalar", sFS[pd], fo_dma_mk[pd])
        mk = k.emit("scalar", lambda h, pd=pd: h.copy(
            foA[:, pd * 512:(pd + 1) * 512], ps_d[pd][:, :]), act)
        fo_free[pd] = (act, mk)
        k.wait("sync", act, mk)
        k.emit("sync", lambda h, t=t, pd=pd: h.dma_start(
            ffn_hbm[t, :, 0:512],
            foA[:, pd * 512:(pd + 1) * 512]), sFS[pd], 16)
        fo_dma_mk[pd] = sFS[pd].val

    esD2p.close()

    k.barrier()
    esD2 = ExitStack()
    w2b = k.sb(esD2, "w2b", [128, 2 * IC * 512], BF16)
    hqT_sb = k.sb(esD2, "hqT_sb", [128, 2 * I_FF], BF16)
    foB = k.sb(esD2, "foB_sb", [128, 2 * 512], F32)
    fe_sb = k.sb(esD2, "fe_sb", [128, 2 * D], F32)
    x2e_sb = k.sb(esD2, "x2e_sb", [128, 2 * D], F32)
    fln_sb = k.sb(esD2, "fln_sb", [128, 2 * D], F32)
    o_sb = k.sb(esD2, "o_sb", [128, 2 * D], F32)
    statE = k.sb(esD2, "statE", [128, 2 * 24], F32)
    mvE = k.sb(esD2, "mvE", [128, 2 * 2], F32)
    veE = k.sb(esD2, "veE", [128, 2], F32)
    sdE = k.sb(esD2, "sdE", [128, 2], F32)
    rstdE = k.sb(esD2, "rstdE", [128, 2], F32)
    negmrE = k.sb(esD2, "negmrE", [128, 2], F32)
    ps_dB = [k.ps(esD2, f"ps_dB{i}", [128, 512]) for i in range(2)]

    w2b_val = [0, 0]
    hql_val = [0, 0]
    w2b_pe_done = [0, 0]
    hq_pe_done = [0, 0]
    foB_free = [None, None]
    foB_dma_mk = [0, 0]
    fs3_mark = {}

    def load_w2b(db):
        buf = db % 2
        k.wait("sync", pe, w2b_pe_done[buf])
        k.emit("sync", lambda h, db=db, buf=buf: h.dma_start(
            w2b[:, buf * IC * 512:(buf + 1) * IC * 512].rearrange(
                "p (ic d) -> p ic d", ic=IC),
            w2qT_hbm[:, :, :].rearrange("ic p d -> p ic d")[
                :, :, 512 * db:512 * (db + 1)]), sW2B[buf], 16)
        w2b_val[buf] = sW2B[buf].val

    def load_hqT(t):
        buf = t % 2
        k.wait("sync", pe, hq_pe_done[buf])
        k.emit("sync", lambda h, t=t, buf=buf: h.dma_start(
            hqT_sb[:, buf * I_FF:(buf + 1) * I_FF], hqT_hbm[t % TC, :, :]),
            sHQL[buf], 16)
        hql_val[buf] = sHQL[buf].val

    e_free = [None, None]
    el_val = [0, 0]
    o_free = [0, 0]
    fln_rd = [0, 0]

    def load_e(t):
        buf = t % 2
        k.wmark("sync", e_free[buf])
        k.wait("sync", sEL[buf], sEL[buf].val)
        k.wait("sync", sFS[t % 2], fs3_mark.get(t, 0))
        k.emit("sync", lambda h, t=t, buf=buf: h.dma_start(
            fe_sb[:, buf * D:(buf + 1) * D], ffn_hbm[t, :, :]),
            sEL[buf], 16, batched=True)
        k.emit("sync", lambda h, t=t, buf=buf: h.dma_start(
            x2e_sb[:, buf * D:(buf + 1) * D],
            x2_hbm[128 * t:128 * (t + 1), :]), sEL[buf], 16, batched=True)
        el_val[buf] = sEL[buf].val

    def e_step(t):
        tb = t % 2
        src = fe_sb[:, tb * D:(tb + 1) * D]
        k.wait("vector", sEL[tb], el_val[tb])
        k.wait("vector", dve)
        for c in range(4):
            k.emit("vector", lambda h, c=c, src=src, tb=tb: h.bn_stats(
                statE[:, tb * 24 + 6 * c:tb * 24 + 6 * (c + 1)],
                src[:, 512 * c:512 * (c + 1)]), dve)
        k.wait("vector", dve)
        k.emit("vector", lambda h, tb=tb: h.bn_aggr(
            mvE[:, 2 * tb:2 * tb + 2], statE[:, tb * 24:tb * 24 + 24]), dve)
        k.wait("vector", dve)
        k.emit("vector", lambda h, tb=tb: h.tensor_scalar(
            veE[:, tb:tb + 1], mvE[:, 2 * tb + 1:2 * tb + 2], EPS_LN, None,
            ALU.add), dve)
        k.wait("scalar", dve)
        k.emit("scalar", lambda h, tb=tb: h.activation(
            sdE[:, tb:tb + 1], veE[:, tb:tb + 1], AF.Sqrt), act)
        k.wait("vector", act)
        k.emit("vector", lambda h, tb=tb: h.reciprocal(
            rstdE[:, tb:tb + 1], sdE[:, tb:tb + 1]), dve)
        k.wait("vector", dve)
        k.emit("vector", lambda h, tb=tb: h.tensor_scalar(
            negmrE[:, tb:tb + 1], mvE[:, 2 * tb:2 * tb + 1],
            rstdE[:, tb:tb + 1], -1.0, ALU.mult, ALU.mult), dve)
        k.wait("scalar", dve)
        k.wait("scalar", dve, fln_rd[tb])
        fln_mk = k.emit("scalar", lambda h, src=src, tb=tb: h.activation(
            fln_sb[:, tb * D:(tb + 1) * D], src, AF.Identity,
            bias=negmrE[:, tb:tb + 1], scale=rstdE[:, tb:tb + 1]), act)
        k.wait("vector", act, fln_mk)
        k.wait("vector", sOUT, o_free[tb])
        o_mk = k.emit("vector", lambda h, tb=tb: h.tensor_tensor(
            o_sb[:, tb * D:(tb + 1) * D], fln_sb[:, tb * D:(tb + 1) * D],
            x2e_sb[:, tb * D:(tb + 1) * D], ALU.add), dve)
        fln_rd[tb] = o_mk
        e_free[tb] = (dve, o_mk)
        k.wait("sync", dve, o_mk)
        k.emit("sync", lambda h, t=t, tb=tb: h.dma_start(
            out_t[128 * t:128 * (t + 1), :], o_sb[:, tb * D:(tb + 1) * D]),
            sOUT, 16)
        o_free[tb] = sOUT.val

    load_w2b(1)
    load_hqT(0)
    load_hqT(1)
    for db in range(1, 4):
        buf = db % 2
        if db + 1 < 4:
            load_w2b(db + 1)
        for t in range(TC):
            tb = t % 2
            pd = t % 2
            k.wait("tensor", sW2B[buf], w2b_val[buf])
            k.wait("tensor", sHQL[tb], hql_val[tb])
            k.wmark("tensor", foB_free[pd])
            for ic in range(IC):
                k.emit("tensor", lambda h, pd=pd, tb=tb, buf=buf, ic=ic:
                       h.matmul(ps_dB[pd][:, :],
                                hqT_sb[:, tb * I_FF + 128 * ic:
                                       tb * I_FF + 128 * (ic + 1)],
                                w2b[:, buf * IC * 512 + ic * 512:
                                    buf * IC * 512 + (ic + 1) * 512],
                                start=(ic == 0), stop=(ic == IC - 1)), pe)
            d_mk = pe.val
            hq_pe_done[tb] = d_mk
            w2b_pe_done[buf] = d_mk
            if t + 2 < TC:
                load_hqT(t + 2)
            elif db + 1 < 4:
                load_hqT(t + 2)
            k.wait("scalar", pe, d_mk)
            k.wait("scalar", sFS[pd], foB_dma_mk[pd])
            mk = k.emit("scalar", lambda h, pd=pd: h.copy(
                foB[:, pd * 512:(pd + 1) * 512], ps_dB[pd][:, :]), act)
            foB_free[pd] = (act, mk)
            k.wait("sync", act, mk)
            k.emit("sync", lambda h, t=t, db=db, pd=pd: h.dma_start(
                ffn_hbm[t, :, 512 * db:512 * (db + 1)],
                foB[:, pd * 512:(pd + 1) * 512]), sFS[pd], 16)
            foB_dma_mk[pd] = sFS[pd].val
            if db == 3:
                fs3_mark[t] = sFS[pd].val
                if t == 1:
                    load_e(0)
                    load_e(1)
                if t >= 3 and (t - 3) < TC:
                    e_step(t - 3)
                    if t - 3 + 2 < TC:
                        load_e(t - 3 + 2)
    e_step(TC - 3)
    load_e(TC - 1)
    e_step(TC - 2)
    e_step(TC - 1)

    esD2.close()

    k.wait("sync", sOUT)
    k.barrier()

    k.play()
    esG.close()
    k.es.close()
    return nc


def make_consts(L):
    d32 = np.arange(RD)
    theta = np.exp((d32 // 2 * 2).astype(np.float64) *
                   (-np.log(10000.0) / RD))
    pos = np.arange(L, dtype=np.float64)
    ang = pos[None, :] * theta[:, None]
    ra = np.sin(ang)
    rb = np.cos(ang)
    ra4 = np.tile(ra, (4, 1)).astype(ml_dtypes.bfloat16)
    rb4 = np.tile(rb, (4, 1)).astype(ml_dtypes.bfloat16)
    perm = np.zeros((128, 128), np.float32)
    for blk in range(4):
        for j in range(16):
            perm[blk * 32 + 2 * j + 1, blk * 32 + 2 * j] = -1.0
            perm[blk * 32 + 2 * j, blk * 32 + 2 * j + 1] = 1.0
    return {
        "c_id": np.eye(128, dtype=ml_dtypes.bfloat16),
        "c_perm": perm.astype(ml_dtypes.bfloat16),
        "c_ra": np.ascontiguousarray(ra4),
        "c_rb": np.ascontiguousarray(rb4),
    }


def kernel(L0_attn_h, **w):
    L = L0_attn_h.shape[1]
    nc = build_nc(L=L)
    consts = make_consts(L)
    in_maps = []
    for core in range(8):
        m = {"x_in": np.ascontiguousarray(L0_attn_h[core])}
        for nm in ("W_dkv b_dkv W_uk b_uk W_uv b_uv W_dq b_dq W_uq b_uq "
                   "W_qr b_qr W_kr b_kr W_fc b_fc").split():
            m[nm] = w["L0_attn_" + nm]
        m["w_gate_up"] = w["L0_ffn_w_gate_up"]
        m["w_down"] = w["L0_ffn_w_down"]
        m.update(consts)
        in_maps.append(m)
    res = run_bass_kernel_spmd(nc, in_maps, core_ids=list(range(8)))
    return np.stack([np.asarray(r["out"]) for r in res.results], axis=0)


# version 36
# speedup vs baseline: 1.0667x; 1.0098x over previous
import numpy as np
import ml_dtypes
import concourse.bass as bass
import concourse.mybir as mybir
from concourse.bass_utils import run_bass_kernel_spmd
from contextlib import ExitStack

F32 = mybir.dt.float32
BF16 = mybir.dt.bfloat16
AF = mybir.ActivationFunctionType
ALU = mybir.AluOpType
AX = mybir.AxisListType

D = 2048
DC = D // 128
CKV = 512
UP = 1024
NH = 16
VD = 64
RD = 32
HD = 96
I_FF = 5632
IC = I_FF // 128
GU = 2 * I_FF
SCALE = 1.0 / (NH ** 0.5 + RD ** 0.5)
EPS_LN = 1e-5
QMAX = 127.0
EPS_Q = 1e-8
MAGIC = 12582912.0

ENGS = ("sync", "tensor", "vector", "scalar", "gpsimd")


class Sem:
    def __init__(self, k, name):
        self.h = k.es.enter_context(k.nc.semaphore(name))
        self.val = 0
        self.name = name


class KB:

    def __init__(self, nc):
        self.nc = nc
        self.es = ExitStack()
        self.ops = {e: [] for e in ENGS}
        self.waited = {}
        self.sems = {}

    def sem(self, name):
        s = Sem(self, name)
        self.sems[name] = s
        return s

    def sb(self, es, name, shape, dtype):
        return es.enter_context(self.nc.sbuf_tensor(name, shape, dtype))

    def ps(self, es, name, shape):
        return es.enter_context(self.nc.psum_tensor(name, shape, F32))

    def wait(self, eng, sem, val=None):
        v = sem.val if val is None else val
        if v <= 0:
            return
        key = (eng, sem.name)
        if self.waited.get(key, -1) >= v:
            return
        self.waited[key] = v

        def fn(h, sem=sem, v=v):
            h.wait_ge(sem.h, v)
        self.ops[eng].append(fn)

    def wmark(self, eng, mark):
        if mark:
            self.wait(eng, mark[0], mark[1])

    def emit(self, eng, build, inc=None, incn=1, batched=False):
        if inc is not None:
            if incn == 16 and not batched:
                self.wait(eng, inc, inc.val)
            inc.val += incn

        def fn(h, build=build, inc=inc, incn=incn):
            ins = build(h)
            if inc is not None:
                ins.then_inc(inc.h, incn)
        self.ops[eng].append(fn)
        return inc.val if inc is not None else None

    def barrier(self):
        for e in ENGS:
            for s in self.sems.values():
                self.wait(e, s)

    def play(self):
        with self.nc.Block() as blk:
            for e in ENGS:
                sub = self.ops[e]

                def run(h, sub=sub):
                    for f in sub:
                        f(h)
                getattr(blk, e)(run)
        self.ops = {e: [] for e in ENGS}


def build_nc(L=4096, dbg=(), phases="ABCD01P2E"):
    import os
    phases = os.environ.get("KPHASES", phases)
    nc = bass.Bass("TRN2", target_bir_lowering=False)
    TB = L // 512
    TC = L // 128
    QBL = min(L, 1024)
    QB = L // QBL
    QJ = QBL // 512
    KC = L // 128

    dt = nc.dram_tensor
    x_in = dt("x_in", [L, D], F32, kind="ExternalInput")
    W = {}
    for nm, shp in [("W_dkv", [D, CKV]), ("b_dkv", [CKV]),
                    ("W_uk", [CKV, UP]), ("b_uk", [UP]),
                    ("W_uv", [CKV, UP]), ("b_uv", [UP]),
                    ("W_dq", [D, CKV]), ("b_dq", [CKV]),
                    ("W_uq", [CKV, UP]), ("b_uq", [UP]),
                    ("W_qr", [CKV, NH * RD]), ("b_qr", [NH * RD]),
                    ("W_kr", [D, RD]), ("b_kr", [RD]),
                    ("W_fc", [UP, D]), ("b_fc", [D]),
                    ("w_gate_up", [GU, D]), ("w_down", [D, I_FF])]:
        W[nm] = dt(nm, shp, F32, kind="ExternalInput")
    c_id = dt("c_id", [128, 128], BF16, kind="ExternalInput")
    c_perm = dt("c_perm", [128, 128], BF16, kind="ExternalInput")
    c_ra = dt("c_ra", [128, L], BF16, kind="ExternalInput")
    c_rb = dt("c_rb", [128, L], BF16, kind="ExternalInput")
    out_t = dt("out", [L, D], F32, kind="ExternalOutput")

    def scratch(name, shape, dtyp):
        kind = "ExternalOutput" if name in dbg else "Internal"
        return dt(name, shape, dtyp, kind=kind)

    k_hbm = scratch("k_hbm", [NH, HD, L], BF16)
    q_hbm = scratch("q_hbm", [NH, HD, L], BF16)
    v_hbm = scratch("v_hbm", [KC, 128, NH * 65], BF16)
    x2_hbm = scratch("x2_hbm", [L, D], F32)
    xq_hbm = scratch("xq_hbm", [TC, 128, D], BF16)
    w1qT_hbm = scratch("w1qT_hbm", [DC, 128, GU], BF16)
    w2qT_hbm = scratch("w2qT_hbm", [IC, 128, D], BF16)
    h_hbm = scratch("h_hbm", [TC, 128, I_FF], BF16)
    hqT_hbm = scratch("hqT_hbm", [TC, 128, I_FF], BF16)
    ffn_hbm = scratch("ffn_hbm", [TC, 128, D], F32)
    ao_dbg = scratch("ao_dbg", [UP, L], BF16) if "ao_dbg" in dbg else None

    k = KB(nc)
    pe = k.sem("pe")
    dve = k.sem("dve")
    act = k.sem("act")
    pool = k.sem("pool")
    sWL = k.sem("sWL")
    sBL = k.sem("sBL")
    sRT = [k.sem(f"sRT{i}") for i in range(2)]
    sX = [k.sem(f"sX{i}") for i in range(2)]
    sKQ = k.sem("sKQ")
    sVL = k.sem("sVL")
    sKH = [k.sem(f"sKH{i}") for i in range(2)]
    sQH = [k.sem(f"sQH{i}") for i in range(2)]
    sX2 = k.sem("sX2")
    sXQ = [k.sem(f"sXQ{i}") for i in range(2)]
    sW1 = [k.sem(f"sW1{i}") for i in range(2)]
    sST = k.sem("sST")
    sW2 = k.sem("sW2")
    sWJ = [k.sem(f"sWJ{i}") for i in range(2)]
    sXQL = [k.sem(f"sXQL{i}") for i in range(2)]
    sHS = [k.sem(f"sHS{i}") for i in range(2)]
    sHL = [k.sem(f"sHL{i}") for i in range(2)]
    sHQS = [k.sem(f"sHQS{i}") for i in range(2)]
    sW2B = [k.sem(f"sW2B{i}") for i in range(2)]
    sHQL = [k.sem(f"sHQL{i}") for i in range(2)]
    sFS = [k.sem(f"sFS{i}") for i in range(2)]
    sEL = [k.sem(f"sEL{i}") for i in range(2)]
    sOUT = k.sem("sOUT")

    esG = ExitStack()
    cid_sb = k.sb(esG, "cid", [128, 128], BF16)
    ones1_128 = k.sb(esG, "ones1_128", [1, 128], BF16)
    ones1_64 = k.sb(esG, "ones1_64", [1, 64], BF16)
    ones96_16 = k.sb(esG, "ones96_16", [96, 16], BF16)
    negc_bc = k.sb(esG, "negc_bc", [128, 16], F32)

    k.emit("sync", lambda h: h.dma_start(cid_sb[:, :], c_id[:, :]), sBL, 16)
    for m in (ones1_128, ones1_64, ones96_16):
        k.emit("gpsimd", lambda h, m=m: h.memset(m[:, :], 1.0), pool)

    esA = ExitStack()
    Wdkv = k.sb(esA, "Wdkv", [128, DC * 512], BF16)
    Wdq = k.sb(esA, "Wdq", [128, DC * 512], BF16)
    Wuk = k.sb(esA, "Wuk", [128, 4 * UP], BF16)
    Wuv = k.sb(esA, "Wuv", [128, 4 * UP], BF16)
    Wuq = k.sb(esA, "Wuq", [128, 4 * UP], BF16)
    Wqr = k.sb(esA, "Wqr", [128, 4 * 512], BF16)
    Wkr = k.sb(esA, "Wkr", [128, DC * RD], BF16)
    bdkv = k.sb(esA, "bdkv", [128, 4], F32)
    bdq = k.sb(esA, "bdq", [128, 4], F32)
    buk = k.sb(esA, "buk", [128, 8], F32)
    buq = k.sb(esA, "buq", [128, 8], F32)
    bqr = k.sb(esA, "bqr", [128, 4], F32)
    bkr = k.sb(esA, "bkr", [32, 1], F32)
    buv_row = k.sb(esA, "buv_row", [1, UP], BF16)
    buvf = k.sb(esA, "buvf", [1, UP], F32)
    buv_bc = k.sb(esA, "buv_bc", [128, UP], F32)
    ra_b = k.sb(esA, "ra_b", [128, 2 * 512], BF16)
    rb_b = k.sb(esA, "rb_b", [128, 2 * 512], BF16)
    perm_sb = k.sb(esA, "perm_sb", [128, 128], BF16)
    x_sb = k.sb(esA, "x_sb", [128, 2 * D], F32)
    xln_tm = k.sb(esA, "xln_tm", [128, 4 * D], BF16)
    x_lnT = k.sb(esA, "x_lnT", [128, 2 * DC * 512], BF16)
    ckv_sb = k.sb(esA, "ckv_sb", [128, 4 * 512], BF16)
    cq_sb = k.sb(esA, "cq_sb", [128, 4 * 512], BF16)
    kst = k.sb(esA, "kst", [128, NH * 512], BF16)
    qst = k.sb(esA, "qst", [128, NH * 512], BF16)
    vstg = k.sb(esA, "vstg", [128, 4 * NH * 65], BF16)
    qrp_sb = k.sb(esA, "qrp_sb", [128, 512], BF16)
    qre_sb = k.sb(esA, "qre_sb", [128, 512], BF16)
    krp_sb = k.sb(esA, "krp_sb", [32, 512], BF16)
    kre_sb = k.sb(esA, "kre_sb", [32, 512], BF16)
    q2m = k.sb(esA, "q2m", [128, 512], F32)
    qam = k.sb(esA, "qam", [128, 512], F32)
    stat = k.sb(esA, "stat", [128, 24], F32)
    mv = k.sb(esA, "mv", [128, 2], F32)
    ve = k.sb(esA, "ve", [128, 1], F32)
    sd = k.sb(esA, "sd", [128, 1], F32)
    rstd = k.sb(esA, "rstd", [128, 1], F32)
    negmr = k.sb(esA, "negmr", [128, 1], F32)

    ps_tr = [k.ps(esA, f"ps_tr{i}", [128, 512]) for i in range(2)]
    ps_c = [k.ps(esA, f"ps_c{i}", [128, 512]) for i in range(2)]
    ps_o = [k.ps(esA, f"ps_o{i}", [128, 512]) for i in range(2)]
    ps_small = k.ps(esA, "ps_small", [32, 512])

    def stage_cast(dst_sb, src3d, nchunks, ow, stg, ld_sem):
        per = max(1, 4096 // ow)
        c0 = 0
        while c0 < nchunks:
            c1 = min(nchunks, c0 + per)
            n = (c1 - c0) * ow
            k.wait("sync", dve)
            k.emit("sync", lambda h, src3d=src3d, c0=c0, c1=c1, n=n, stg=stg:
                   h.dma_start(stg[:, 0:n].rearrange(
                       "p (c o) -> p c o", c=c1 - c0), src3d[:, c0:c1, :]),
                   ld_sem, 16)
            k.wait("vector", ld_sem)
            k.emit("vector", lambda h, dst_sb=dst_sb, c0=c0, n=n, ow=ow,
                   stg=stg: h.tensor_copy(
                       dst_sb[:, c0 * ow:c0 * ow + n], stg[:, 0:n]), dve)
            c0 = c1

    stage_cast(Wdkv, W["W_dkv"][:, :].rearrange("(c p) o -> p c o", p=128),
               DC, 512, x_sb, sWL)
    stage_cast(Wdq, W["W_dq"][:, :].rearrange("(c p) o -> p c o", p=128),
               DC, 512, x_sb, sWL)
    stage_cast(Wuk, W["W_uk"][:, :].rearrange("(c p) o -> p c o", p=128),
               4, UP, x_sb, sWL)
    stage_cast(Wuv, W["W_uv"][:, :].rearrange("(c p) o -> p c o", p=128),
               4, UP, x_sb, sWL)
    stage_cast(Wuq, W["W_uq"][:, :].rearrange("(c p) o -> p c o", p=128),
               4, UP, x_sb, sWL)
    stage_cast(Wqr, W["W_qr"][:, :].rearrange("(c p) o -> p c o", p=128),
               4, 512, x_sb, sWL)
    stage_cast(Wkr, W["W_kr"][:, :].rearrange("(c p) o -> p c o", p=128),
               DC, RD, x_sb, sWL)
    k.emit("sync", lambda h: h.dma_start(
        buvf[:, :], W["b_uv"][:].rearrange("(o c) -> o c", o=1)), sWL, 16)
    k.wait("vector", sWL)
    k.emit("vector", lambda h: h.tensor_copy(buv_row[:, :], buvf[:, :]), dve)

    def f32_load(dst, src_ap):
        k.emit("sync", lambda h, dst=dst, src_ap=src_ap:
               h.dma_start(dst, src_ap, allow_slow_non_contiguous=True),
               sBL, 16)

    f32_load(bdkv[:, :], W["b_dkv"][:].rearrange("(c p) -> p c", p=128))
    f32_load(bdq[:, :], W["b_dq"][:].rearrange("(c p) -> p c", p=128))
    f32_load(buk[:, :], W["b_uk"][:].rearrange("(c p) -> p c", p=128))
    f32_load(buq[:, :], W["b_uq"][:].rearrange("(c p) -> p c", p=128))
    f32_load(bqr[:, :], W["b_qr"][:].rearrange("(c p) -> p c", p=128))
    f32_load(bkr[:, :], W["b_kr"][:].rearrange("(p c) -> p c", c=1))
    f32_load(perm_sb[:, :], c_perm[:, :])

    k.wait("tensor", sWL)
    k.wait("tensor", dve)
    k.wait("tensor", sBL)
    k.wait("tensor", pool)
    k.wait("scalar", sBL)
    k.wait("vector", sBL)
    for j in range(2):
        k.emit("tensor", lambda h, j=j: h.matmul(
            ps_c[j][:, :], ones1_128[:, :], buv_row[:, 512 * j:512 * (j + 1)],
            start=True, stop=True), pe)
    k.wait("scalar", pe)
    buv_mk = 0
    for j in range(2):
        buv_mk = k.emit("scalar", lambda h, j=j: h.copy(
            buv_bc[:, 512 * j:512 * (j + 1)], ps_c[j][:, :]), act)

    x_free = [(dve, dve.val), (dve, dve.val)]
    rt_free = [None, None]
    ps_tr_free = [None, None]
    ps_c_free = [(act, buv_mk), (act, buv_mk)]
    ps_o_free = [None, None]
    kqv_store_mark = 0

    def ln_stats(src, n512):
        k.wait("vector", dve)
        for c in range(n512):
            k.emit("vector", lambda h, c=c, src=src: h.bn_stats(
                stat[:, 6 * c:6 * c + 6], src[:, 512 * c:512 * (c + 1)]),
                dve)
        k.wait("vector", dve)
        k.emit("vector", lambda h, n512=n512: h.bn_aggr(
            mv[:, :], stat[:, 0:6 * n512]), dve)
        k.wait("vector", dve)
        k.emit("vector", lambda h: h.tensor_scalar(
            ve[:, :], mv[:, 1:2], EPS_LN, None, ALU.add), dve)
        k.wait("scalar", dve)
        k.emit("scalar", lambda h: h.activation(sd[:, :], ve[:, :], AF.Sqrt),
               act)
        k.wait("vector", act)
        k.emit("vector", lambda h: h.reciprocal(rstd[:, :], sd[:, :]), dve)
        k.wait("vector", dve)
        k.emit("vector", lambda h: h.tensor_scalar(
            negmr[:, :], mv[:, 0:1], rstd[:, :], -1.0, ALU.mult, ALU.mult),
            dve)

    xlnT_mk = [0, 0]
    xlnT_rd = [0, 0]
    rt_val = [0, 0]

    nrm_mks = [0, 0, 0, 0]

    def ln_vec(b):
        bb = b % 2
        k.wmark("sync", rt_free[bb])
        k.emit("sync", lambda h, b=b, bb=bb: h.dma_start(
            ra_b[:, 512 * bb:512 * (bb + 1)],
            c_ra[:, 512 * b:512 * (b + 1)]), sRT[bb], 16)
        k.emit("sync", lambda h, b=b, bb=bb: h.dma_start(
            rb_b[:, 512 * bb:512 * (bb + 1)],
            c_rb[:, 512 * b:512 * (b + 1)]), sRT[bb], 16)
        rt_val[bb] = sRT[bb].val
        for cb in range(4):
            tc = 4 * b + cb
            xb = tc % 2
            k.wmark("sync", x_free[xb])
            k.emit("sync", lambda h, tc=tc, xb=xb: h.dma_start(
                x_sb[:, xb * D:(xb + 1) * D],
                x_in[128 * tc:128 * (tc + 1), :]), sX[xb], 16)
            xld_val = sX[xb].val
            xc = x_sb[:, xb * D:(xb + 1) * D]
            k.wait("vector", sX[xb], xld_val)
            ln_stats(xc, 4)
            dst = xln_tm[:, cb * D:(cb + 1) * D]
            k.wait("scalar", dve)
            k.wait("scalar", pe)
            k.wait("scalar", sX[xb], xld_val)
            nrm_mks[cb] = k.emit(
                "scalar", lambda h, dst=dst, xc=xc: h.activation(
                    dst, xc, AF.Identity, bias=negmr[:, :],
                    scale=rstd[:, :]), act)
            x_free[xb] = (act, nrm_mks[cb])

    def ln_pe(b):
        bb = b % 2
        for cb in range(4):
            dst = xln_tm[:, cb * D:(cb + 1) * D]
            for g in range(4):
                pt = g % 2
                k.wait("tensor", act, nrm_mks[cb])
                k.wmark("tensor", ps_tr_free[pt])
                for jj in range(4):
                    dc = 4 * g + jj
                    k.emit("tensor", lambda h, pt=pt, jj=jj, dc=dc, dst=dst:
                           h.matmul(ps_tr[pt][:, 128 * jj:128 * (jj + 1)],
                                    dst[:, 128 * dc:128 * (dc + 1)],
                                    cid_sb[:, :], start=True, stop=True), pe)
                k.wait("scalar", pe)
                k.wait("scalar", pe, xlnT_rd[bb])
                dstT = x_lnT[:, :].rearrange(
                    "p (u dc t) -> p u dc t", u=2, dc=DC)[
                        :, bb, 4 * g:4 * g + 4, 128 * cb:128 * (cb + 1)]
                mk = k.emit(
                    "scalar", lambda h, pt=pt, dstT=dstT: h.copy(
                        dstT, ps_tr[pt][:, :].rearrange(
                            "p (j t) -> p j t", j=4)), act)
                ps_tr_free[pt] = (act, mk)
        xlnT_mk[bb] = ps_tr_free[1][1]

    def rest_part(b):
        nonlocal kqv_store_mark
        bb = b % 2
        base = bb * DC * 512
        t0 = 512 * b
        xT = lambda dc: x_lnT[:, base + dc * 512:base + (dc + 1) * 512]
        k.wait("tensor", act, xlnT_mk[bb])
        c_mk = 0
        for Wsb, bsb, cdst in ((Wdkv, bdkv, ckv_sb), (Wdq, bdq, cq_sb)):
            for cs in range(4):
                pc = cs % 2
                k.wmark("tensor", ps_c_free[pc])
                for dc in range(DC):
                    k.emit("tensor", lambda h, pc=pc, Wsb=Wsb, dc=dc, cs=cs:
                           h.matmul(ps_c[pc][:, :],
                                    Wsb[:, dc * 512 + 128 * cs:
                                        dc * 512 + 128 * (cs + 1)],
                                    xT(dc),
                                    start=(dc == 0), stop=(dc == DC - 1)), pe)
                k.wait("scalar", pe)
                mk = k.emit(
                    "scalar", lambda h, pc=pc, cdst=cdst, bsb=bsb, cs=cs:
                    h.activation(cdst[:, 512 * cs:512 * (cs + 1)],
                                 ps_c[pc][:, :], AF.Identity,
                                 bias=bsb[:, cs:cs + 1]), act)
                ps_c_free[pc] = (act, mk)
                c_mk = mk

        k.wait("tensor", act, c_mk)
        k.wait("scalar", sKQ, kqv_store_mark)
        for Wsb, bsb, sdst, csrc in ((Wuk, buk, kst, ckv_sb),
                                     (Wuq, buq, qst, cq_sb)):
            for oc in range(8):
                po = oc % 2
                k.wmark("tensor", ps_o_free[po])
                for cc in range(4):
                    k.emit("tensor", lambda h, po=po, Wsb=Wsb, oc=oc, cc=cc,
                           csrc=csrc: h.matmul(
                               ps_o[po][:, :],
                               Wsb[:, cc * UP + 128 * oc:
                                   cc * UP + 128 * (oc + 1)],
                               csrc[:, cc * 512:(cc + 1) * 512],
                               start=(cc == 0), stop=(cc == 3)), pe)
                k.wait("vector", pe)
                k.wait("vector", sKQ, kqv_store_mark)
                mk = 0
                for hh in range(2):
                    hd = 2 * oc + hh
                    mk = k.emit(
                        "vector", lambda h, po=po, hh=hh, hd=hd, sdst=sdst,
                        bsb=bsb, oc=oc: h.tensor_scalar(
                            sdst[0:64, 512 * hd:512 * hd + 512],
                            ps_o[po][64 * hh:64 * (hh + 1), :],
                            bsb[64 * hh:64 * (hh + 1), oc:oc + 1], None,
                            ALU.add), dve)
                ps_o_free[po] = (dve, mk)

        k.wait("vector", sKQ, kqv_store_mark)
        k.wait("vector", act, buv_mk)
        k.wait("gpsimd", sKQ, kqv_store_mark)
        vre = vstg[:, :].rearrange("p (c hh d) -> p c hh d", c=4, hh=NH)
        k.emit("gpsimd", lambda h, vre=vre: h.memset(
            vre[:, :, :, 64:65], 1.0), pool)
        for ts in range(4):
            for half in range(2):
                po = (ts * 2 + half) % 2
                k.wmark("tensor", ps_o_free[po])
                for cc in range(4):
                    k.emit("tensor", lambda h, po=po, ts=ts, half=half, cc=cc:
                           h.matmul(ps_o[po][:, :],
                                    ckv_sb[:, cc * 512 + 128 * ts:
                                           cc * 512 + 128 * (ts + 1)],
                                    Wuv[:, cc * UP + 512 * half:
                                        cc * UP + 512 * (half + 1)],
                                    start=(cc == 0), stop=(cc == 3)), pe)
                k.wait("vector", pe)
                vdst = vre[:, ts, 8 * half:8 * half + 8, 0:64]
                mk = k.emit("vector", lambda h, po=po, vdst=vdst, half=half:
                            h.tensor_tensor(
                                vdst, ps_o[po][:, :].rearrange(
                                    "p (hh d) -> p hh d", hh=8),
                                buv_bc[:, 512 * half:512 * (half + 1)]
                                .rearrange("p (hh d) -> p hh d", hh=8),
                                ALU.add), dve)
                ps_o_free[po] = (dve, mk)

        for cc in range(4):
            po = cc % 2
            k.wmark("tensor", ps_o_free[po])
            for c4 in range(4):
                k.emit("tensor", lambda h, po=po, cc=cc, c4=c4: h.matmul(
                    ps_o[po][:, :],
                    Wqr[:, c4 * 512 + 128 * cc:c4 * 512 + 128 * (cc + 1)],
                    cq_sb[:, c4 * 512:(c4 + 1) * 512],
                    start=(c4 == 0), stop=(c4 == 3)), pe)
            k.wait("scalar", pe)
            k.wait("scalar", dve)
            qrp_mk = k.emit("scalar", lambda h, po=po, cc=cc: h.activation(
                qrp_sb[:, :], ps_o[po][:, :], AF.Identity,
                bias=bqr[:, cc:cc + 1]), act)
            ps_o_free[po] = (act, qrp_mk)
            po2 = (cc + 1) % 2
            k.wait("tensor", act, qrp_mk)
            k.wmark("tensor", ps_o_free[po2])
            rot_mk = k.emit("tensor", lambda h, po2=po2: h.matmul(
                ps_o[po2][:, :], perm_sb[:, :], qrp_sb[:, :],
                start=True, stop=True), pe)
            k.wait("vector", pe, rot_mk)
            k.wait("vector", sRT[bb], rt_val[bb])
            k.wait("vector", dve)
            k.emit("vector", lambda h, po2=po2, bb=bb: h.tensor_tensor(
                q2m[:, :], ps_o[po2][:, :],
                rb_b[:, 512 * bb:512 * (bb + 1)], ALU.mult), dve)
            q2_mk = k.emit("vector", lambda h, bb=bb: h.tensor_tensor(
                qam[:, :], qrp_sb[:, :],
                ra_b[:, 512 * bb:512 * (bb + 1)], ALU.mult), dve)
            ps_o_free[po2] = (dve, q2_mk)
            k.wait("vector", dve)
            k.wait("vector", pe)
            qre_mk = k.emit("vector", lambda h: h.tensor_tensor(
                qre_sb[:, :], q2m[:, :], qam[:, :], ALU.add), dve)
            for j in range(4):
                hd = 4 * cc + j
                k.wait("tensor", dve, qre_mk)
                k.wait("tensor", act)
                k.emit("tensor", lambda h, j=j: h.matmul(
                    ps_small[:, :], cid_sb[:, 32 * j:32 * (j + 1)],
                    qre_sb[:, :], start=True, stop=True), pe)
                k.wait("scalar", pe)
                k.emit("scalar", lambda h, hd=hd: h.copy(
                    qst[64:96, 512 * hd:512 * hd + 512], ps_small[:, :]), act)

        k.wait("tensor", act)
        for dc in range(DC):
            k.emit("tensor", lambda h, dc=dc: h.matmul(
                ps_small[:, :], Wkr[:, dc * RD:(dc + 1) * RD],
                xT(dc),
                start=(dc == 0), stop=(dc == DC - 1)), pe)
        k.wait("scalar", pe)
        k.wait("scalar", pool)
        k.wait("scalar", dve)
        krp_mk = k.emit("scalar", lambda h: h.activation(
            krp_sb[:, :], ps_small[:, :], AF.Identity, bias=bkr[:, :]), act)
        k.wait("tensor", act, krp_mk)
        rot_mk = k.emit("tensor", lambda h: h.matmul(
            ps_small[:, :], perm_sb[0:32, 0:32], krp_sb[:, :],
            start=True, stop=True), pe)
        xlnT_rd[bb] = pe.val
        k.wait("vector", pe, rot_mk)
        k.wait("vector", dve)
        k.emit("vector", lambda h, bb=bb: h.tensor_tensor(
            q2m[0:32, :], ps_small[:, :],
            rb_b[0:32, 512 * bb:512 * (bb + 1)], ALU.mult), dve)
        k.emit("vector", lambda h, bb=bb: h.tensor_tensor(
            qam[0:32, :], krp_sb[:, :],
            ra_b[0:32, 512 * bb:512 * (bb + 1)], ALU.mult), dve)
        rt_free[bb] = (dve, dve.val)
        k.wait("vector", dve)
        k.wait("vector", pool)
        kre_mk = k.emit("vector", lambda h: h.tensor_tensor(
            kre_sb[:, :], q2m[0:32, :], qam[0:32, :], ALU.add), dve)
        k.wait("vector", dve, kre_mk)
        k.wait("vector", act)
        k.wait("vector", sKQ, kqv_store_mark)
        for hd in range(NH):
            k.emit("vector", lambda h, hd=hd: h.tensor_copy(
                kst[64:96, 512 * hd:512 * hd + 512], kre_sb[:, :]), dve)

        k.wait("sync", act)
        k.wait("sync", dve)
        for st_, hbm in ((kst, k_hbm), (qst, q_hbm)):
            k.emit("sync", lambda h, st_=st_, hbm=hbm, t0=t0: h.dma_start(
                hbm[:, :, :].rearrange("hh p t -> p hh t")[:, :, t0:t0 + 512],
                st_[0:96, :].rearrange("p (hh t) -> p hh t", hh=NH)),
                sKQ, 16)
        k.emit("sync", lambda h, b=b: h.dma_start(
            v_hbm[:, :, :].rearrange("c p d -> p c d")[:, 4 * b:4 * b + 4, :],
            vstg[:, :].rearrange("p (c d) -> p c d", c=4)), sKQ, 16)
        kqv_store_mark = sKQ.val

    ln_vec(0)
    ln_pe(0)
    for b in range(TB):
        if b + 1 < TB:
            ln_vec(b + 1)
        rest_part(b)
        if b + 1 < TB:
            ln_pe(b + 1)

    esA.close()


    if "B" not in phases:
        k.barrier()
        k.play()
        return nc

    k.barrier()
    esBC = ExitStack()
    ao_sb = k.sb(esBC, "ao_sb", [128, 8 * L], BF16)
    esB = ExitStack()
    v_sb = k.sb(esB, "v_sb", [128, KC * NH * 65], BF16)
    kh_sb = k.sb(esB, "kh_sb", [96, 2 * L], BF16)
    qh_sb = k.sb(esB, "qh_sb", [96, 2 * L], BF16)
    probs = k.sb(esB, "probs", [128, 6 * QBL], BF16)
    sq_sb = k.sb(esB, "sq_sb", [96, 2 * L], BF16)
    qn_h = k.sb(esB, "qn_h", [16, 2], F32)
    tmp16 = k.sb(esB, "tmp16", [16, 1], F32)
    negc1 = k.sb(esB, "negc1", [16, 1], BF16)
    rvb_sb = k.sb(esB, "rvb_sb", [1, QBL], BF16)
    rbc_sb = k.sb(esB, "rbc_sb", [64, QBL], BF16)
    ps_s = [k.ps(esB, f"ps_s{i}", [128, QBL]) for i in range(2)]
    ps_av = k.ps(esB, "ps_av", [65, QBL])
    ps_rb = k.ps(esB, "ps_rb", [128, max(QBL, 1024)])

    negc_mark = {}
    ps_rb_free = None
    sq_rd = 0
    sq_mk = [0, 0]
    sq_pool_mk = 0

    kh_val = [0, 0]
    qh_val = [0, 0]
    head_pe_done = [0, 0]

    def load_head(hd):
        buf = hd % 2
        k.wait("sync", pe, head_pe_done[buf])
        k.wait("sync", dve, sq_pool_mk)
        k.emit("sync", lambda h, hd=hd, buf=buf: h.dma_start(
            kh_sb[:, buf * L:(buf + 1) * L], k_hbm[hd, :, :]), sKH[buf], 16)
        kh_val[buf] = sKH[buf].val
        k.emit("sync", lambda h, hd=hd, buf=buf: h.dma_start(
            qh_sb[:, buf * L:(buf + 1) * L], q_hbm[hd, :, :]), sQH[buf], 16)
        qh_val[buf] = sQH[buf].val

    def head_sq(hd):
        nonlocal sq_pool_mk
        buf = hd % 2
        for which, src_sb in ((0, qh_sb), (1, kh_sb)):
            k.wait("vector", (sQH if which == 0 else sKH)[buf],
                   (qh_val if which == 0 else kh_val)[buf])
            k.wait("vector", pe, sq_rd)
            k.wait("vector", dve)
            sq_mk[which] = k.emit(
                "vector", lambda h, src_sb=src_sb, buf=buf, which=which:
                h.tensor_tensor(sq_sb[:, which * L:(which + 1) * L],
                                src_sb[:, buf * L:(buf + 1) * L],
                                src_sb[:, buf * L:(buf + 1) * L],
                                ALU.mult), dve)
        sq_pool_mk = dve.val

    norm_red = {}

    def head_norm1_part(hd, part):
        nonlocal ps_rb_free, sq_rd
        NJ = L // 512
        if part == 0:
            k.wait("tensor", dve, max(sq_mk))
            k.wmark("tensor", ps_rb_free)
            norm_red.clear()
        for idx in (2 * part, 2 * part + 1):
            which, j = idx // NJ, idx % NJ
            if idx >= 2:
                k.wait("tensor", dve, norm_red[idx - 2])
            nm = k.emit("tensor", lambda h, idx=idx, which=which, j=j:
                        h.matmul(
                            ps_rb[0:16, 512 * (idx % 2):512 * (idx % 2) + 512],
                            ones96_16[:, :],
                            sq_sb[:, which * L + 512 * j:
                                  which * L + 512 * (j + 1)],
                            start=True, stop=True), pe)
            k.wait("vector", pe, nm)
            k.wait("vector", dve)
            norm_red[idx] = k.emit("vector", lambda h, idx=idx: h.reduce_max(
                tmp16[:, :],
                ps_rb[0:16, 512 * (idx % 2):512 * (idx % 2) + 512],
                axis=AX.X), dve)
            k.wait("vector", dve)
            if j == 0:
                k.emit("vector", lambda h, which=which: h.tensor_copy(
                    qn_h[:, which:which + 1], tmp16[:, :]), dve)
            else:
                k.emit("vector", lambda h, which=which: h.tensor_tensor(
                    qn_h[:, which:which + 1], qn_h[:, which:which + 1],
                    tmp16[:, :], ALU.max), dve)
        if 2 * part + 1 == 2 * NJ - 1:
            sq_rd = pe.val
            k.wait("vector", dve)
            k.emit("vector", lambda h: h.tensor_tensor(
                tmp16[:, :], qn_h[:, 0:1], qn_h[:, 1:2], ALU.mult), dve)
            ps_rb_free = (dve, dve.val)

    def head_norm1(hd):
        for p in range(L // 512):
            head_norm1_part(hd, p)

    def head_norm2(hd):
        nonlocal ps_rb_free
        k.wait("scalar", dve, sq_pool_mk)
        k.wait("scalar", dve)
        k.emit("scalar", lambda h: h.activation(tmp16[:, :], tmp16[:, :],
                                                AF.Sqrt), act)
        k.wait("vector", act)
        k.emit("vector", lambda h: h.tensor_scalar(
            negc1[:, :], tmp16[:, :], -SCALE, None, ALU.mult), dve)
        k.wait("tensor", dve)
        k.wmark("tensor", ps_rb_free)
        nb = k.emit("tensor", lambda h: h.matmul(
            ps_rb[0:64, 0:1], ones1_64[:, :], negc1[0:1, :],
            start=True, stop=True), pe)
        k.wait("vector", pe, nb)
        k.emit("vector", lambda h, hd=hd: h.tensor_copy(
            negc_bc[0:64, hd:hd + 1], ps_rb[0:64, 0:1]), dve)
        k.wait("vector", dve)
        negc_mark[hd] = k.emit("vector", lambda h, hd=hd: h.tensor_copy(
            negc_bc[64:128, hd:hd + 1], ps_rb[0:64, 0:1]), dve)
        ps_rb_free = (dve, negc_mark[hd])

    load_head(0)
    pending_epi = None
    k.emit("sync", lambda h: h.dma_start(
        v_sb[:, :].rearrange("p (c d) -> p c d", c=KC),
        v_hbm[:, :, :].rearrange("c p d -> p c d")), sVL, 16)
    head_sq(0)
    head_norm1(0)
    head_norm2(0)
    scores_ok = None
    av_ok = None
    exp_tail = 0
    rvb_rd = 0

    for hd in range(NH):
        buf = hd % 2
        negc_hd_mk = negc_mark[hd]
        for qb in range(QB):
            if hd + 1 < NH and qb == 0:
                load_head(hd + 1)
                if QB == 1:
                    head_sq(hd + 1)
                    head_norm1(hd + 1)
                    head_norm2(hd + 1)
            q0 = qb * QBL
            exp_mark = {}
            av_mark = {}

            def emit_av(c):
                pb = c % 6
                k.wait("tensor", act, exp_mark[c])
                if c == 0:
                    k.wmark("tensor", av_ok)
                for j in range(QJ):
                    k.emit("tensor", lambda h, c=c, pb=pb, j=j, hd=hd:
                           h.matmul(ps_av[:, 512 * j:512 * (j + 1)],
                                    v_sb[:, (NH * 65) * c + 65 * hd:
                                         (NH * 65) * c + 65 * hd + 65],
                                    probs[:, pb * QBL + 512 * j:
                                          pb * QBL + 512 * (j + 1)],
                                    start=(c == 0), stop=(c == KC - 1)), pe)
                av_mark[c] = pe.val

            for c in range(KC):
                pb = c % 6
                sb2 = c % 2
                k.wait("tensor", sKH[buf], kh_val[buf])
                k.wait("tensor", sQH[buf], qh_val[buf])
                k.wait("tensor", sVL, 16)
                if c > 1:
                    k.wait("tensor", act, exp_mark[c - 2])
                else:
                    k.wait("tensor", act, exp_tail)
                    if c == 0:
                        k.wmark("tensor", scores_ok)
                for j in range(QJ):
                    k.emit("tensor", lambda h, buf=buf, c=c, j=j, q0=q0,
                           sb2=sb2:
                           h.matmul(ps_s[sb2][:, 512 * j:512 * (j + 1)],
                                    kh_sb[:, buf * L + 128 * c:
                                          buf * L + 128 * (c + 1)],
                                    qh_sb[:, buf * L + q0 + 512 * j:
                                          buf * L + q0 + 512 * (j + 1)],
                                    start=True, stop=True), pe)
                sc_mk = pe.val
                if c == min(1, KC - 1) and pending_epi is not None:
                    pending_epi()
                    pending_epi = None
                if c >= 5:
                    emit_av(c - 5)
                if hd + 1 < NH and QB > 1 and c == 6 and qb == 0:
                    head_sq(hd + 1)
                if (hd + 1 < NH and QB > 1 and qb == 1 and c >= 5
                        and (c - 5) % 3 == 0 and (c - 5) // 3 < L // 512):
                    head_norm1_part(hd + 1, (c - 5) // 3)
                if hd + 1 < NH and QB > 1 and c == 4 and qb == 2:
                    head_norm2(hd + 1)
                k.wait("scalar", dve, negc_hd_mk)
                k.wait("scalar", pe, sc_mk)
                if c >= 6:
                    k.wait("scalar", pe, av_mark[c - 6])
                exp_mark[c] = k.emit(
                    "scalar", lambda h, pb=pb, hd=hd, sb2=sb2: h.activation(
                        probs[:, pb * QBL:(pb + 1) * QBL], ps_s[sb2][:, :],
                        AF.Exp,
                        bias=negc_bc[:, hd:hd + 1], scale=SCALE), act)
            for cc in range(max(0, KC - 5), KC):
                emit_av(cc)
            k.wait("vector", pe, av_mark[KC - 1])
            k.wait("vector", pe, rvb_rd)
            k.wait("vector", dve)
            def _recip(h):
                with nc.allow_low_precision(reason="1/l to bf16 as before"):
                    return h.reciprocal(rvb_sb[:, :], ps_av[64:65, :])
            recip_mk = k.emit("vector", _recip, dve)

            def mk_epi(recip_mk=recip_mk, hd=hd, q0=q0):
                def epi():
                    nonlocal ps_rb_free, scores_ok, av_ok, rvb_rd
                    k.wait("tensor", dve, recip_mk)
                    k.wmark("tensor", ps_rb_free)
                    for j in range(QJ):
                        k.emit("tensor", lambda h, j=j: h.matmul(
                            ps_rb[0:64, 512 * j:512 * (j + 1)],
                            ones1_64[:, :],
                            rvb_sb[:, 512 * j:512 * (j + 1)],
                            start=True, stop=True), pe)
                    rbc_mk = pe.val
                    rvb_rd = rbc_mk
                    k.wait("vector", pe, rbc_mk)
                    k.emit("vector", lambda h: h.tensor_copy(
                        rbc_sb[:, :], ps_rb[0:64, 0:QBL]), dve)
                    ps_rb_free = (dve, dve.val)
                    k.wait("vector", dve)
                    cc_, hh_ = hd // 2, hd % 2
                    dst = ao_sb[64 * hh_:64 * (hh_ + 1),
                                cc_ * L + q0:cc_ * L + q0 + QBL]
                    ao_mk = k.emit("vector", lambda h, dst=dst:
                                   h.tensor_tensor(dst, ps_av[0:64, :],
                                                   rbc_sb[:, :], ALU.mult),
                                   dve)
                    scores_ok = (dve, ao_mk)
                    av_ok = (dve, ao_mk)
                return epi
            pending_epi = mk_epi()
            exp_tail = exp_mark[KC - 1]
        head_pe_done[buf] = pe.val

    if pending_epi is not None:
        pending_epi()
        pending_epi = None

    if ao_dbg is not None:
        k.wait("sync", dve)
        k.emit("sync", lambda h: h.dma_start(
            ao_dbg[:, :].rearrange("(cc pp) t -> pp cc t", pp=128),
            ao_sb[:, :].rearrange("p (cc t) -> p cc t", cc=8)), sKQ, 16)
    esB.close()

    if "C" not in phases:
        k.barrier()
        k.play()
        return nc

    k.barrier()
    esC = ExitStack()
    Wfc = k.sb(esC, "Wfc", [128, 8 * D], BF16)
    bfc_row = k.sb(esC, "bfc_row", [1, D], BF16)
    xc_sb = k.sb(esC, "xc_sb", [128, 2 * D], F32)
    y_sb = k.sb(esC, "y_sb", [128, 2 * D], F32)
    x3_sb = k.sb(esC, "x3_sb", [128, 2 * D], F32)
    xq_tm = k.sb(esC, "xq_tm", [128, 2 * D], BF16)
    xqst = k.sb(esC, "xqst", [128, 2 * D], BF16)
    qm = k.sb(esC, "qm", [128, 2], F32)
    qsc = k.sb(esC, "qsc", [128, 2], F32)
    qrc = k.sb(esC, "qrc", [128, 2], F32)
    statC = k.sb(esC, "statC", [128, 2 * 24], F32)
    mvC = k.sb(esC, "mvC", [128, 2 * 2], F32)
    veC = k.sb(esC, "veC", [128, 2], F32)
    sdC = k.sb(esC, "sdC", [128, 2], F32)
    rstdC = k.sb(esC, "rstdC", [128, 2], F32)
    negmrC = k.sb(esC, "negmrC", [128, 2], F32)
    ps_y = k.ps(esC, "ps_y", [128, D])
    ps_t2 = [k.ps(esC, f"ps_t2{i}", [128, 512]) for i in range(2)]

    bfcf = k.sb(esC, "bfcf", [1, D], F32)
    stage_cast(Wfc, W["W_fc"][:, :].rearrange("(c p) o -> p c o", p=128),
               8, D, xc_sb, sWL)
    k.emit("sync", lambda h: h.dma_start(
        bfcf[:, :], W["b_fc"][:].rearrange("(o c) -> o c", o=1)), sWL, 16)
    k.wait("vector", sWL)
    bfc_mk = k.emit("vector", lambda h: h.tensor_copy(
        bfc_row[:, :], bfcf[:, :]), dve)

    def ln_chainC(par, src, dst, wr_free_mk, src_mk=None):
        k.wmark("vector", src_mk)
        k.wait("vector", dve)
        for c in range(4):
            k.emit("vector", lambda h, c=c, par=par, src=src: h.bn_stats(
                statC[:, par * 24 + 6 * c:par * 24 + 6 * (c + 1)],
                src[:, 512 * c:512 * (c + 1)]), dve)
        k.wait("vector", dve)
        k.emit("vector", lambda h, par=par: h.bn_aggr(
            mvC[:, 2 * par:2 * par + 2],
            statC[:, par * 24:par * 24 + 24]), dve)
        k.wait("vector", dve)
        k.emit("vector", lambda h, par=par: h.tensor_scalar(
            veC[:, par:par + 1], mvC[:, 2 * par + 1:2 * par + 2], EPS_LN,
            None, ALU.add), dve)
        k.wait("scalar", dve)
        k.emit("scalar", lambda h, par=par: h.activation(
            sdC[:, par:par + 1], veC[:, par:par + 1], AF.Sqrt), act)
        k.wait("vector", act)
        k.emit("vector", lambda h, par=par: h.reciprocal(
            rstdC[:, par:par + 1], sdC[:, par:par + 1]), dve)
        k.wait("vector", dve)
        k.emit("vector", lambda h, par=par: h.tensor_scalar(
            negmrC[:, par:par + 1], mvC[:, 2 * par:2 * par + 1],
            rstdC[:, par:par + 1], -1.0, ALU.mult, ALU.mult), dve)
        k.wait("scalar", dve)
        k.wmark("scalar", wr_free_mk)
        return k.emit("scalar", lambda h, par=par, src=src, dst=dst:
                      h.activation(dst, src, AF.Identity,
                                   bias=negmrC[:, par:par + 1],
                                   scale=rstdC[:, par:par + 1]), act)

    yev_mk = [[None] * 4, [None] * 4]
    xq_mk = [0, 0]
    yln_rd = [0, 0]
    x3_rd = [0, 0]
    y_rd = [0, 0]
    xC_free = [None, None]
    xq_dma_mk = [0, 0]
    x2_dma_mk = [0, 0]
    tgroup_done = [0, 0]

    def emit_tgroup(t):
        par = t % 2
        for g in range(4):
            pt = g % 2
            k.wait("tensor", dve, xq_mk[par])
            k.wait("tensor", act)
            for jj in range(4):
                dc = 4 * g + jj
                k.emit("tensor", lambda h, pt=pt, jj=jj, dc=dc, par=par:
                       h.matmul(ps_t2[pt][:, 128 * jj:128 * (jj + 1)],
                                xq_tm[:, par * D + 128 * dc:
                                      par * D + 128 * (dc + 1)],
                                cid_sb[:, :], start=True, stop=True), pe)
            k.wait("scalar", pe)
            k.wait("scalar", sXQ[par], xq_dma_mk[par])
            k.emit("scalar", lambda h, pt=pt, g=g, par=par: h.copy(
                xqst[:, par * D + 512 * g:par * D + 512 * (g + 1)],
                ps_t2[pt][:, :]), act)
        tgroup_done[par] = pe.val
        k.wait("sync", act)
        k.emit("sync", lambda h, t=t, par=par: h.dma_start(
            xq_hbm[t, :, :], xqst[:, par * D:(par + 1) * D]), sXQ[par], 16)
        xq_dma_mk[par] = sXQ[par].val

    k.wait("sync", dve)
    for t0_ in range(2):
        k.emit("sync", lambda h, t0_=t0_: h.dma_start(
            xc_sb[:, t0_ * D:(t0_ + 1) * D],
            x_in[128 * t0_:128 * (t0_ + 1), :]), sX[t0_], 16)
    xld_val = [sX[0].val, sX[1].val]

    for t in range(TC):
        par = t % 2
        for db in range(4):
            if t >= 1:
                k.wmark("tensor", yev_mk[1 - par][db])
            k.wait("tensor", dve, bfc_mk)
            k.emit("tensor", lambda h, db=db: h.matmul(
                ps_y[:, 512 * db:512 * (db + 1)], ones1_128[:, :],
                bfc_row[:, 512 * db:512 * (db + 1)],
                start=True, stop=False), pe)
            for cc in range(8):
                k.emit("tensor", lambda h, db=db, cc=cc, t=t: h.matmul(
                    ps_y[:, 512 * db:512 * (db + 1)],
                    ao_sb[:, cc * L + 128 * t:cc * L + 128 * (t + 1)],
                    Wfc[:, cc * D + 512 * db:cc * D + 512 * (db + 1)],
                    start=False, stop=(cc == 7)), pe)
            ymk = pe.val
            k.wait("scalar", pe, ymk)
            k.wait("scalar", act, y_rd[par])
            yev_mk[par][db] = (act, k.emit(
                "scalar", lambda h, db=db, par=par: h.copy(
                    y_sb[:, par * D + 512 * db:par * D + 512 * (db + 1)],
                    ps_y[:, 512 * db:512 * (db + 1)]), act))
        if t >= 2:
            emit_tgroup(t - 2)
        ysrc = y_sb[:, par * D:(par + 1) * D]
        yln_mk = ln_chainC(par, ysrc, x3_sb[:, par * D:(par + 1) * D],
                           (dve, x3_rd[par]), src_mk=yev_mk[par][3])
        y_rd[par] = yln_mk
        k.wait("vector", act, yln_mk)
        k.wait("vector", sX[par], xld_val[par])
        k.wait("vector", sX2, x2_dma_mk[par])
        x2_mk = k.emit("vector", lambda h, par=par: h.tensor_tensor(
            xc_sb[:, par * D:(par + 1) * D],
            x3_sb[:, par * D:(par + 1) * D],
            xc_sb[:, par * D:(par + 1) * D], ALU.add), dve)
        x3_rd[par] = x2_mk
        k.wait("sync", dve, x2_mk)
        k.emit("sync", lambda h, t=t, par=par: h.dma_start(
            x2_hbm[128 * t:128 * (t + 1), :],
            xc_sb[:, par * D:(par + 1) * D]), sX2, 16)
        x2_dma_mk[par] = sX2.val
        x3_mk = ln_chainC(par, xc_sb[:, par * D:(par + 1) * D],
                          x3_sb[:, par * D:(par + 1) * D],
                          (dve, x3_rd[par]))
        if t + 2 < TC:
            k.wait("sync", act, x3_mk)
            k.wait("sync", sX2, x2_dma_mk[par])
            k.emit("sync", lambda h, t=t, par=par: h.dma_start(
                xc_sb[:, par * D:(par + 1) * D],
                x_in[128 * (t + 2):128 * (t + 3), :]), sX[par], 16)
            xld_val[par] = sX[par].val
        k.wait("vector", act, x3_mk)
        k.emit("vector", lambda h, par=par: h.reduce_max(
            qm[:, par:par + 1], x3_sb[:, par * D:(par + 1) * D],
            axis=AX.X, apply_absolute_value=True), dve)
        k.wait("vector", dve)
        k.emit("vector", lambda h, par=par: h.tensor_scalar(
            qsc[:, par:par + 1], qm[:, par:par + 1], 1.0 / QMAX, EPS_Q,
            ALU.mult, ALU.max), dve)
        k.wait("vector", dve)
        k.emit("vector", lambda h, par=par: h.reciprocal(
            qrc[:, par:par + 1], qsc[:, par:par + 1]), dve)
        k.wait("vector", dve)
        k.emit("vector", lambda h, par=par: h.tensor_scalar(
            x3_sb[:, par * D:(par + 1) * D],
            x3_sb[:, par * D:(par + 1) * D],
            qrc[:, par:par + 1], MAGIC, ALU.mult, ALU.add), dve)
        k.wait("vector", dve)
        k.wait("vector", pe, tgroup_done[par])
        m2 = k.emit("vector", lambda h, par=par: h.tensor_scalar(
            xq_tm[:, par * D:(par + 1) * D],
            x3_sb[:, par * D:(par + 1) * D], MAGIC, qsc[:, par:par + 1],
            ALU.subtract, ALU.mult), dve)
        xq_mk[par] = m2
        x3_rd[par] = m2

    emit_tgroup(TC - 2)
    emit_tgroup(TC - 1)

    esC.close()
    esBC.close()

    if "0" not in phases:
        k.barrier()
        k.play()
        return nc

    k.barrier()
    esD0 = ExitStack()
    w1raw = k.sb(esD0, "w1raw", [128, 2 * 4 * D], F32)
    w1q = k.sb(esD0, "w1q", [128, 4 * D], BF16)
    stg = k.sb(esD0, "stg", [128, 2 * DC * 512], BF16)
    qmD = k.sb(esD0, "qmD", [128, 1], F32)
    qscD = k.sb(esD0, "qscD", [128, 1], F32)
    qrcD = k.sb(esD0, "qrcD", [128, 1], F32)
    ps_tD = [k.ps(esD0, f"ps_tD{i}", [128, 512]) for i in range(2)]

    NB1 = GU // 512
    w1_free = [None, None]
    stg_free = [0, 0]
    tD_free = [None, None]
    w1_val_arr = [0, 0]
    w1q_rd = [0, 0, 0, 0]

    def load_w1(blk):
        wb = blk % 2
        k.wmark("gpsimd", w1_free[wb])
        k.emit("gpsimd", lambda h, blk=blk, wb=wb: h.dma_start(
            w1raw[:, :].rearrange("p (u c d) -> p u c d", u=2, c=4)[:, wb],
            W["w_gate_up"][512 * blk:512 * (blk + 1), :].rearrange(
                "(c p) d -> p c d", p=128)), sW1[wb], 16)
        w1_val_arr[wb] = sW1[wb].val

    load_w1(0)
    for blk in range(NB1):
        wb = blk % 2
        if blk + 1 < NB1:
            load_w1(blk + 1)
        w1_val = w1_val_arr[wb]
        for oc in range(4):
            src = w1raw[:, (4 * wb + oc) * D:(4 * wb + oc + 1) * D]
            k.wait("vector", sW1[wb], w1_val)
            k.wait("vector", dve)
            k.wait("vector", pe, w1q_rd[oc])
            k.emit("vector", lambda h, src=src: h.reduce_max(
                qmD[:, :], src, axis=AX.X, apply_absolute_value=True), dve)
            k.wait("vector", dve)
            k.emit("vector", lambda h: h.tensor_scalar(
                qscD[:, :], qmD[:, :], 1.0 / QMAX, EPS_Q, ALU.mult, ALU.max), dve)
            k.wait("vector", dve)
            k.emit("vector", lambda h: h.reciprocal(qrcD[:, :], qscD[:, :]),
                   dve)
            k.wait("vector", dve)
            k.emit("vector", lambda h, src=src: h.tensor_scalar(
                src, src, qrcD[:, :], MAGIC, ALU.mult, ALU.add), dve)
            k.wait("vector", dve)
            wq_mk = k.emit("vector", lambda h, src=src, oc=oc: h.tensor_scalar(
                w1q[:, oc * D:(oc + 1) * D], src, MAGIC, qscD[:, :],
                ALU.subtract, ALU.mult), dve)
            w1_free[wb] = (dve, wq_mk)
            for g in range(4):
                pt = g % 2
                k.wait("tensor", dve, wq_mk)
                k.wmark("tensor", tD_free[pt])
                for jj in range(4):
                    dc = 4 * g + jj
                    k.emit("tensor", lambda h, pt=pt, jj=jj, dc=dc, oc=oc:
                           h.matmul(ps_tD[pt][:, 128 * jj:128 * (jj + 1)],
                                    w1q[:, oc * D + 128 * dc:
                                        oc * D + 128 * (dc + 1)],
                                    cid_sb[:, :], start=True, stop=True), pe)
                w1q_rd[oc] = pe.val
                k.wait("scalar", pe)
                k.wait("scalar", sST, stg_free[wb])
                dstT = stg[:, :].rearrange(
                    "p (u hc o) -> p u hc o", u=2, hc=DC)[
                        :, wb, 4 * g:4 * g + 4, 128 * oc:128 * (oc + 1)]
                mk = k.emit(
                    "scalar", lambda h, pt=pt, dstT=dstT: h.copy(
                        dstT, ps_tD[pt][:, :].rearrange(
                            "p (j t) -> p j t", j=4)), act)
                tD_free[pt] = (act, mk)
        k.wait("sync", act)
        k.emit("sync", lambda h, blk=blk, wb=wb: h.dma_start(
            w1qT_hbm[:, :, :].rearrange("hc p o -> p hc o")[
                :, :, 512 * blk:512 * (blk + 1)],
            stg[:, wb * DC * 512:(wb + 1) * DC * 512].rearrange(
                "p (hc o) -> p hc o", hc=DC)), sST, 16)
        stg_free[wb] = sST.val

    esD0.close()

    if "1" not in phases:
        k.barrier()
        k.play()
        return nc

    k.barrier()
    esD0b = ExitStack()
    w2raw = k.sb(esD0b, "w2raw", [128, 2 * I_FF], F32)
    w2q = k.sb(esD0b, "w2q", [128, 2 * I_FF], BF16)
    stg2 = k.sb(esD0b, "stg2", [128, IC * 256], BF16)
    qmD2 = k.sb(esD0b, "qmD2", [128, 1], F32)
    qscD2 = k.sb(esD0b, "qscD2", [128, 1], F32)
    qrcD2 = k.sb(esD0b, "qrcD2", [128, 1], F32)
    ps_tE = [k.ps(esD0b, f"ps_tE{i}", [128, 512]) for i in range(2)]

    w2_free = None
    stg2_free = 0
    tD_free = [None, None]
    w2q_rd = [0, 0]
    for rnd in range(8):
        k.wmark("gpsimd", w2_free)
        k.emit("gpsimd", lambda h, rnd=rnd: h.dma_start(
            w2raw[:, :].rearrange("p (c i) -> p c i", c=2),
            W["w_down"][256 * rnd:256 * (rnd + 1), :].rearrange(
                "(c p) i -> p c i", p=128)), sW2, 16)
        w2_val = sW2.val
        for dc2 in range(2):
            src = w2raw[:, dc2 * I_FF:(dc2 + 1) * I_FF]
            k.wait("vector", sW2, w2_val)
            k.wait("vector", dve)
            k.wait("vector", pe, w2q_rd[dc2])
            k.emit("vector", lambda h, src=src: h.reduce_max(
                qmD2[:, :], src, axis=AX.X, apply_absolute_value=True), dve)
            k.wait("vector", dve)
            k.emit("vector", lambda h: h.tensor_scalar(
                qscD2[:, :], qmD2[:, :], 1.0 / QMAX, EPS_Q, ALU.mult, ALU.max), dve)
            k.wait("vector", dve)
            k.emit("vector", lambda h: h.reciprocal(qrcD2[:, :], qscD2[:, :]),
                   dve)
            k.wait("vector", dve)
            k.emit("vector", lambda h, src=src: h.tensor_scalar(
                src, src, qrcD2[:, :], MAGIC, ALU.mult, ALU.add), dve)
            k.wait("vector", dve)
            wq_mk = k.emit("vector", lambda h, src=src, dc2=dc2:
                           h.tensor_scalar(
                               w2q[:, dc2 * I_FF:(dc2 + 1) * I_FF], src,
                               MAGIC, qscD2[:, :], ALU.subtract, ALU.mult),
                           dve)
            if dc2 == 1:
                w2_free = (dve, wq_mk)
            for g in range(11):
                pt = g % 2
                k.wait("tensor", dve, wq_mk)
                k.wmark("tensor", tD_free[pt])
                for jj in range(4):
                    ic = 4 * g + jj
                    k.emit("tensor", lambda h, pt=pt, jj=jj, ic=ic, dc2=dc2:
                           h.matmul(ps_tE[pt][:, 128 * jj:128 * (jj + 1)],
                                    w2q[:, dc2 * I_FF + 128 * ic:
                                        dc2 * I_FF + 128 * (ic + 1)],
                                    cid_sb[:, :], start=True, stop=True), pe)
                w2q_rd[dc2] = pe.val
                k.wait("scalar", pe)
                k.wait("scalar", sST, stg2_free)
                dstT = stg2[:, :].rearrange(
                    "p (ic d) -> p ic d", ic=IC)[
                        :, 4 * g:4 * g + 4, 128 * dc2:128 * (dc2 + 1)]
                mk = k.emit(
                    "scalar", lambda h, pt=pt, dstT=dstT: h.copy(
                        dstT, ps_tE[pt][:, :].rearrange(
                            "p (j t) -> p j t", j=4)), act)
                tD_free[pt] = (act, mk)
        k.wait("sync", act)
        k.emit("sync", lambda h, rnd=rnd: h.dma_start(
            w2qT_hbm[:, :, :].rearrange("ic p d -> p ic d")[
                :, :, 256 * rnd:256 * (rnd + 1)],
            stg2[:, :].rearrange("p (ic d) -> p ic d", ic=IC)), sST, 16)
        stg2_free = sST.val

    esD0b.close()

    if "D" not in phases:
        k.barrier()
        k.play()
        return nc

    k.barrier()
    esD1 = ExitStack()
    w1g = k.sb(esD1, "w1g", [128, 2 * DC * 512], BF16)
    w1u = k.sb(esD1, "w1u", [128, 2 * DC * 512], BF16)
    xq_sb = k.sb(esD1, "xq_sb", [128, 2 * D], BF16)
    sg_sb = k.sb(esD1, "sg_sb", [128, 2 * 512], F32)
    hst = k.sb(esD1, "hst", [128, 2 * 512], BF16)
    ps_g = [k.ps(esD1, f"ps_g{i}", [128, 512]) for i in range(2)]
    ps_u = [k.ps(esD1, f"ps_u{i}", [128, 512]) for i in range(2)]

    NJ = I_FF // 512
    wj_val = [0, 0]
    xql_val = [0, 0]
    wj_pe_done = [0, 0]
    xq_pe_done = [0, 0]

    def load_wj(j):
        buf = j % 2
        k.wait("sync", pe, wj_pe_done[buf])
        k.emit("sync", lambda h, j=j, buf=buf: h.dma_start(
            w1g[:, buf * DC * 512:(buf + 1) * DC * 512].rearrange(
                "p (hc o) -> p hc o", hc=DC),
            w1qT_hbm[:, :, :].rearrange("hc p o -> p hc o")[
                :, :, 512 * j:512 * (j + 1)]), sWJ[buf], 16)
        k.emit("sync", lambda h, j=j, buf=buf: h.dma_start(
            w1u[:, buf * DC * 512:(buf + 1) * DC * 512].rearrange(
                "p (hc o) -> p hc o", hc=DC),
            w1qT_hbm[:, :, :].rearrange("hc p o -> p hc o")[
                :, :, I_FF + 512 * j:I_FF + 512 * (j + 1)]), sWJ[buf], 16)
        wj_val[buf] = sWJ[buf].val

    def load_xq(t):
        buf = t % 2
        k.wait("sync", pe, xq_pe_done[buf])
        k.emit("sync", lambda h, t=t, buf=buf: h.dma_start(
            xq_sb[:, buf * D:(buf + 1) * D], xq_hbm[t % TC, :, :]),
            sXQL[buf], 16)
        xql_val[buf] = sXQL[buf].val

    load_wj(0)
    load_xq(0)
    load_xq(1)
    ps_g_free = [None, None]
    ps_u_free = [None, None]
    sg_free = [None, None]
    hst_free = [0, 0]
    for j in range(NJ):
        buf = j % 2
        if j + 1 < NJ:
            load_wj(j + 1)
        for t in range(TC):
            tb = t % 2
            pg = t % 2
            k.wait("tensor", sWJ[buf], wj_val[buf])
            k.wait("tensor", sXQL[tb], xql_val[tb])
            k.wmark("tensor", ps_g_free[pg])
            for dc in range(DC):
                k.emit("tensor", lambda h, pg=pg, buf=buf, tb=tb, dc=dc:
                       h.matmul(ps_g[pg][:, :],
                                xq_sb[:, tb * D + 128 * dc:
                                      tb * D + 128 * (dc + 1)],
                                w1g[:, buf * DC * 512 + dc * 512:
                                    buf * DC * 512 + (dc + 1) * 512],
                                start=(dc == 0), stop=(dc == DC - 1)), pe)
            g_mk = pe.val
            k.wmark("tensor", ps_u_free[pg])
            for dc in range(DC):
                k.emit("tensor", lambda h, pg=pg, buf=buf, tb=tb, dc=dc:
                       h.matmul(ps_u[pg][:, :],
                                xq_sb[:, tb * D + 128 * dc:
                                      tb * D + 128 * (dc + 1)],
                                w1u[:, buf * DC * 512 + dc * 512:
                                    buf * DC * 512 + (dc + 1) * 512],
                                start=(dc == 0), stop=(dc == DC - 1)), pe)
            u_mk = pe.val
            xq_pe_done[tb] = pe.val
            wj_pe_done[buf] = pe.val
            if t + 2 < TC:
                load_xq(t + 2)
            elif j + 1 < NJ:
                load_xq(t + 2)
            k.wait("scalar", pe, g_mk)
            k.wmark("scalar", sg_free[pg])
            silu_mk = k.emit(
                "scalar", lambda h, pg=pg: h.activation(
                    sg_sb[:, pg * 512:(pg + 1) * 512], ps_g[pg][:, :],
                    AF.Silu), act)
            ps_g_free[pg] = (act, silu_mk)
            k.wait("vector", act, silu_mk)
            k.wait("vector", pe, u_mk)
            k.wait("vector", sHS[pg], hst_free[pg])
            h_mk = k.emit("vector", lambda h, pg=pg: h.tensor_tensor(
                hst[:, pg * 512:(pg + 1) * 512],
                sg_sb[:, pg * 512:(pg + 1) * 512], ps_u[pg][:, :],
                ALU.mult), dve)
            ps_u_free[pg] = (dve, h_mk)
            sg_free[pg] = (dve, h_mk)
            k.wait("sync", dve, h_mk)
            k.emit("sync", lambda h, t=t, j=j, pg=pg: h.dma_start(
                h_hbm[t, :, 512 * j:512 * (j + 1)],
                hst[:, pg * 512:(pg + 1) * 512]), sHS[pg], 16)
            hst_free[pg] = sHS[pg].val

    esD1.close()

    if "P" not in phases:
        k.barrier()
        k.play()
        return nc

    k.barrier()
    esD2p = ExitStack()
    h_sb = k.sb(esD2p, "h_sb", [128, 2 * I_FF], BF16)
    hq_sb = k.sb(esD2p, "hq_sb", [128, I_FF], BF16)
    hqf = k.sb(esD2p, "hqf", [128, I_FF], F32)
    hqstg = k.sb(esD2p, "hqstg", [128, 2 * I_FF], BF16)
    w2b0 = k.sb(esD2p, "w2b0", [128, IC * 512], BF16)
    foA = k.sb(esD2p, "foA", [128, 2 * 512], F32)
    qmD3 = k.sb(esD2p, "qmD3", [128, 1], F32)
    qscD3 = k.sb(esD2p, "qscD3", [128, 1], F32)
    qrcD3 = k.sb(esD2p, "qrcD3", [128, 1], F32)
    ps_tF = [k.ps(esD2p, f"ps_tF{i}", [128, 512]) for i in range(2)]
    ps_d = [k.ps(esD2p, f"ps_d{i}", [128, 512]) for i in range(2)]

    k.emit("gpsimd", lambda h: h.dma_start(
        w2b0[:, :].rearrange("p (ic d) -> p ic d", ic=IC),
        w2qT_hbm[:, :, :].rearrange("ic p d -> p ic d")[:, :, 0:512]),
        sW2B[0], 16)
    w2b0_val = sW2B[0].val

    h_free = [None, None]
    hl_val = [0, 0]
    hqstg_free = [(0, 0), (0, 0)]
    tD_free = [None, None]
    hq_rd = 0
    fo_free = [None, None]
    fo_dma_mk = [0, 0]

    def load_h(t):
        buf = t % 2
        k.wmark("gpsimd", h_free[buf])
        k.emit("gpsimd", lambda h, t=t, buf=buf: h.dma_start(
            h_sb[:, buf * I_FF:(buf + 1) * I_FF], h_hbm[t, :, :]),
            sHL[buf], 16)
        hl_val[buf] = sHL[buf].val

    load_h(0)
    load_h(1)
    for t in range(TC):
        tb = t % 2
        src = h_sb[:, tb * I_FF:(tb + 1) * I_FF]
        k.wait("vector", sHL[tb], hl_val[tb])
        k.wait("vector", dve)
        k.emit("vector", lambda h, src=src: h.reduce_max(
            qmD3[:, :], src, axis=AX.X, apply_absolute_value=True), dve)
        k.wait("vector", dve)
        k.emit("vector", lambda h: h.tensor_scalar(
            qscD3[:, :], qmD3[:, :], 1.0 / QMAX, EPS_Q, ALU.mult, ALU.max),
            dve)
        k.wait("vector", dve)
        k.emit("vector", lambda h: h.reciprocal(qrcD3[:, :], qscD3[:, :]),
               dve)
        k.wait("vector", dve)
        hf_mk = k.emit("vector", lambda h, src=src: h.tensor_scalar(
            hqf[:, :], src, qrcD3[:, :], MAGIC, ALU.mult, ALU.add), dve)
        h_free[tb] = (dve, hf_mk)
        k.wait("vector", dve)
        k.wait("vector", pe, hq_rd)
        hq_mk = k.emit("vector", lambda h: h.tensor_scalar(
            hq_sb[:, :], hqf[:, :], MAGIC, qscD3[:, :], ALU.subtract,
            ALU.mult), dve)
        if t + 2 < TC:
            load_h(t + 2)
        for g in range(11):
            pt = g % 2
            k.wait("tensor", dve, hq_mk)
            k.wmark("tensor", tD_free[pt])
            for jj in range(4):
                ic = 4 * g + jj
                k.emit("tensor", lambda h, pt=pt, jj=jj, ic=ic: h.matmul(
                    ps_tF[pt][:, 128 * jj:128 * (jj + 1)],
                    hq_sb[:, 128 * ic:128 * (ic + 1)], cid_sb[:, :],
                    start=True, stop=True), pe)
            k.wait("scalar", pe)
            k.wait("scalar", sHQS[tb], hqstg_free[tb][0])
            k.wait("scalar", pe, hqstg_free[tb][1])
            mk = k.emit(
                "scalar", lambda h, pt=pt, g=g, tb=tb: h.copy(
                    hqstg[:, tb * I_FF + 512 * g:tb * I_FF + 512 * (g + 1)],
                    ps_tF[pt][:, :]), act)
            tD_free[pt] = (act, mk)
        hq_rd = pe.val
        k.wait("sync", act)
        k.emit("sync", lambda h, t=t, tb=tb: h.dma_start(
            hqT_hbm[t, :, :], hqstg[:, tb * I_FF:(tb + 1) * I_FF]),
            sHQS[tb], 16)
        pd = t % 2
        k.wait("tensor", sW2B[0], w2b0_val)
        k.wait("tensor", act)
        k.wmark("tensor", fo_free[pd])
        for ic in range(IC):
            k.emit("tensor", lambda h, pd=pd, tb=tb, ic=ic: h.matmul(
                ps_d[pd][:, :],
                hqstg[:, tb * I_FF + 128 * ic:tb * I_FF + 128 * (ic + 1)],
                w2b0[:, ic * 512:(ic + 1) * 512],
                start=(ic == 0), stop=(ic == IC - 1)), pe)
        d_mk = pe.val
        hqstg_free[tb] = (sHQS[tb].val, d_mk)
        k.wait("scalar", pe, d_mk)
        k.wait("scalar", sFS[pd], fo_dma_mk[pd])
        mk = k.emit("scalar", lambda h, pd=pd: h.copy(
            foA[:, pd * 512:(pd + 1) * 512], ps_d[pd][:, :]), act)
        fo_free[pd] = (act, mk)
        k.wait("sync", act, mk)
        k.emit("sync", lambda h, t=t, pd=pd: h.dma_start(
            ffn_hbm[t, :, 0:512],
            foA[:, pd * 512:(pd + 1) * 512]), sFS[pd], 16)
        fo_dma_mk[pd] = sFS[pd].val

    esD2p.close()

    k.barrier()
    esD2 = ExitStack()
    w2b = k.sb(esD2, "w2b", [128, 2 * IC * 512], BF16)
    hqT_sb = k.sb(esD2, "hqT_sb", [128, 2 * I_FF], BF16)
    foB = k.sb(esD2, "foB_sb", [128, 2 * 512], F32)
    fe_sb = k.sb(esD2, "fe_sb", [128, 2 * D], F32)
    x2e_sb = k.sb(esD2, "x2e_sb", [128, 2 * D], F32)
    fln_sb = k.sb(esD2, "fln_sb", [128, 2 * D], F32)
    o_sb = k.sb(esD2, "o_sb", [128, 2 * D], F32)
    statE = k.sb(esD2, "statE", [128, 2 * 24], F32)
    mvE = k.sb(esD2, "mvE", [128, 2 * 2], F32)
    veE = k.sb(esD2, "veE", [128, 2], F32)
    sdE = k.sb(esD2, "sdE", [128, 2], F32)
    rstdE = k.sb(esD2, "rstdE", [128, 2], F32)
    negmrE = k.sb(esD2, "negmrE", [128, 2], F32)
    ps_dB = [k.ps(esD2, f"ps_dB{i}", [128, 512]) for i in range(2)]

    w2b_val = [0, 0]
    hql_val = [0, 0]
    w2b_pe_done = [0, 0]
    hq_pe_done = [0, 0]
    foB_free = [None, None]
    foB_dma_mk = [0, 0]
    fs3_mark = {}

    def load_w2b(db):
        buf = db % 2
        k.wait("gpsimd", pe, w2b_pe_done[buf])
        k.emit("gpsimd", lambda h, db=db, buf=buf: h.dma_start(
            w2b[:, buf * IC * 512:(buf + 1) * IC * 512].rearrange(
                "p (ic d) -> p ic d", ic=IC),
            w2qT_hbm[:, :, :].rearrange("ic p d -> p ic d")[
                :, :, 512 * db:512 * (db + 1)]), sW2B[buf], 16)
        w2b_val[buf] = sW2B[buf].val

    def load_hqT(t):
        buf = t % 2
        k.wait("gpsimd", pe, hq_pe_done[buf])
        k.emit("gpsimd", lambda h, t=t, buf=buf: h.dma_start(
            hqT_sb[:, buf * I_FF:(buf + 1) * I_FF], hqT_hbm[t % TC, :, :]),
            sHQL[buf], 16)
        hql_val[buf] = sHQL[buf].val

    e_free = [None, None]
    el_val = [0, 0]
    o_free = [0, 0]
    fln_rd = [0, 0]

    def load_e(t):
        buf = t % 2
        k.wmark("sync", e_free[buf])
        k.wait("sync", sEL[buf], sEL[buf].val)
        k.wait("sync", sFS[t % 2], fs3_mark.get(t, 0))
        k.emit("sync", lambda h, t=t, buf=buf: h.dma_start(
            fe_sb[:, buf * D:(buf + 1) * D], ffn_hbm[t, :, :]),
            sEL[buf], 16, batched=True)
        k.emit("sync", lambda h, t=t, buf=buf: h.dma_start(
            x2e_sb[:, buf * D:(buf + 1) * D],
            x2_hbm[128 * t:128 * (t + 1), :]), sEL[buf], 16, batched=True)
        el_val[buf] = sEL[buf].val

    def e_step(t):
        tb = t % 2
        src = fe_sb[:, tb * D:(tb + 1) * D]
        k.wait("vector", sEL[tb], el_val[tb])
        k.wait("vector", dve)
        for c in range(4):
            k.emit("vector", lambda h, c=c, src=src, tb=tb: h.bn_stats(
                statE[:, tb * 24 + 6 * c:tb * 24 + 6 * (c + 1)],
                src[:, 512 * c:512 * (c + 1)]), dve)
        k.wait("vector", dve)
        k.emit("vector", lambda h, tb=tb: h.bn_aggr(
            mvE[:, 2 * tb:2 * tb + 2], statE[:, tb * 24:tb * 24 + 24]), dve)
        k.wait("vector", dve)
        k.emit("vector", lambda h, tb=tb: h.tensor_scalar(
            veE[:, tb:tb + 1], mvE[:, 2 * tb + 1:2 * tb + 2], EPS_LN, None,
            ALU.add), dve)
        k.wait("scalar", dve)
        k.emit("scalar", lambda h, tb=tb: h.activation(
            sdE[:, tb:tb + 1], veE[:, tb:tb + 1], AF.Sqrt), act)
        k.wait("vector", act)
        k.emit("vector", lambda h, tb=tb: h.reciprocal(
            rstdE[:, tb:tb + 1], sdE[:, tb:tb + 1]), dve)
        k.wait("vector", dve)
        k.emit("vector", lambda h, tb=tb: h.tensor_scalar(
            negmrE[:, tb:tb + 1], mvE[:, 2 * tb:2 * tb + 1],
            rstdE[:, tb:tb + 1], -1.0, ALU.mult, ALU.mult), dve)
        k.wait("scalar", dve)
        k.wait("scalar", dve, fln_rd[tb])
        fln_mk = k.emit("scalar", lambda h, src=src, tb=tb: h.activation(
            fln_sb[:, tb * D:(tb + 1) * D], src, AF.Identity,
            bias=negmrE[:, tb:tb + 1], scale=rstdE[:, tb:tb + 1]), act)
        k.wait("vector", act, fln_mk)
        k.wait("vector", sOUT, o_free[tb])
        o_mk = k.emit("vector", lambda h, tb=tb: h.tensor_tensor(
            o_sb[:, tb * D:(tb + 1) * D], fln_sb[:, tb * D:(tb + 1) * D],
            x2e_sb[:, tb * D:(tb + 1) * D], ALU.add), dve)
        fln_rd[tb] = o_mk
        e_free[tb] = (dve, o_mk)
        k.wait("sync", dve, o_mk)
        k.emit("sync", lambda h, t=t, tb=tb: h.dma_start(
            out_t[128 * t:128 * (t + 1), :], o_sb[:, tb * D:(tb + 1) * D]),
            sOUT, 16)
        o_free[tb] = sOUT.val

    load_w2b(1)
    load_hqT(0)
    load_hqT(1)
    for db in range(1, 4):
        buf = db % 2
        if db + 1 < 4:
            load_w2b(db + 1)
        for t in range(TC):
            tb = t % 2
            pd = t % 2
            k.wait("tensor", sW2B[buf], w2b_val[buf])
            k.wait("tensor", sHQL[tb], hql_val[tb])
            k.wmark("tensor", foB_free[pd])
            for ic in range(IC):
                k.emit("tensor", lambda h, pd=pd, tb=tb, buf=buf, ic=ic:
                       h.matmul(ps_dB[pd][:, :],
                                hqT_sb[:, tb * I_FF + 128 * ic:
                                       tb * I_FF + 128 * (ic + 1)],
                                w2b[:, buf * IC * 512 + ic * 512:
                                    buf * IC * 512 + (ic + 1) * 512],
                                start=(ic == 0), stop=(ic == IC - 1)), pe)
            d_mk = pe.val
            hq_pe_done[tb] = d_mk
            w2b_pe_done[buf] = d_mk
            if t + 2 < TC:
                load_hqT(t + 2)
            elif db + 1 < 4:
                load_hqT(t + 2)
            k.wait("scalar", pe, d_mk)
            k.wait("scalar", sFS[pd], foB_dma_mk[pd])
            mk = k.emit("scalar", lambda h, pd=pd: h.copy(
                foB[:, pd * 512:(pd + 1) * 512], ps_dB[pd][:, :]), act)
            foB_free[pd] = (act, mk)
            k.wait("sync", act, mk)
            k.emit("sync", lambda h, t=t, db=db, pd=pd: h.dma_start(
                ffn_hbm[t, :, 512 * db:512 * (db + 1)],
                foB[:, pd * 512:(pd + 1) * 512]), sFS[pd], 16)
            foB_dma_mk[pd] = sFS[pd].val
            if db == 3:
                fs3_mark[t] = sFS[pd].val
                if t == 1:
                    load_e(0)
                    load_e(1)
                if t >= 3 and (t - 3) < TC:
                    e_step(t - 3)
                    if t - 3 + 2 < TC:
                        load_e(t - 3 + 2)
    e_step(TC - 3)
    load_e(TC - 1)
    e_step(TC - 2)
    e_step(TC - 1)

    esD2.close()

    k.wait("sync", sOUT)
    k.barrier()

    k.play()
    esG.close()
    k.es.close()
    return nc


def make_consts(L):
    d32 = np.arange(RD)
    theta = np.exp((d32 // 2 * 2).astype(np.float64) *
                   (-np.log(10000.0) / RD))
    pos = np.arange(L, dtype=np.float64)
    ang = pos[None, :] * theta[:, None]
    ra = np.sin(ang)
    rb = np.cos(ang)
    ra4 = np.tile(ra, (4, 1)).astype(ml_dtypes.bfloat16)
    rb4 = np.tile(rb, (4, 1)).astype(ml_dtypes.bfloat16)
    perm = np.zeros((128, 128), np.float32)
    for blk in range(4):
        for j in range(16):
            perm[blk * 32 + 2 * j + 1, blk * 32 + 2 * j] = -1.0
            perm[blk * 32 + 2 * j, blk * 32 + 2 * j + 1] = 1.0
    return {
        "c_id": np.eye(128, dtype=ml_dtypes.bfloat16),
        "c_perm": perm.astype(ml_dtypes.bfloat16),
        "c_ra": np.ascontiguousarray(ra4),
        "c_rb": np.ascontiguousarray(rb4),
    }


def kernel(L0_attn_h, **w):
    L = L0_attn_h.shape[1]
    nc = build_nc(L=L)
    consts = make_consts(L)
    in_maps = []
    for core in range(8):
        m = {"x_in": np.ascontiguousarray(L0_attn_h[core])}
        for nm in ("W_dkv b_dkv W_uk b_uk W_uv b_uv W_dq b_dq W_uq b_uq "
                   "W_qr b_qr W_kr b_kr W_fc b_fc").split():
            m[nm] = w["L0_attn_" + nm]
        m["w_gate_up"] = w["L0_ffn_w_gate_up"]
        m["w_down"] = w["L0_ffn_w_down"]
        m.update(consts)
        in_maps.append(m)
    res = run_bass_kernel_spmd(nc, in_maps, core_ids=list(range(8)))
    return np.stack([np.asarray(r["out"]) for r in res.results], axis=0)
